# version 18
# baseline (speedup 1.0000x reference)
import contextlib
import numpy as np
import ml_dtypes
import concourse.bass as bass
import concourse.mybir as mybir
from concourse.bass_utils import run_bass_kernel_spmd
from concourse.bass import IndirectOffsetOnAxis

F32 = mybir.dt.float32
BF16 = mybir.dt.bfloat16
AF = mybir.ActivationFunctionType
ALU = mybir.AluOpType

B, S, D = 2, 8192, 1024
W = 256
CH = 2048
TP = CH + W
TC = 384
NTC = TP // TC
NPT = D // 128
NTT = TP // 128
BCH = 512
NBC = S // BCH
NHP = 8
NKT = S // 128
HD = 64
I_FF = 2048
NPI = I_FF // 128
NE = 8
EPS_OUT = 1e-5
EPS_IN = 1e-6
BANK = 512

SPARSE_MOE = True
DBG_SKIP = set()
CAP = -(-(TP // 2) // TC) * TC
NCE = CAP // TC
NCC = CAP // 128
TCK = TC // 128
DUMP = NE * CAP
BIGF = 4.0e6


class Prog:
    def __init__(self, nc, block, es):
        self.nc, self.block, self.es = nc, block, es
        self._sems = {}

    def sem(self, name):
        if name not in self._sems:
            self._sems[name] = self.es.enter_context(self.nc.semaphore(name))
        return self._sems[name]

    def vg(self, v, inst):
        self._vch = getattr(self, "_vch", 0) + 1
        inst.then_inc(self.sem("vch"), 1)
        v.wait_ge(self.sem("vch"), self._vch)


class Chain:

    def __init__(self, p, tag):
        self.p = p
        self.tag = tag
        self.cums = {}
        self.steps = []
        self.last = None

    def _bump(self, name, amt):
        v = self.cums.get(name, 0) + amt
        self.cums[name] = v
        return v

    def step(self, eng, fn, dma=False):
        semname = (self.tag + "_D" + eng) if dma else self.tag
        amt = 16 if dma else 1
        v = self._bump(semname, amt)
        self.steps.append((eng, fn, semname, amt, self.last, dma))
        self.last = (semname, v)

    def tail(self):
        return self.last

    def flush(self):
        per_eng = {}
        for (eng, fn, semname, amt, wait, dma) in self.steps:
            per_eng.setdefault(eng, []).append((wait, fn, semname, amt))
        for eng_name, items in per_eng.items():
            def section(e, items=items):
                for wait, fn, semname, amt in items:
                    if wait is not None:
                        e.wait_ge(self.p.sem(wait[0]), wait[1])
                    fn(e).then_inc(self.p.sem(semname), amt)
            getattr(self.p.block, eng_name)(section)
        self.steps = []


def _last(*insts):
    return insts[-1]


def pipelined_mm(h, tag, items, nbank=4, bank0=0):
    nc, p, blk, C = h["nc"], h["p"], h["blk"], h["C"]
    reg = h.setdefault("_pmm", {}).setdefault(
        tag, {"M": 0, "Evector": 0, "Escalar": 0, "Egpsimd": 0, "S0": 0, "S1": 0})
    bM = reg["M"]
    bE = {en: reg["E" + en] for en in ("vector", "scalar", "gpsimd")}
    sMM = p.sem(tag + "M")
    engs_used = {it.get("evac_eng", "vector") for it in items}
    sEV = {en: p.sem(tag + "E" + en[0]) for en in engs_used}
    C.flush()
    tl = C.tail()
    n = len(items)
    has_store = any(it.get("store") for it in items)
    st_count = {}
    st_at = []
    ev_cnt = {"vector": 0, "scalar": 0, "gpsimd": 0}
    ev_at = []
    for i, it in enumerate(items):
        cl = it.get("st_class", i % 2)
        if it.get("store"):
            st_count[cl] = st_count.get(cl, 0) + 1
            st_at.append((cl, st_count[cl]))
        else:
            st_at.append((cl, 0))
        en = it.get("evac_eng", "vector")
        ev_cnt[en] += 1
        ev_at.append((en, ev_cnt[en]))
    bS = {cl: reg.setdefault("S%r" % (cl,), 0) for cl in st_count}
    sST = {cl: p.sem(tag + "S%r" % (cl,)) for cl in st_count}

    @blk.tensor
    def _(t):
        if tl:
            t.wait_ge(p.sem(tl[0]), tl[1])
        for i, it in enumerate(items):
            if i >= nbank:
                en, c = ev_at[i - nbank]
                t.wait_ge(sEV[en], bE[en] + c)
            for w in it.get("t_waits", ()):
                t.wait_ge(p.sem(w[0]), w[1])
            it["mm"](t, bank0 + i % nbank).then_inc(sMM, 1)

    def evac_section(engname):
        def sect(e):
            if tl:
                e.wait_ge(p.sem(tl[0]), tl[1])
            for i, it in enumerate(items):
                if it.get("evac_eng", "vector") != engname:
                    continue
                e.wait_ge(sMM, bM + i + 1)
                bl = it.get("buf_lag")
                if bl is not None and i - bl >= 0:
                    jcl, jc = st_at[i - bl]
                    if jc > 0:
                        e.wait_ge(sST[jcl], (bS[jcl] + jc) * 16)
                for w in it.get("v_waits", ()):
                    e.wait_ge(p.sem(w[0]), w[1])
                it["evac"](e, bank0 + i % nbank).then_inc(sEV[engname], 1)
        return sect
    engs = {it.get("evac_eng", "vector") for it in items}
    for en in engs:
        getattr(blk, en)(evac_section(en))

    if has_store:
        st_engs = {it.get("store_eng", "sync") for it in items if it.get("store")}
        def store_section(engname):
            def sect(sy):
                if tl:
                    sy.wait_ge(p.sem(tl[0]), tl[1])
                for i, it in enumerate(items):
                    if not it.get("store") or it.get("store_eng", "sync") != engname:
                        continue
                    en, c = ev_at[i]
                    sy.wait_ge(sEV[en], bE[en] + c)
                    it["store"](sy).then_inc(sST[st_at[i][0]], 16)
            return sect
        for en in st_engs:
            getattr(blk, en)(store_section(en))

    reg["M"] += n
    for en in ("vector", "scalar", "gpsimd"):
        reg["E" + en] += ev_cnt[en]
    for cl, c in st_count.items():
        reg["S%r" % (cl,)] += c
    fin = {en: reg["E" + en] for en in engs}
    fin_s = {cl: reg["S%r" % (cl,)] for cl in st_count}

    def join(v):
        for en, c in fin.items():
            v.wait_ge(sEV[en], c)
        for cl, c in fin_s.items():
            v.wait_ge(sST[cl], c * 16)
        return v.tensor_copy(h["idf"][0:1, 0:1], h["idf"][0:1, 0:1])
    C.step("vector", join)


def build_program(stages=("A",), dbg_out=None):
    nc = bass.Bass("TRN2", target_bir_lowering=False)
    es = contextlib.ExitStack()

    dram = {}
    def din(name, shape, dt=F32):
        dram[name] = nc.dram_tensor(name, shape, dt, kind="ExternalInput").ap()
        return dram[name]
    def dint(name, shape, dt):
        dram[name] = nc.dram_tensor(name, shape, dt, kind="Internal").ap()
        return dram[name]

    din("ident", [128, 128])
    dint("xres", [128, NPT * TP], F32)
    if "IN" in stages:
        din("xs", [TP, D])
    else:
        din("xres_in", [128, NPT * TP], F32)
    if "C" in stages or "D" in stages:
        din("mask", [128, TP])
    if "C" in stages:
        din("l1_w", [D]); din("l1_u", [D]); din("l1_mk", [D]); din("l1_mv", [D]); din("l1_mr", [D])
        din("Wk_T", [D, D]); din("Wv_T", [D, D]); din("Wr_T", [D, D]); din("Wo_T", [D, D])
        dint("kf_scr", [D, TP], F32)
        dint("vf_scr", [D, TP], F32)
        dint("sr_scr", [D, TP], BF16)
        dint("rw_scr", [D, TP], BF16)
    if "D" in stages:
        din("f_Wr_T", [D, D]); din("f_Wk_T", [D, D]); din("f_Wv_T", [D, D])
        din("f_mur", [D]); din("f_muk", [D])
    if "B" in stages:
        din("sh_gate_T", [D, I_FF]); din("sh_up_T", [D, I_FF]); din("sh_down_T", [I_FF, D])
        din("gate_w", [NE, D, I_FF]); din("up_w", [NE, D, I_FF]); din("down_w", [NE, I_FF, D])
        din("router_T", [D, NE]); din("r_bias", [NE])
        dint("ln1_scr", [D, TP], BF16)
        if SPARSE_MOE:
            dint("ln1_tm", [TP, D], BF16)
            dint("xg_e", [NE * CAP + 1, D], BF16)
            dint("eout_e", [NE * CAP + 1, D], F32)
    if "A" in stages:
        din("xb", [S, D])
        din("w_q", [D, D]); din("w_k", [D, D]); din("w_v", [D, D]); din("w_proj", [D, D])
        din("b_proj", [D]); din("ln_g", [D]); din("ln_b", [D])
        dint("kT_scr", [D, S], BF16)
        dint("v_scr", [NHP, S, 130], BF16)
        dint("qT_scr", [D, TP], BF16)
        dint("oT_scr", [D, TP], BF16)

    out_names = []
    if "OUT" in stages:
        dram["y"] = nc.dram_tensor("y", [CH, D], F32, kind="ExternalOutput").ap()
        out_names.append("y")
    if dbg_out:
        dram[dbg_out] = nc.dram_tensor(dbg_out, [128, NPT * TP], F32,
                                       kind="ExternalOutput").ap()
        out_names.append(dbg_out)

    with es:
        blk = es.enter_context(nc.Block())
        p = Prog(nc, blk, es)
        sbuf = lambda nm, shp, dt: es.enter_context(nc.sbuf_tensor(nm, shp, dt))

        h = {"dram": dram, "stages": stages}
        h["ones1f"] = ones1f = sbuf("ones1f", [128, 1], F32)
        h["onesrf"] = onesrf = sbuf("onesrf", [1, 128], F32)
        h["idf"] = idf = sbuf("idf", [128, 128], F32)
        h["idb"] = idb = sbuf("idb", [128, 128], BF16)
        h["epsA"] = epsA = sbuf("epsA", [1, 1], F32)
        h["epsB"] = epsB = sbuf("epsB", [1, 1], F32)
        rows = sbuf("rows", [1, TP + 2 * BCH], F32)
        h["RTMP"] = rows[0:1, 0:TP]
        h["RB0"] = rows[0:1, TP:TP + BCH]
        h["RB1"] = rows[0:1, TP + BCH:TP + 2 * BCH]
        h["m0"] = sbuf("m0", [1, TP], F32)
        h["r0"] = sbuf("r0", [1, TP], F32)
        PSf = es.enter_context(nc.psum_tensor("PSf", [128, 8 * BANK], F32))
        h["PSb"] = PSb = lambda k, off=0, n=BANK: PSf[:, k * BANK + off: k * BANK + off + n]
        h["PSf"] = PSf
        h["sbuf"] = sbuf
        h["nc"], h["p"], h["blk"] = nc, p, blk

        C = Chain(p, "main")
        h["C"] = C

        self_p = p
        ls_reg = {"T": 0, "S": 0}

        def ln_stats_rows(ch, src_tiles, npt, n, sq_buf, mrow, rrow, trow, eps_ap, nelem):
            ch.flush()
            tl = ch.tail()
            bT, bS = ls_reg["T"], ls_reg["S"]
            sT, sS = p.sem("lsT"), p.sem("lsS")

            @blk.scalar
            def _(s):
                if tl:
                    s.wait_ge(p.sem(tl[0]), tl[1])
                for pt in range(npt):
                    if pt >= 2:
                        s.wait_ge(sT, bT + pt - 1)
                    s.activation(sq_buf[:, (pt % 2) * n:(pt % 2 + 1) * n],
                                 src_tiles(pt), AF.Square).then_inc(sS, 1)

            @blk.tensor
            def _(t):
                if tl:
                    t.wait_ge(p.sem(tl[0]), tl[1])
                for pt in range(npt):
                    t.matmul(PSb(4, 0, n)[0:1, :], ones1f[:], src_tiles(pt),
                             start=(pt == 0), stop=(pt == npt - 1),
                             skip_group_check=True)
                    t.wait_ge(sS, bS + pt + 1)
                    t.matmul(PSb(5, 0, n)[0:1, :], ones1f[:],
                             sq_buf[:, (pt % 2) * n:(pt % 2 + 1) * n],
                             start=(pt == 0), stop=(pt == npt - 1),
                             skip_group_check=True).then_inc(sT, 1)
            ls_reg["T"] += npt
            ls_reg["S"] += npt

            def v_rows(v):
                v.wait_ge(sT, ls_reg["T"])
                i1 = v.tensor_scalar_mul(mrow, PSb(4, 0, n)[0:1, :], 1.0 / nelem)
                v.tensor_scalar_mul(trow, PSb(5, 0, n)[0:1, :], 1.0 / nelem)
                self_p.vg(v, i1)
                i2 = v.tensor_tensor(rrow, mrow, mrow, ALU.mult)
                self_p.vg(v, i2)
                return v.tensor_tensor(trow, trow, rrow, ALU.subtract)
            ch.step("vector", v_rows)
            ch.step("scalar", lambda s: s.activation(trow, trow, AF.Sqrt, bias=eps_ap[:, 0:1]))
            ch.step("vector", lambda v: v.reciprocal(rrow, trow))
        h["ln_stats_rows"] = ln_stats_rows

        def bcast_row(ch, row_ap, n, bank):
            ch.step("tensor", lambda t: t.matmul(PSb(bank, 0, n), onesrf[:], row_ap,
                                                 start=True, stop=True))
        h["bcast_row"] = bcast_row

        def normalize(ch, src_tiles, dst_tiles, npt, n, mb, rb):
            def v_norm(v):
                for pt in range(npt):
                    d = dst_tiles(pt)
                    i0 = v.tensor_tensor(d, src_tiles(pt), mb, ALU.subtract)
                    self_p.vg(v, i0)
                    i1 = v.tensor_tensor(d, d, rb, ALU.mult)
                return i1
            ch.step("vector", v_norm)
        h["normalize"] = normalize

        def mm_acc(t, psum_ap, lhsT_fn, rhs_fn, nk):
            for kt in range(nk):
                i1 = t.matmul(psum_ap, lhsT_fn(kt), rhs_fn(kt),
                              start=(kt == 0), stop=(kt == nk - 1))
            return i1
        h["mm_acc"] = mm_acc

        C.step("vector", lambda v: _last(
            v.memset(ones1f[:], 1.0), v.memset(onesrf[:], 1.0),
            v.memset(epsA[:], EPS_OUT), v.memset(epsB[:], EPS_IN)))
        C.step("sync", lambda sy: sy.dma_start(out=idf[:], in_=dram["ident"]), dma=True)
        C.step("gpsimd", lambda g: g.dma_start(out=idb[:], in_=dram["ident"]), dma=True)
        C.flush()

        if "IN" not in stages:
            C.step("sync", lambda sy: sy.dma_start(out=dram["xres"],
                                                   in_=dram["xres_in"]), dma=True)
        if "A" in stages:
            emit_stage_a(h)
        if "B" in stages:
            emit_stage_b(h)
        if "C" in stages:
            emit_stage_c(h)
        if "D" in stages:
            emit_stage_d(h)
        if "OUT" in stages:
            emit_output(h)

        if dbg_out:
            C.flush()
            tl = C.tail()
            st = p.sem("st_dbg")

            @blk.sync
            def _(sy):
                if tl:
                    sy.wait_ge(p.sem(tl[0]), tl[1])
                sy.dma_start(out=dram[dbg_out], in_=dram["xres"]).then_inc(st, 16)
                sy.wait_ge(st, 16)

        C.flush()
        finals = dict(C.cums)

        @blk.sync
        def _(sy):
            for name, val in finals.items():
                sy.wait_ge(p.sem(name), val)
    return nc, out_names


def emit_transpose_in(h, xT):
    nc, p, blk, C = h["nc"], h["p"], h["blk"], h["C"]
    PSb, idf = h["PSb"], h["idf"]
    d_src = h["dram"]["xs"]
    with contextlib.ExitStack() as ph:
        xstage = ph.enter_context(nc.sbuf_tensor("xstage", [128, 2 * D], F32))
        xld = [p.sem("xld0"), p.sem("xld1")]
        xtp, xcp = p.sem("xtp"), p.sem("xcp")
        C.flush()
        tl = C.tail()

        @blk.sync
        def _(sy):
            if tl:
                sy.wait_ge(p.sem(tl[0]), tl[1])
            for tt in range(NTT):
                if tt >= 2:
                    sy.wait_ge(xcp, tt - 1)
                sy.dma_start(out=xstage[:, (tt % 2) * D:(tt % 2 + 1) * D],
                             in_=d_src[tt * 128:(tt + 1) * 128, :]).then_inc(xld[tt % 2], 16)

        @blk.tensor
        def _(t):
            if tl:
                t.wait_ge(p.sem(tl[0]), tl[1])
            for tt in range(NTT):
                t.wait_ge(xld[tt % 2], (tt // 2 + 1) * 16)
                if tt >= 2:
                    t.wait_ge(xcp, tt - 1)
                for pt in range(NPT):
                    bank = (tt % 2) * 2 + pt // 4
                    ins = t.transpose(PSb(bank, (pt % 4) * 128, 128),
                                      xstage[:, (tt % 2) * D + pt * 128:
                                             (tt % 2) * D + (pt + 1) * 128],
                                      idf[:])
                ins.then_inc(xtp, 1)

        @blk.vector
        def _(v):
            for tt in range(NTT):
                v.wait_ge(xtp, tt + 1)
                for pt in range(NPT):
                    bank = (tt % 2) * 2 + pt // 4
                    ins = v.tensor_copy(xT[:, pt * TP + tt * 128: pt * TP + (tt + 1) * 128],
                                        PSb(bank, (pt % 4) * 128, 128))
                ins.then_inc(xcp, 1)

        C.step("vector", lambda v: _last(v.wait_ge(xcp, NTT),
                                         v.tensor_copy(h["idf"][0:1, 0:1], h["idf"][0:1, 0:1])))
        C.flush()


def emit_stage_a(h):
    nc, p, blk, C, dram = h["nc"], h["p"], h["blk"], h["C"], h["dram"]
    PSb, PSf, sbuf = h["PSb"], h["PSf"], h["sbuf"]
    ones1f, onesrf, idf, idb, epsA, epsB = (h[k] for k in
                                            ("ones1f", "onesrf", "idf", "idb", "epsA", "epsB"))
    RTMP, RB0, RB1, m0, r0 = h["RTMP"], h["RB0"], h["RB1"], h["m0"], h["r0"]
    ln_stats_rows, bcast_row, normalize, mm_acc = (
        h[k] for k in ("ln_stats_rows", "bcast_row", "normalize", "mm_acc"))
    d_kT, d_vh, d_qT, d_oT, d_xres = (dram[k] for k in
                                      ("kT_scr", "v_scr", "qT_scr", "oT_scr", "xres"))
    Wt = lambda w, kt, mt: w[:, kt * D + mt * 128: kt * D + (mt + 1) * 128]

    vecs = sbuf("vecs_a", [128, 24], F32)
    for i, nm in enumerate(("b_proj", "ln_g", "ln_b")):
        C.step("sync", lambda sy, i=i, nm=nm: sy.dma_start(
            out=vecs[:, i * 8:(i + 1) * 8],
            in_=dram[nm].rearrange("(t p) -> p t", p=128),
            allow_slow_non_contiguous=True), dma=True)
    BPJ, LNG, LNB = vecs[:, 0:8], vecs[:, 8:16], vecs[:, 16:24]

    with contextlib.ExitStack() as s1:
        xT = s1.enter_context(nc.sbuf_tensor("xT", [128, NPT * TP], F32))
        XT = lambda pt, tc: xT[:, pt * TP + tc * TC: pt * TP + (tc + 1) * TC]
        emit_transpose_in(h, xT)

        with contextlib.ExitStack() as ph:
            sq = ph.enter_context(nc.sbuf_tensor("sq_a", [128, 2 * TC], F32))
            for tc in range(NTC):
                ln_stats_rows(C, lambda pt, tc=tc: XT(pt, tc), NPT, TC, sq,
                              m0[:, tc * TC:(tc + 1) * TC], r0[:, tc * TC:(tc + 1) * TC],
                              RTMP[:, 0:TC], epsA, D)
            C.flush()

        with contextlib.ExitStack() as ph:
            wq_sb = ph.enter_context(nc.sbuf_tensor("wq_sb", [128, NPT * D], BF16))
            lnq = ph.enter_context(nc.sbuf_tensor("lnq", [128, NPT * TC], BF16))
            qst = ph.enter_context(nc.sbuf_tensor("qst", [128, 2 * TC], BF16))
            C.step("gpsimd", lambda g: g.dma_start(
                out=wq_sb[:].rearrange("p (k n) -> p k n", n=D),
                in_=dram["w_q"].rearrange("(k p) n -> p k n", p=128)), dma=True)
            for tc in range(NTC):
                bcast_row(C, m0[:, tc * TC:(tc + 1) * TC], TC, 6)
                bcast_row(C, r0[:, tc * TC:(tc + 1) * TC], TC, 7)
                normalize(C, lambda pt, tc=tc: XT(pt, tc),
                          lambda pt: lnq[:, pt * TC:(pt + 1) * TC], NPT, TC,
                          PSb(6, 0, TC), PSb(7, 0, TC))
                items = []
                for mt in range(NPT):
                    items.append({
                        "mm": lambda t, bk, mt=mt: mm_acc(
                            t, PSb(bk, 0, TC), lambda kt: Wt(wq_sb, kt, mt),
                            lambda kt: lnq[:, kt * TC:(kt + 1) * TC], NPT),
                        "evac": lambda e, bk, mt=mt: e.tensor_copy(
                            qst[:, (mt % 2) * TC:(mt % 2 + 1) * TC], PSb(bk, 0, TC)),
                        "store": lambda sy, mt=mt, tc=tc: sy.dma_start(
                            out=d_qT[mt * 128:(mt + 1) * 128, tc * TC:(tc + 1) * TC],
                            in_=qst[:, (mt % 2) * TC:(mt % 2 + 1) * TC]),
                        "st_class": mt % 2,
                        "buf_lag": 2})
                pipelined_mm(h, "pA5", items, nbank=2, bank0=0)

        C.step("sync", lambda sy: sy.dma_start(out=d_xres, in_=xT[:]), dma=True)
        C.flush()

    with contextlib.ExitStack() as ph:
        wk_sb = ph.enter_context(nc.sbuf_tensor("wk_sb", [128, NPT * D], BF16))
        wv_sb = ph.enter_context(nc.sbuf_tensor("wv_sb", [128, NPT * D], BF16))
        bstage = ph.enter_context(nc.sbuf_tensor("bstage", [128, 2 * (BCH // 128) * D], F32))
        sjunk = ph.enter_context(nc.sbuf_tensor("sjunk_a2", [128, D], F32))
        vjunk = ph.enter_context(nc.sbuf_tensor("vjunk_a2", [128, D], BF16))
        stat = ph.enter_context(nc.sbuf_tensor("stat_a2", [128, 16], F32))
        epsc = ph.enter_context(nc.sbuf_tensor("epsc_a2", [128, 1], F32))
        lnb = ph.enter_context(nc.sbuf_tensor("lnb", [128, NPT * BCH], BF16))
        kst = ph.enter_context(nc.sbuf_tensor("kst", [128, NPT * BCH], BF16))
        vst = ph.enter_context(nc.sbuf_tensor("vst", [128, NPT * BCH], F32))
        vks = ph.enter_context(nc.sbuf_tensor("vks", [128, 2 * (BCH // 128) * 130], BF16))
        NT4 = BCH // 128
        NG = NT4 * NPT // 4
        BST = lambda bc, t: bstage[:, ((bc % 2) * NT4 + t) * D:
                                   ((bc % 2) * NT4 + t + 1) * D]
        VKS = lambda ipt: vks[:, (ipt % 2) * NT4 * 130:(ipt % 2 + 1) * NT4 * 130]

        sLdA = [p.sem("aLd0"), p.sem("aLd1")]
        sSq, sVar, sSqt, sStat, sNrm, sTpA, sCpA = (p.sem(x) for x in
            ("aSq", "aVar", "aSqt", "aStat", "aNrm", "aTpA", "aCpA"))
        sKmm, sVmm, sEvA, sVkTp, sVkCp = (p.sem(x) for x in
            ("aKmm", "aVmm", "aEvA", "aVkTp", "aVkCp"))
        sStK = [p.sem("aStK0"), p.sem("aStK1")]
        sStV = [p.sem("aStV0"), p.sem("aStV1")]

        C.step("gpsimd", lambda g: g.dma_start(
            out=wk_sb[:].rearrange("p (k n) -> p k n", n=D),
            in_=dram["w_k"].rearrange("(k p) n -> p k n", p=128)), dma=True)
        C.step("gpsimd", lambda g: g.dma_start(
            out=wv_sb[:].rearrange("p (k n) -> p k n", n=D),
            in_=dram["w_v"].rearrange("(k p) n -> p k n", p=128)), dma=True)
        C.step("vector", lambda v: _last(v.memset(vks[:], 1.0),
                                         v.memset(stat[:], 0.0),
                                         v.memset(epsc[:], EPS_OUT),
                                         v.memset(vjunk[:], 0.0),
                                         v.memset(sjunk[:], 0.0)))
        C.flush()
        tl = C.tail()
        wload_target = C.cums["main_Dgpsimd"]

        @blk.sync
        def _(sy):
            if tl:
                sy.wait_ge(p.sem(tl[0]), tl[1])
            for bc in range(NBC):
                if bc >= 2:
                    sy.wait_ge(sTpA, (bc - 1) * NG)
                sy.dma_start(
                    out=bstage[:, (bc % 2) * NT4 * D:(bc % 2 + 1) * NT4 * D]
                    .rearrange("p (t n) -> p t n", n=D),
                    in_=dram["xb"][bc * BCH:(bc + 1) * BCH, :].rearrange(
                        "(t p) n -> p t n", p=128)).then_inc(sLdA[bc % 2], 16)
                for mt in range(NPT):
                    sy.wait_ge(sEvA, bc * 16 + 2 * mt + 1)
                    sy.dma_start(
                        out=d_kT[mt * 128:(mt + 1) * 128, bc * BCH:(bc + 1) * BCH],
                        in_=kst[:, mt * BCH:(mt + 1) * BCH]
                    ).then_inc(sStK[bc % 2], 16)
                for pt in range(NPT):
                    ipt = bc * 8 + pt
                    sy.wait_ge(sVkCp, ipt + 1)
                    sy.dma_start(
                        out=d_vh[pt, bc * BCH:(bc + 1) * BCH, :].rearrange(
                            "(t p) c -> p t c", p=128),
                        in_=VKS(ipt)[:, 0:NT4 * 130].rearrange(
                            "p (t c) -> p t c", c=130)).then_inc(sStV[ipt % 2], 16)

        @blk.scalar
        def _(s):
            if tl:
                s.wait_ge(p.sem(tl[0]), tl[1])
            for bc in range(NBC):
                s.wait_ge(sLdA[bc % 2], (bc // 2 + 1) * 16)
                if bc >= 1:
                    s.wait_ge(sStat, bc)
                for t in range(NT4):
                    if bc * NT4 + t >= 1:
                        s.wait_ge(sSq, bc * NT4 + t)
                    s.activation(sjunk[:], BST(bc, t), AF.Square,
                                 accum_out=stat[:, 4 + t:5 + t]).then_inc(sSq, 1)
                s.wait_ge(sVar, bc + 1)
                s.activation(stat[:, 12:16], stat[:, 4:8], AF.Sqrt,
                             bias=epsc[:, 0:1]).then_inc(sSqt, 1)

        @blk.vector
        def _(v):
            if tl:
                v.wait_ge(p.sem(tl[0]), tl[1])
            for bc in range(NBC):
                v.wait_ge(sLdA[bc % 2], (bc // 2 + 1) * 16)
                if bc >= 1:
                    v.wait_ge(sStat, bc)
                for t in range(NT4):
                    i1 = v.tensor_scalar(vjunk[:], BST(bc, t), 1.0, 0.0,
                                         ALU.mult, ALU.add,
                                         accum_out=stat[:, t:t + 1])
                    p.vg(v, i1)
                v.wait_ge(sSq, (bc + 1) * NT4)
                i1 = v.tensor_scalar(stat[:, 8:12], stat[:, 0:4], 1.0 / D, None,
                                     ALU.mult)
                p.vg(v, i1)
                i2 = v.tensor_tensor(stat[:, 0:4], stat[:, 8:12], stat[:, 8:12],
                                     ALU.mult)
                p.vg(v, i2)
                i3 = v.tensor_scalar(stat[:, 4:8], stat[:, 4:8], 1.0 / D, None,
                                     ALU.mult)
                p.vg(v, i3)
                i4 = v.tensor_tensor(stat[:, 4:8], stat[:, 4:8], stat[:, 0:4],
                                     ALU.subtract).then_inc(sVar, 1)
                v.wait_ge(sSqt, bc + 1)
                i6 = v.reciprocal(stat[:, 12:16], stat[:, 12:16])
                p.vg(v, i6)
                for t in range(NT4):
                    if bc >= 1:
                        v.wait_ge(sTpA, (bc - 1) * NG + 2 * t + 2)
                    i7 = v.tensor_scalar(BST(bc, t), BST(bc, t),
                                         stat[:, 8 + t:9 + t], stat[:, 12 + t:13 + t],
                                         ALU.subtract, ALU.mult)
                    ins = i7
                    if t < NT4 - 1:
                        p.vg(v, i7)
                ins.then_inc(sNrm, NT4)
                v.nop().then_inc(sStat, 1)
                for g in range(NG):
                    v.wait_ge(sTpA, bc * NG + g + 1)
                    for i in range(4):
                        gi = g * 4 + i
                        pt, tt4 = divmod(gi, NT4)
                        ins = v.tensor_copy(
                            lnb[:, pt * BCH + tt4 * 128: pt * BCH + (tt4 + 1) * 128],
                            PSb(2 + g % 2, i * 128, 128))
                    ins.then_inc(sCpA, 1)
                if bc >= 1:
                    v.wait_ge(sStK[(bc - 1) % 2], ((bc - 1) // 2 + 1) * 8 * 16)
                for mt in range(NPT):
                    v.wait_ge(sKmm, bc * 8 + mt + 1)
                    v.tensor_copy(kst[:, mt * BCH:(mt + 1) * BCH],
                                  PSb(4 + mt % 2, 0, BCH)).then_inc(sEvA, 1)
                    v.wait_ge(sVmm, bc * 8 + mt + 1)
                    if bc >= 1:
                        v.wait_ge(sVkTp, (bc - 1) * 8 + mt + 1)
                    v.tensor_copy(vst[:, mt * BCH:(mt + 1) * BCH],
                                  PSb(6 + mt % 2, 0, BCH)).then_inc(sEvA, 1)
                for pt in range(NPT):
                    ipt = bc * 8 + pt
                    v.wait_ge(sVkTp, ipt + 1)
                    if ipt >= 2:
                        v.wait_ge(sStV[ipt % 2], ((ipt - 2) // 2 + 1) * 16)
                    for tt4 in range(NT4):
                        v.tensor_copy(VKS(ipt)[:, tt4 * 130 + 0: tt4 * 130 + 64],
                                      PSb(0 + pt % 2, tt4 * 128, 64))
                        ins = v.tensor_copy(
                            VKS(ipt)[:, tt4 * 130 + 65: tt4 * 130 + 129],
                            PSb(0 + pt % 2, tt4 * 128 + 64, 64))
                    ins.then_inc(sVkCp, 1)

        @blk.tensor
        def _(t):
            if tl:
                t.wait_ge(p.sem(tl[0]), tl[1])
            t.wait_ge(p.sem("main_Dgpsimd"), wload_target)
            for bc in range(NBC):
                for g in range(NG):
                    t.wait_ge(sNrm, (bc + 1) * NT4)
                    if bc * NG + g >= 2:
                        t.wait_ge(sCpA, bc * NG + g - 1)
                    for i in range(4):
                        gi = g * 4 + i
                        pt, tt4 = divmod(gi, NT4)
                        ins = t.transpose(PSb(2 + g % 2, i * 128, 128),
                                          BST(bc, tt4)[:, pt * 128:(pt + 1) * 128],
                                          idf[:])
                    ins.then_inc(sTpA, 1)
                t.wait_ge(sCpA, bc * NG + NG)
                for mt in range(NPT):
                    ikv = bc * 8 + mt
                    if ikv >= 2:
                        t.wait_ge(sEvA, (ikv - 2) * 2 + 2)
                    for kt in range(NPT):
                        ins = t.matmul(PSb(4 + mt % 2, 0, BCH), Wt(wk_sb, kt, mt),
                                       lnb[:, kt * BCH:(kt + 1) * BCH],
                                       start=(kt == 0), stop=(kt == NPT - 1),
                                       skip_group_check=True)
                    ins.then_inc(sKmm, 1)
                    if ikv >= 2:
                        t.wait_ge(sEvA, (ikv - 2) * 2 + 3)
                    for kt in range(NPT):
                        ins = t.matmul(PSb(6 + mt % 2, 0, BCH), Wt(wv_sb, kt, mt),
                                       lnb[:, kt * BCH:(kt + 1) * BCH],
                                       start=(kt == 0), stop=(kt == NPT - 1),
                                       skip_group_check=True)
                    ins.then_inc(sVmm, 1)
                for pt in range(NPT):
                    ipt = bc * 8 + pt
                    t.wait_ge(sEvA, bc * 16 + 2 * pt + 2)
                    if ipt >= 2:
                        t.wait_ge(sVkCp, ipt - 1)
                    for tt4 in range(NT4):
                        ins = t.transpose(PSb(0 + pt % 2, tt4 * 128, 128),
                                          vst[:, pt * BCH + tt4 * 128:
                                              pt * BCH + (tt4 + 1) * 128],
                                          idf[:])
                    ins.then_inc(sVkTp, 1)

        def a2join(v):
            v.wait_ge(sStK[0], ((NBC + 1) // 2) * 8 * 16)
            v.wait_ge(sStK[1], (NBC // 2) * 8 * 16)
            v.wait_ge(sStV[0], (NBC * 8 // 2) * 16)
            v.wait_ge(sStV[1], (NBC * 8 // 2) * 16)
            return v.tensor_copy(stat[0:1, 0:1], stat[0:1, 0:1])
        C.step("vector", a2join)
        C.flush()

    emit_attention(h, d_kT, d_vh, d_qT, d_oT)

    with contextlib.ExitStack() as s3:
        xT = s3.enter_context(nc.sbuf_tensor("xT5", [128, NPT * TP], F32))
        ybuf = s3.enter_context(nc.sbuf_tensor("ybuf", [128, NPT * TP], F32))
        wp_sb = s3.enter_context(nc.sbuf_tensor("wp_sb", [128, NPT * D], BF16))
        och = s3.enter_context(nc.sbuf_tensor("och", [128, NPT * TC], BF16))
        sq5 = s3.enter_context(nc.sbuf_tensor("sq5", [128, 2 * TC], F32))
        XT = lambda pt, tc: xT[:, pt * TP + tc * TC: pt * TP + (tc + 1) * TC]
        YT = lambda pt, tc: ybuf[:, pt * TP + tc * TC: pt * TP + (tc + 1) * TC]
        m1, r1 = m0, r0

        C.step("sync", lambda sy: sy.dma_start(out=xT[:], in_=d_xres), dma=True)
        C.step("gpsimd", lambda g: g.dma_start(
            out=wp_sb[:].rearrange("p (k n) -> p k n", n=D),
            in_=dram["w_proj"].rearrange("(k p) n -> p k n", p=128)), dma=True)

        for tc in range(NTC):
            C.step("sync", lambda sy, tc=tc: sy.dma_start(
                out=och[:].rearrange("p (k n) -> p k n", n=TC),
                in_=d_oT[:, tc * TC:(tc + 1) * TC].rearrange("(k p) n -> p k n", p=128)), dma=True)
            bcast_row(C, m0[:, tc * TC:(tc + 1) * TC], TC, 6)
            bcast_row(C, r0[:, tc * TC:(tc + 1) * TC], TC, 7)
            items = []
            for mt in range(NPT):
                def v_y(v, bk, mt=mt, tc=tc):
                    y = YT(mt, tc)
                    i0 = v.tensor_scalar_add(y, PSb(bk, 0, TC), BPJ[:, mt:mt + 1])
                    p.vg(v, i0)
                    i1 = v.tensor_tensor(sq5[:, 0:TC], XT(mt, tc), PSb(6, 0, TC),
                                         ALU.subtract)
                    p.vg(v, i1)
                    i2 = v.tensor_tensor(sq5[:, 0:TC], sq5[:, 0:TC], PSb(7, 0, TC),
                                         ALU.mult)
                    p.vg(v, i2)
                    return v.tensor_tensor(y, y, sq5[:, 0:TC], ALU.add)
                items.append({
                    "mm": lambda t, bk, mt=mt: mm_acc(
                        t, PSb(bk, 0, TC), lambda kt: Wt(wp_sb, kt, mt),
                        lambda kt: och[:, kt * TC:(kt + 1) * TC], NPT),
                    "evac": v_y})
            pipelined_mm(h, "pA5", items, nbank=4, bank0=0)
        for tc in range(NTC):
            ln_stats_rows(C, lambda pt, tc=tc: YT(pt, tc), NPT, TC, sq5,
                          m1[:, tc * TC:(tc + 1) * TC], r1[:, tc * TC:(tc + 1) * TC],
                          RTMP[:, 0:TC], epsB, D)
        C.flush()
        for tc in range(NTC):
            bcast_row(C, m1[:, tc * TC:(tc + 1) * TC], TC, 6)
            bcast_row(C, r1[:, tc * TC:(tc + 1) * TC], TC, 7)
            def v_fin(v, tc=tc):
                for pt in range(NPT):
                    n = sq5[:, 0:TC]
                    i0 = v.tensor_tensor(n, YT(pt, tc), PSb(6, 0, TC), ALU.subtract)
                    p.vg(v, i0)
                    i1 = v.tensor_tensor(n, n, PSb(7, 0, TC), ALU.mult)
                    p.vg(v, i1)
                    i2 = v.scalar_tensor_tensor(XT(pt, tc), n, LNG[:, pt:pt + 1], XT(pt, tc),
                                                ALU.mult, ALU.add)
                    p.vg(v, i2)
                    i3 = v.tensor_scalar_add(XT(pt, tc), XT(pt, tc), LNB[:, pt:pt + 1])
                return i3
            C.step("vector", v_fin)
            C.flush()
        C.step("sync", lambda sy: sy.dma_start(out=d_xres, in_=xT[:]), dma=True)
        C.flush()


def emit_precast(h):
    nc, p, blk, dram = h["nc"], h["p"], h["blk"], h["dram"]
    sPc = p.sem("pcast")
    C = h["C"]
    C.flush()
    tl = C.tail()

    @blk.gpsimd
    def _(g):
        if tl:
            g.wait_ge(p.sem(tl[0]), tl[1])
        for e in range(NE):
            g.dma_start(out=dram["wbf_g"][e], in_=dram["gate_w"][e]).then_inc(sPc, 16)
            g.dma_start(out=dram["wbf_u"][e], in_=dram["up_w"][e]).then_inc(sPc, 16)
            g.dma_start(out=dram["wbf_d"][e], in_=dram["down_w"][e]).then_inc(sPc, 16)
    h["sPcast"] = sPc


def emit_stage_b(h):
    nc, p, blk, C, dram = h["nc"], h["p"], h["blk"], h["C"], h["dram"]
    PSb, PSf, sbuf = h["PSb"], h["PSf"], h["sbuf"]
    ones1f, onesrf, idf, epsA = h["ones1f"], h["onesrf"], h["idf"], h["epsA"]
    RTMP, m0, r0 = h["RTMP"], h["m0"], h["r0"]
    ln_stats_rows, bcast_row, normalize = (h[k] for k in
                                           ("ln_stats_rows", "bcast_row", "normalize"))
    d_xres, d_ln1 = dram["xres"], dram["ln1_scr"]
    NKI = I_FF // 128
    w8T = sbuf("w8T", [8, TP], F32)
    h["w8T"] = w8T

    with contextlib.ExitStack() as s1:
        xT = s1.enter_context(nc.sbuf_tensor("xTb", [128, NPT * TP], F32))
        XT = lambda pt, tc: xT[:, pt * TP + tc * TC: pt * TP + (tc + 1) * TC]
        sq = s1.enter_context(nc.sbuf_tensor("sq_b", [128, 2 * TC], F32))
        lnc = s1.enter_context(nc.sbuf_tensor("lnc_b", [128, NPT * TC], BF16))
        rtf = s1.enter_context(nc.sbuf_tensor("rtf_b", [128, NPT * NE], F32))
        lnf = s1.enter_context(nc.sbuf_tensor("lnf_b", [128, NPT * TC], F32))
        pr = s1.enter_context(nc.sbuf_tensor("pr_b", [128, NTT * NE], F32))
        prb = s1.enter_context(nc.sbuf_tensor("prb_b", [128, NTT * NE], F32))
        w8 = s1.enter_context(nc.sbuf_tensor("w8_b", [128, NTT * NE], F32))
        neg = s1.enter_context(nc.sbuf_tensor("neg_b", [128, NE], F32))
        rb1 = s1.enter_context(nc.sbuf_tensor("rb1_b", [1, NE], F32))
        rbb = s1.enter_context(nc.sbuf_tensor("rbb_b", [128, NE], F32))
        rowm = s1.enter_context(nc.sbuf_tensor("rowm_b", [128, 24], F32))
        mskd = s1.enter_context(nc.sbuf_tensor("mskd_b", [128, NE], F32))
        gei = s1.enter_context(nc.sbuf_tensor("gei_b", [128, NE], mybir.dt.int32))
        if SPARSE_MOE:
            lntm0 = s1.enter_context(nc.sbuf_tensor("lntm0_b", [128, D], BF16))

        C.step("sync", lambda sy: sy.dma_start(out=xT[:], in_=d_xres), dma=True)
        for tc in range(NTC):
            ln_stats_rows(C, lambda pt, tc=tc: XT(pt, tc), NPT, TC, sq,
                          m0[:, tc * TC:(tc + 1) * TC], r0[:, tc * TC:(tc + 1) * TC],
                          RTMP[:, 0:TC], epsA, D)
        C.flush()

        C.step("sync", lambda sy: sy.dma_start(out=rb1[:], in_=dram["r_bias"][None, :]),
               dma=True)
        C.step("sync", lambda sy: sy.dma_start(
            out=rtf[:].rearrange("p (k n) -> p k n", n=NE),
            in_=dram["router_T"].rearrange("(k p) n -> p k n", p=128)), dma=True)
        bcast_row(C, rb1[0:1, :], NE, 5)
        C.step("vector", lambda v: _last(v.memset(neg[:], -1e30),
                                         v.tensor_copy(rbb[:], PSb(5, 0, NE))))
        for tc in range(NTC):
            bcast_row(C, m0[:, tc * TC:(tc + 1) * TC], TC, 6)
            bcast_row(C, r0[:, tc * TC:(tc + 1) * TC], TC, 7)
            normalize(C, lambda pt, tc=tc: XT(pt, tc),
                      lambda pt: lnf[:, pt * TC:(pt + 1) * TC], NPT, TC,
                      PSb(6, 0, TC), PSb(7, 0, TC))
            TCK3 = TC // 128
            C.step("tensor", lambda t, tc=tc: _last(*[
                t.matmul(PSb(4, st3 * NE, NE)[0:128, :],
                         lnf[:, kt * TC + st3 * 128: kt * TC + (st3 + 1) * 128],
                         rtf[:, kt * NE:(kt + 1) * NE],
                         start=(kt == 0), stop=(kt == NPT - 1),
                         skip_group_check=True)
                for st3 in range(TCK3) for kt in range(NPT)]))
            def v_lg(v, tc=tc):
                for st3 in range(TCK3):
                    tt = tc * TCK3 + st3
                    ins = v.tensor_tensor(pr[:, tt * NE:(tt + 1) * NE],
                                          PSb(4, st3 * NE, NE)[0:128, :], rbb[:],
                                          ALU.add)
                return ins
            C.step("vector", v_lg)
            C.step("scalar", lambda s, tc=tc: s.activation(
                prb[:, tc * TCK3 * NE:(tc + 1) * TCK3 * NE],
                pr[:, tc * TCK3 * NE:(tc + 1) * TCK3 * NE], AF.Sigmoid))
            def v_top2(v, tc=tc):
                for st3 in range(TCK3):
                    tt = tc * TCK3 + st3
                    lz = pr[:, tt * NE:(tt + 1) * NE]
                    i1 = v.reduce_max(rowm[:, st3 * 2:st3 * 2 + 1], lz,
                                      axis=mybir.AxisListType.X)
                    p.vg(v, i1)
                    i2 = v.tensor_scalar(gei[:], lz, rowm[:, st3 * 2:st3 * 2 + 1],
                                         None, ALU.is_ge)
                    p.vg(v, i2)
                    i3 = v.select(mskd[:], gei[:], neg[:], lz, add_drain=True)
                    p.vg(v, i3)
                    i4 = v.reduce_max(rowm[:, st3 * 2 + 1:st3 * 2 + 2], mskd[:],
                                      axis=mybir.AxisListType.X)
                    if st3 < TCK3 - 1:
                        p.vg(v, i4)
                return i4
            C.step("vector", v_top2)
            C.step("scalar", lambda s: s.activation(
                rowm[:, 8:8 + 2 * TCK3], rowm[:, 0:2 * TCK3], AF.Sigmoid))
            def v_top2b(v, tc=tc):
                for st3 in range(TCK3):
                    tt = tc * TCK3 + st3
                    lz = pr[:, tt * NE:(tt + 1) * NE]
                    pz = prb[:, tt * NE:(tt + 1) * NE]
                    wz = w8[:, tt * NE:(tt + 1) * NE]
                    den = rowm[:, 16 + st3:17 + st3]
                    i5 = v.tensor_tensor(den, rowm[:, 8 + st3 * 2:9 + st3 * 2],
                                         rowm[:, 9 + st3 * 2:10 + st3 * 2], ALU.add)
                    p.vg(v, i5)
                    i6 = v.reciprocal(den, den)
                    p.vg(v, i6)
                    i7 = v.tensor_scalar(wz, lz, rowm[:, st3 * 2 + 1:st3 * 2 + 2],
                                         None, ALU.is_ge)
                    p.vg(v, i7)
                    i8 = v.tensor_tensor(wz, wz, pz, ALU.mult)
                    p.vg(v, i8)
                    i9 = v.tensor_scalar(wz, wz, den, None, ALU.mult)
                    if st3 < TCK3 - 1:
                        p.vg(v, i9)
                return i9
            C.step("vector", v_top2b)
            C.step("tensor", lambda t, tc=tc: _last(*[
                t.transpose(PSb(6, st3 * 128, 128)[0:NE, :],
                            w8[:, (tc * TCK3 + st3) * NE:(tc * TCK3 + st3 + 1) * NE],
                            idf[:]) for st3 in range(TCK3)]))
            def v_w8c(v, tc=tc):
                for st3 in range(TCK3):
                    tt = tc * TCK3 + st3
                    ins = v.tensor_copy(w8T[:, tt * 128:(tt + 1) * 128],
                                        PSb(6, st3 * 128, 128)[0:NE, :])
                return ins
            C.step("vector", v_w8c)
            def v_cast(v, tc=tc):
                for pt in range(NPT):
                    i1 = v.tensor_copy(lnc[:, pt * TC:(pt + 1) * TC],
                                       lnf[:, pt * TC:(pt + 1) * TC])
                return i1
            C.step("vector", v_cast)
            for pt in range(NPT):
                C.step("sync", lambda sy, pt=pt, tc=tc: sy.dma_start(
                    out=d_ln1[pt * 128:(pt + 1) * 128, tc * TC:(tc + 1) * TC],
                    in_=lnc[:, pt * TC:(pt + 1) * TC]), dma=True)
            if SPARSE_MOE:
                for st3 in range(TC // 128):
                    def t_tm(t, st3=st3, tc=tc):
                        for pt in range(NPT):
                            ins = t.transpose(
                                PSb(pt // 4, (pt % 4) * 128, 128),
                                lnf[:, pt * TC + st3 * 128: pt * TC + (st3 + 1) * 128],
                                idf[:])
                        return ins
                    C.step("tensor", t_tm)
                    def v_tm(v, st3=st3):
                        for pt in range(NPT):
                            ins = v.tensor_copy(lntm0[:, pt * 128:(pt + 1) * 128],
                                                PSb(pt // 4, (pt % 4) * 128, 128))
                        return ins
                    C.step("vector", v_tm)
                    C.step("sync", lambda sy, st3=st3, tc=tc: sy.dma_start(
                        out=dram["ln1_tm"][tc * TC + st3 * 128: tc * TC + (st3 + 1) * 128, :],
                        in_=lntm0[:]), dma=True)
            C.flush()

    if SPARSE_MOE:
        emit_b15(h)
        emit_b2_sparse(h)
        return

        with contextlib.ExitStack() as ph:
            wg = ph.enter_context(nc.sbuf_tensor("wg_b", [128, NPT * I_FF], BF16))
            wu = ph.enter_context(nc.sbuf_tensor("wu_b", [128, NPT * I_FF], BF16))
            wd = ph.enter_context(nc.sbuf_tensor("wd_b", [128, NKI * D], BF16))
            lntc = ph.enter_context(nc.sbuf_tensor("lntc_b", [128, 2 * NPT * TC], BF16))
            gu = ph.enter_context(nc.sbuf_tensor("gu_b", [128, NKI * TC], BF16))
            wbb = ph.enter_context(nc.sbuf_tensor("wbb_b", [128, TC], F32))
            wrow = ph.enter_context(nc.sbuf_tensor("wrow_b", [1, TC], F32))
            tmp5 = ph.enter_context(nc.sbuf_tensor("tmp5_b", [128, 2 * TC], F32))
            WgT = lambda kt, mt: wg[:, kt * I_FF + mt * 128: kt * I_FF + (mt + 1) * 128]
            WuT = lambda kt, mt: wu[:, kt * I_FF + mt * 128: kt * I_FF + (mt + 1) * 128]
            WdT = lambda kt, mt: wd[:, kt * D + mt * 128: kt * D + (mt + 1) * 128]
            LNS = lambda pair: lntc[:, (pair % 2) * NPT * TC:(pair % 2 + 1) * NPT * TC]
            TMP = lambda i8: tmp5[:, (i8 % 2) * TC:(i8 % 2 + 1) * TC]

            sGt, sSl, sUp, sGv, sDn, sEv, sBc, sWc = (p.sem(x) for x in
                ("b2Gt", "b2Sl", "b2Up", "b2Gv", "b2Dn", "b2Ev", "b2Bc", "b2Wc"))
            sWg, sWu, sWd = p.sem("b2Wg"), p.sem("b2Wu"), p.sem("b2Wd")
            sLn = [p.sem("b2Ln0"), p.sem("b2Ln1")]
            sWr = p.sem("b2Wr")
            sAc = [p.sem("b2Ac0"), p.sem("b2Ac1")]
            C.flush()
            tl = C.tail()
            NP2 = NTC * 8
            srcs_of = lambda pe: ((dram["sh_gate_T"], dram["sh_up_T"], dram["sh_down_T"])
                                  if pe == 0 else
                                  (dram["gate_w"][pe - 1], dram["up_w"][pe - 1],
                                   dram["down_w"][pe - 1]))

            @blk.gpsimd
            def _(g):
                if tl:
                    g.wait_ge(p.sem(tl[0]), tl[1])
                for pe in range(NE + 1):
                    srcs = srcs_of(pe)
                    if pe >= 1:
                        g.wait_ge(sGt, pe * NTC * 16)
                    g.dma_start(out=wg[:].rearrange("p (k n) -> p k n", n=I_FF),
                                in_=srcs[0].rearrange("(k p) n -> p k n", p=128)
                                ).then_inc(sWg, 16)
                    if pe >= 1:
                        g.wait_ge(sUp, pe * NTC * 16)
                    g.dma_start(out=wu[:].rearrange("p (k n) -> p k n", n=I_FF),
                                in_=srcs[1].rearrange("(k p) n -> p k n", p=128)
                                ).then_inc(sWu, 16)
                    if pe >= 1:
                        g.wait_ge(sDn, pe * NP2)
                        g.wait_ge(sAc[0], (pe * NP2 // 2) * 16)
                        g.wait_ge(sAc[1], (pe * NP2 // 2) * 16)
                    g.dma_start(out=wd[:].rearrange("p (k n) -> p k n", n=D),
                                in_=srcs[2].rearrange("(k p) n -> p k n", p=128)
                                ).then_inc(sWd, 16)
                    for tc in range(NTC):
                        for mtD in range(NPT):
                            i8 = (pe * NTC + tc) * 8 + mtD
                            g.wait_ge(sEv, i8 + 1)
                            g.dma_start(
                                out=d_xres[:, mtD * TP + tc * TC: mtD * TP + (tc + 1) * TC],
                                in_=TMP(i8), accum_op=ALU.add).then_inc(sAc[i8 % 2], 16)

            @blk.sync
            def _(sy):
                if tl:
                    sy.wait_ge(p.sem(tl[0]), tl[1])
                for pe in range(NE + 1):
                    for tc in range(NTC):
                        pair = pe * NTC + tc
                        if pair >= 2:
                            sy.wait_ge(sUp, (pair - 1) * 16)
                        sy.dma_start(out=LNS(pair).rearrange("p (k n) -> p k n", n=TC),
                                     in_=d_ln1[:, tc * TC:(tc + 1) * TC].rearrange(
                                         "(k p) n -> p k n", p=128)
                                     ).then_inc(sLn[pair % 2], 16)
                        if pe >= 1:
                            wb = (pe - 1) * NTC + tc
                            sy.wait_ge(sBc, wb)
                            sy.dma_start(out=wrow[:],
                                         in_=w8T[pe - 1:pe, tc * TC:(tc + 1) * TC]
                                         ).then_inc(sWr, 16)

            @blk.tensor
            def _(t):
                if tl:
                    t.wait_ge(p.sem(tl[0]), tl[1])
                for pe in range(NE + 1):
                    for tc in range(NTC):
                        pair = pe * NTC + tc
                        t.wait_ge(sLn[pair % 2], (pair // 2 + 1) * 16)
                        if pe >= 1:
                            wb = (pe - 1) * NTC + tc
                            t.wait_ge(sWr, (wb + 1) * 16)
                            if wb >= 1:
                                t.wait_ge(sWc, wb)
                            t.matmul(PSb(6, 0, TC), onesrf[:], wrow[0:1, :],
                                     start=True, stop=True,
                                     skip_group_check=True).then_inc(sBc, 1)
                        if tc == 0:
                            t.wait_ge(sWg, (pe + 1) * 16)
                            t.wait_ge(sWu, (pe + 1) * 16)
                        for mt in range(NKI):
                            i16 = pair * 16 + mt
                            if i16 >= 2:
                                t.wait_ge(sSl, i16 - 1)
                            for kt in range(NPT):
                                ins = t.matmul(PSb(mt % 2, 0, TC), WgT(kt, mt),
                                               LNS(pair)[:, kt * TC:(kt + 1) * TC],
                                               start=(kt == 0), stop=(kt == NPT - 1),
                                               skip_group_check=True)
                            ins.then_inc(sGt, 1)
                            if i16 >= 2:
                                t.wait_ge(sGv, i16 - 1)
                            for kt in range(NPT):
                                ins = t.matmul(PSb(2 + mt % 2, 0, TC), WuT(kt, mt),
                                               LNS(pair)[:, kt * TC:(kt + 1) * TC],
                                               start=(kt == 0), stop=(kt == NPT - 1),
                                               skip_group_check=True)
                            ins.then_inc(sUp, 1)
                        if tc == 0:
                            t.wait_ge(sWd, (pe + 1) * 16)
                        t.wait_ge(sGv, pair * 16 + 16)
                        for mtD in range(NPT):
                            i8 = pair * 8 + mtD
                            if i8 >= 2:
                                t.wait_ge(sEv, i8 - 1)
                            for kt in range(NKI):
                                ins = t.matmul(PSb(4 + mtD % 2, 0, TC), WdT(kt, mtD),
                                               gu[:, kt * TC:(kt + 1) * TC],
                                               start=(kt == 0), stop=(kt == NKI - 1),
                                               skip_group_check=True)
                            ins.then_inc(sDn, 1)

            @blk.scalar
            def _(s):
                for pe in range(NE + 1):
                    for tc in range(NTC):
                        pair = pe * NTC + tc
                        for mt in range(NKI):
                            i16 = pair * 16 + mt
                            s.wait_ge(sGt, i16 + 1)
                            if pair >= 1:
                                s.wait_ge(sDn, pair * 8)
                            s.activation(gu[:, mt * TC:(mt + 1) * TC],
                                         PSb(mt % 2, 0, TC), AF.Silu).then_inc(sSl, 1)

            @blk.vector
            def _(v):
                for pe in range(NE + 1):
                    for tc in range(NTC):
                        pair = pe * NTC + tc
                        if pe >= 1:
                            wb = (pe - 1) * NTC + tc
                            v.wait_ge(sBc, wb + 1)
                            v.wait_ge(sEv, pair * 8)
                            v.tensor_copy(wbb[:], PSb(6, 0, TC)).then_inc(sWc, 1)
                        for mt in range(NKI):
                            i16 = pair * 16 + mt
                            v.wait_ge(sUp, i16 + 1)
                            v.wait_ge(sSl, i16 + 1)
                            v.tensor_tensor(gu[:, mt * TC:(mt + 1) * TC],
                                            gu[:, mt * TC:(mt + 1) * TC],
                                            PSb(2 + mt % 2, 0, TC), ALU.mult).then_inc(sGv, 1)
                        for mtD in range(NPT):
                            i8 = pair * 8 + mtD
                            v.wait_ge(sDn, i8 + 1)
                            if i8 >= 2:
                                v.wait_ge(sAc[i8 % 2], ((i8 - 2) // 2 + 1) * 16)
                            if pe == 0:
                                v.tensor_copy(TMP(i8), PSb(4 + mtD % 2, 0, TC)
                                              ).then_inc(sEv, 1)
                            else:
                                v.tensor_tensor(TMP(i8), PSb(4 + mtD % 2, 0, TC), wbb[:],
                                                ALU.mult).then_inc(sEv, 1)

            def joinf(v):
                v.wait_ge(sAc[0], ((NE + 1) * NP2 // 2) * 16)
                v.wait_ge(sAc[1], ((NE + 1) * NP2 // 2) * 16)
                return v.tensor_copy(wbb[0:1, 0:1], wbb[0:1, 0:1])
            C.step("vector", joinf)
            C.flush()




I32 = mybir.dt.int32


def emit_b15(h):
    nc, p, blk, C, dram = h["nc"], h["p"], h["blk"], h["C"], h["dram"]
    PSb, sbuf, idf = h["PSb"], h["sbuf"], h["idf"]
    w8T = h["w8T"]
    h["fi1"] = fi1 = sbuf("fi1", [128, NTT], I32)
    h["fi2"] = fi2 = sbuf("fi2", [128, NTT], I32)
    h["w1T"] = w1T = sbuf("w1T", [128, NTT], F32)
    h["w2T"] = w2T = sbuf("w2T", [128, NTT], F32)
    with contextlib.ExitStack() as s0:
        sb = lambda nm, shp, dt=F32: s0.enter_context(nc.sbuf_tensor(nm, shp, dt))
        m8 = sb("m8_i", [8, TP]); pf8 = sb("pf8_i", [8, TP])
        slm = sb("slm_i", [8, TP]); cnt8 = sb("cnt8_i", [8, TP])
        r1 = sb("r1_i", [8, TP]); r2 = sb("r2_i", [8, TP])
        son = sb("son_i", [8, TP]); bigv = sb("bigv_i", [8, TP])
        ltm = sb("ltm_i", [8, 8]); ec8 = sb("ec8_i", [128, 8])
        t8 = sb("t8_i", [128, 48]); f1f = sb("f1f_i", [128, NTT])
        f2f = sb("f2f_i", [128, NTT])

        def v_prep(v):
            i = v.memset(son[:], 1.0)
            p.vg(v, i)
            i = v.tensor_scalar(m8[:], w8T[:], 0.0, None, ALU.is_gt)
            p.vg(v, i)
            i = v.tensor_tensor_scan(pf8[:], son[:], m8[:], 0.0, ALU.mult, ALU.add)
            p.vg(v, i)
            i = v.tensor_scalar(bigv[:], m8[:], -BIGF, BIGF, ALU.mult, ALU.add)
            p.vg(v, i)
            i = v.scalar_tensor_tensor(slm[:], pf8[:], -1.0, m8[:], ALU.add, ALU.mult)
            p.vg(v, i)
            i = v.tensor_tensor(slm[:], slm[:], bigv[:], ALU.add)
            p.vg(v, i)
            i = v.tensor_scalar(son[:], slm[:], float(CAP), BIGF, ALU.is_ge, ALU.mult)
            p.vg(v, i)
            return v.tensor_tensor(slm[:], slm[:], son[:], ALU.add)
        C.step("vector", v_prep)
        C.step("gpsimd", lambda g: _last(
            g.iota(ltm[:], pattern=[[-1, NE]], channel_multiplier=1,
                   allow_small_or_imprecise_dtypes=True),
            g.iota(ec8[:], pattern=[[CAP, NE]], channel_multiplier=0,
                   allow_small_or_imprecise_dtypes=True)))
        C.step("vector", lambda v: v.tensor_scalar(ltm[:], ltm[:], 0.0, None,
                                                   ALU.is_lt))
        for tcc in range(NTC):
            C.step("tensor", lambda t, tcc=tcc: t.matmul(
                PSb(0, 0, TC)[0:8, :], ltm[0:8, 0:8],
                m8[:, tcc * TC:(tcc + 1) * TC], start=True, stop=True))
            C.step("vector", lambda v, tcc=tcc: v.tensor_copy(
                cnt8[:, tcc * TC:(tcc + 1) * TC], PSb(0, 0, TC)[0:8, :]))

        def v_rank(v):
            i = v.tensor_scalar(son[:], cnt8[:], 0.0, None, ALU.is_equal)
            p.vg(v, i)
            i = v.tensor_tensor(r1[:], son[:], m8[:], ALU.mult)
            p.vg(v, i)
            i = v.tensor_scalar(son[:], cnt8[:], 1.0, None, ALU.is_equal)
            p.vg(v, i)
            return v.tensor_tensor(r2[:], son[:], m8[:], ALU.mult)
        C.step("vector", v_rank)

        for ch in range(NTT):
            def t_tr(t, ch=ch):
                for k, x in enumerate((slm, r1, r2, w8T)):
                    ins = t.transpose(PSb(1, k * 8, 8),
                                      x[:, ch * 128:(ch + 1) * 128],
                                      idf[0:8, 0:8])
                return ins
            C.step("tensor", t_tr)
            def v_col(v, ch=ch):
                stp, scp, r1t, r2t, w8t, acc = (t8[:, k * 8:(k + 1) * 8]
                                                for k in range(6))
                i = v.tensor_copy(stp, PSb(1, 0, 8))
                p.vg(v, i)
                i = v.tensor_tensor(scp, stp, ec8[:], ALU.add)
                i2 = v.tensor_copy(r1t, PSb(1, 8, 8))
                i3 = v.tensor_copy(r2t, PSb(1, 16, 8))
                i4 = v.tensor_copy(w8t, PSb(1, 24, 8))
                p.vg(v, i4)
                i = v.tensor_tensor(acc, scp, r1t, ALU.mult)
                p.vg(v, i)
                i = v.reduce_sum(f1f[:, ch:ch + 1], acc, axis=mybir.AxisListType.X)
                p.vg(v, i)
                i = v.tensor_tensor(acc, scp, r2t, ALU.mult)
                p.vg(v, i)
                i = v.reduce_sum(f2f[:, ch:ch + 1], acc, axis=mybir.AxisListType.X)
                p.vg(v, i)
                i = v.tensor_tensor(acc, w8t, r1t, ALU.mult)
                p.vg(v, i)
                i = v.reduce_sum(w1T[:, ch:ch + 1], acc, axis=mybir.AxisListType.X)
                p.vg(v, i)
                i = v.tensor_tensor(acc, w8t, r2t, ALU.mult)
                p.vg(v, i)
                return v.reduce_sum(w2T[:, ch:ch + 1], acc, axis=mybir.AxisListType.X)
            C.step("vector", v_col)

        def v_cast(v):
            i = v.tensor_scalar(f1f[:], f1f[:], float(DUMP), None, ALU.min)
            p.vg(v, i)
            i = v.tensor_scalar(f2f[:], f2f[:], float(DUMP), None, ALU.min)
            p.vg(v, i)
            i = v.tensor_copy(fi1[:], f1f[:])
            p.vg(v, i)
            return v.tensor_copy(fi2[:], f2f[:])
        C.step("vector", v_cast)
        C.flush()


def emit_b2_sparse(h):
    nc, p, blk, C, dram = h["nc"], h["p"], h["blk"], h["C"], h["dram"]
    PSb, sbuf, idf, idb, onesrf = (h[k] for k in
                                   ("PSb", "sbuf", "idf", "idb", "onesrf"))
    d_xres, d_ln1 = dram["xres"], dram["ln1_scr"]
    d_tm, d_xg, d_eo = dram["ln1_tm"], dram["xg_e"], dram["eout_e"]
    fi1, fi2, w1T, w2T = h["fi1"], h["fi2"], h["w1T"], h["w2T"]
    NKI = I_FF // 128
    PSf = h["PSf"]
    PSB16 = lambda bank, kt: PSf[:, bank * BANK + kt * 64:
                                 bank * BANK + kt * 64 + 64].bitcast(BF16)

    with contextlib.ExitStack() as ph:
        wg = ph.enter_context(nc.sbuf_tensor("wg_b", [128, NPT * I_FF], BF16))
        wu = ph.enter_context(nc.sbuf_tensor("wu_b", [128, NPT * I_FF], BF16))
        wd = ph.enter_context(nc.sbuf_tensor("wd_b", [128, NKI * D], BF16))
        lntc = ph.enter_context(nc.sbuf_tensor("lntc_b", [128, NPT * TC], BF16))
        gu = ph.enter_context(nc.sbuf_tensor("gu_b", [128, 2 * NKI * TC], BF16))
        xeT = ph.enter_context(nc.sbuf_tensor("xeT_b", [128, 2 * NPT * TC], BF16))
        xgl = ph.enter_context(nc.sbuf_tensor("xgl_b", [128, 2 * D], BF16))
        stg = ph.enter_context(nc.sbuf_tensor("stg_b", [128, D], F32))
        tmp5 = ph.enter_context(nc.sbuf_tensor("tmp5_b", [128, 2 * TC], F32))
        lntm = ph.enter_context(nc.sbuf_tensor("lntm_b", [128, 2 * D], BF16))
        WgT = lambda kt, mt: wg[:, kt * I_FF + mt * 128: kt * I_FF + (mt + 1) * 128]
        WuT = lambda kt, mt: wu[:, kt * I_FF + mt * 128: kt * I_FF + (mt + 1) * 128]
        WdT = lambda kt, mt: wd[:, kt * D + mt * 128: kt * D + (mt + 1) * 128]
        GU = lambda q: gu[:, (q % 2) * NKI * TC:(q % 2 + 1) * NKI * TC]
        XE = lambda cg: xeT[:, (cg % 2) * NPT * TC:(cg % 2 + 1) * NPT * TC]
        TMP = lambda i8: tmp5[:, (i8 % 2) * TC:(i8 % 2 + 1) * TC]

        hLn = [p.sem("shLn0"), p.sem("shLn1")]
        hGt, hSl, hUp, hGv, hDn, hEv = (p.sem(x) for x in
            ("shGt", "shSl", "shUp", "shGv", "shDn", "shEv"))
        hAc = [p.sem("shAc0"), p.sem("shAc1")]
        hW = p.sem("shW")
        sLt = [p.sem("dsLt0"), p.sem("dsLt1")]
        sDx = [p.sem("dsDx0"), p.sem("dsDx1")]
        rWg, rWu, rWd = p.sem("rtWg"), p.sem("rtWu"), p.sem("rtWd")
        rXg = [p.sem("rtXg0"), p.sem("rtXg1")]
        rTr, rTe = p.sem("rtTr"), p.sem("rtTe")
        rG, rSl, rU, rGv = p.sem("rtG"), p.sem("rtSl"), p.sem("rtU"), p.sem("rtGv")
        rDn, rDe = p.sem("rtDn"), p.sem("rtDe")
        rSt = [p.sem("rtSt0"), p.sem("rtSt1")]

        def v_z(v):
            return v.memset(stg[0:1, 0:D], 0.0)
        C.step("vector", v_z)
        C.step("sync", lambda sy: sy.dma_start(out=d_eo[DUMP:DUMP + 1, :],
                                               in_=stg[0:1, 0:D]), dma=True)
        C.flush()
        tl = C.tail()

        xg_wait = {}
        xgc = [0, 0]
        st_wait = {}
        stc = [0, 0]
        for pe in range(1, NE + 1):
            for cc in range(NCC):
                xgc[cc % 2] += 1
                xg_wait[(pe, cc)] = (cc % 2, xgc[cc % 2])
            for cc in range(NCC):
                stc[cc % 2] += 1
                st_wait[(pe, cc)] = (cc % 2, stc[cc % 2])
        xg_tot = tuple(xgc)
        st_tot = tuple(stc)
        SCT0 = 32 * ((NTT + 1) // 2)
        SCT1 = 32 * (NTT // 2)

        def gu_free_wait(e, q):
            if q < 2:
                return
            qq = q - 2
            if qq < NTC:
                e.wait_ge(hDn, (qq + 1) * NPT)
            else:
                e.wait_ge(rDn, (qq - NTC + 1) * TCK * 2)

        @blk.gpsimd
        def _(g):
            g.wait_ge(p.sem(tl[0]), tl[1])
            g.dma_start(out=wg[:].rearrange("p (k n) -> p k n", n=I_FF),
                        in_=dram["sh_gate_T"].rearrange("(k p) n -> p k n", p=128)
                        ).then_inc(hW, 16)
            g.dma_start(out=wu[:].rearrange("p (k n) -> p k n", n=I_FF),
                        in_=dram["sh_up_T"].rearrange("(k p) n -> p k n", p=128)
                        ).then_inc(hW, 16)
            g.dma_start(out=wd[:].rearrange("p (k n) -> p k n", n=D),
                        in_=dram["sh_down_T"].rearrange("(k p) n -> p k n", p=128)
                        ).then_inc(hW, 16)

        @blk.gpsimd
        def _(g):
            g.wait_ge(p.sem(tl[0]), tl[1])
            for ch in range(0 if "dist" in DBG_SKIP else NTT):
                g.wait_ge(sLt[ch % 2], (ch // 2 + 1) * 16)
                src = lntm[:, (ch % 2) * D:(ch % 2 + 1) * D]
                g.indirect_dma_start(
                    out=d_xg, out_offset=IndirectOffsetOnAxis(fi1[:, ch:ch + 1], 0),
                    in_=src, in_offset=None).then_inc(sDx[ch % 2], 16)
                g.indirect_dma_start(
                    out=d_xg, out_offset=IndirectOffsetOnAxis(fi2[:, ch:ch + 1], 0),
                    in_=src, in_offset=None).then_inc(sDx[ch % 2], 16)

        @blk.sync
        def _(sy):
            sy.wait_ge(p.sem(tl[0]), tl[1])
            for ch in range(NTT):
                if ch >= 2:
                    sy.wait_ge(sDx[ch % 2], (ch // 2) * 32)
                sy.dma_start(out=lntm[:, (ch % 2) * D:(ch % 2 + 1) * D],
                             in_=d_tm[ch * 128:(ch + 1) * 128, :]
                             ).then_inc(sLt[ch % 2], 16)
            for tc in range(NTC):
                if tc >= 1:
                    sy.wait_ge(hUp, tc * NKI)
                sy.dma_start(out=lntc[:].rearrange("p (k n) -> p k n", n=TC),
                             in_=d_ln1[:, tc * TC:(tc + 1) * TC].rearrange(
                                 "(k p) n -> p k n", p=128)
                             ).then_inc(hLn[tc % 2], 16)
            for pe in range(1, (0 if "routed" in DBG_SKIP else NE) + 1):
                def xg_load(sy, cc):
                    if pe == 1 and cc == 0:
                        sy.wait_ge(sDx[0], SCT0)
                        sy.wait_ge(sDx[1], SCT1)
                    tidx = (pe - 1) * NCC + cc
                    if tidx >= 2:
                        sy.wait_ge(rTr, tidx - 1)
                    sy.dma_start(out=xgl[:, (cc % 2) * D:(cc % 2 + 1) * D],
                                 in_=d_xg[(pe - 1) * CAP + cc * 128:
                                          (pe - 1) * CAP + (cc + 1) * 128, :]
                                 ).then_inc(rXg[cc % 2], 16)
                def eo_store(sy, cc):
                    de_idx = (pe - 1) * NCC + cc
                    sy.wait_ge(rDe, de_idx + 1)
                    sy.dma_start(out=d_eo[(pe - 1) * CAP + cc * 128:
                                          (pe - 1) * CAP + (cc + 1) * 128, :],
                                 in_=stg[:]).then_inc(rSt[cc % 2], 16)
                for cc in range(NCC):
                    xg_load(sy, cc)
                    if cc >= TCK:
                        eo_store(sy, cc - TCK)
                for cc in range(NCC - TCK, NCC):
                    eo_store(sy, cc)

        @blk.tensor
        def _(t):
            t.wait_ge(p.sem(tl[0]), tl[1])
            for tc in range(NTC):
                t.wait_ge(hLn[tc % 2], (tc // 2 + 1) * 16)
                if tc == 0:
                    t.wait_ge(hW, 48)
                for mt in range(NKI):
                    i16 = tc * NKI + mt
                    if i16 >= 2:
                        t.wait_ge(hSl, i16 - 1)
                    _last(*[t.matmul(PSb(mt % 2, 0, TC), WgT(kt, mt),
                                     lntc[:, kt * TC:(kt + 1) * TC],
                                     start=(kt == 0), stop=(kt == NPT - 1),
                                     skip_group_check=True)
                            for kt in range(NPT)]).then_inc(hGt, 1)
                    if i16 >= 2:
                        t.wait_ge(hGv, i16 - 1)
                    _last(*[t.matmul(PSb(2 + mt % 2, 0, TC), WuT(kt, mt),
                                     lntc[:, kt * TC:(kt + 1) * TC],
                                     start=(kt == 0), stop=(kt == NPT - 1),
                                     skip_group_check=True)
                            for kt in range(NPT)]).then_inc(hUp, 1)

                t.wait_ge(hGv, (tc + 1) * NKI)
                for mtD in range(NPT):
                    i8 = tc * NPT + mtD
                    if i8 >= 2:
                        t.wait_ge(hEv, i8 - 1)
                    _last(*[t.matmul(PSb(4 + mtD % 2, 0, TC), WdT(kt, mtD),
                                     GU(tc)[:, kt * TC:(kt + 1) * TC],
                                     start=(kt == 0), stop=(kt == NKI - 1),
                                     skip_group_check=True)
                            for kt in range(NKI)]).then_inc(hDn, 1)
            for pe in range(1, (0 if "routed" in DBG_SKIP else NE) + 1):
                for ch in range(NCE):
                    cg = (pe - 1) * NCE + ch
                    for cc in range(ch * TCK, (ch + 1) * TCK):
                        tidx = (pe - 1) * NCC + cc
                        par, cnt = xg_wait[(pe, cc)]
                        t.wait_ge(rXg[par], cnt * 16)
                        if tidx >= 2:
                            t.wait_ge(rTe, tidx - 1)
                        if cg >= 2 and cc % TCK == 0:
                            t.wait_ge(rU, (cg - 1) * NKI)
                        _last(*[t.transpose(PSB16(6 + cc % 2, kt),
                                            xgl[:, (cc % 2) * D + kt * 128:
                                                (cc % 2) * D + (kt + 1) * 128],
                                            idb[:])
                                for kt in range(NPT)]).then_inc(rTr, 1)
                    if ch == 0:
                        t.wait_ge(rWg, pe * 16)
                        t.wait_ge(rWu, pe * 16)
                    t.wait_ge(rTe, (pe - 1) * NCC + (ch + 1) * TCK)
                    for mt in range(NKI):
                        gidx = cg * NKI + mt
                        if gidx >= 2:
                            t.wait_ge(rSl, gidx - 1)
                        elif pe == 1:
                            t.wait_ge(hSl, NTC * NKI)
                            t.wait_ge(hGv, NTC * NKI)
                        _last(*[t.matmul(PSb(mt % 2, 0, TC), WgT(kt, mt),
                                         XE(cg)[:, kt * TC:(kt + 1) * TC],
                                         start=(kt == 0), stop=(kt == NPT - 1),
                                         skip_group_check=True)
                                for kt in range(NPT)]).then_inc(rG, 1)
                        if gidx >= 2:
                            t.wait_ge(rGv, gidx - 1)
                        _last(*[t.matmul(PSb(2 + mt % 2, 0, TC), WuT(kt, mt),
                                         XE(cg)[:, kt * TC:(kt + 1) * TC],
                                         start=(kt == 0), stop=(kt == NPT - 1),
                                         skip_group_check=True)
                                for kt in range(NPT)]).then_inc(rU, 1)
                    if ch == 0:
                        t.wait_ge(rWd, pe * 16)
                    t.wait_ge(rGv, (cg + 1) * NKI)
                    q = NTC + cg
                    for cc in range(ch * TCK, (ch + 1) * TCK):
                        de_idx = (pe - 1) * NCC + cc
                        if de_idx >= 1:
                            t.wait_ge(rDe, de_idx)
                        if pe == 1 and cc == 0:
                            t.wait_ge(hEv, NTC * NPT)
                        for nh in range(2):
                            _last(*[t.matmul(
                                PSb(4 + nh, 0, 512),
                                GU(q)[:, kt * TC + (cc % TCK) * 128:
                                      kt * TC + (cc % TCK) * 128 + 128],
                                wd[:, kt * D + nh * 512: kt * D + (nh + 1) * 512],
                                start=(kt == 0), stop=(kt == NKI - 1),
                                skip_group_check=True)
                                for kt in range(NKI)]).then_inc(rDn, 1)

        @blk.scalar
        def _(s):
            for tc in range(NTC):
                for mt in range(NKI):
                    i16 = tc * NKI + mt
                    s.wait_ge(hGt, i16 + 1)
                    if mt == 0:
                        gu_free_wait(s, tc)
                    s.activation(GU(tc)[:, mt * TC:(mt + 1) * TC],
                                 PSb(mt % 2, 0, TC), AF.Silu).then_inc(hSl, 1)
            for pe in range(1, (0 if "routed" in DBG_SKIP else NE) + 1):
                for ch in range(NCE):
                    cg = (pe - 1) * NCE + ch
                    q = NTC + cg
                    for mt in range(NKI):
                        gidx = cg * NKI + mt
                        s.wait_ge(rG, gidx + 1)
                        if mt == 0:
                            gu_free_wait(s, q)
                        s.activation(GU(q)[:, mt * TC:(mt + 1) * TC],
                                     PSb(mt % 2, 0, TC), AF.Silu).then_inc(rSl, 1)

        def _dn_evac(v, pe, cc):
            de_idx = (pe - 1) * NCC + cc
            v.wait_ge(rDn, de_idx * 2 + 2)
            if de_idx >= 1:
                pr = st_wait[(pe, cc - 1)] if cc >= 1 else st_wait[(pe - 1, NCC - 1)]
                v.wait_ge(rSt[pr[0]], pr[1] * 16)
            i = v.tensor_copy(stg[:, 0:512], PSb(4, 0, 512))
            p.vg(v, i)
            v.tensor_copy(stg[:, 512:1024], PSb(5, 0, 512)).then_inc(rDe, 1)

        @blk.vector
        def _(v):
            v.wait_ge(p.sem(tl[0]), tl[1])
            for tc in range(NTC):
                for mt in range(NKI):
                    i16 = tc * NKI + mt
                    v.wait_ge(hUp, i16 + 1)
                    v.wait_ge(hSl, i16 + 1)
                    v.tensor_tensor(GU(tc)[:, mt * TC:(mt + 1) * TC],
                                    GU(tc)[:, mt * TC:(mt + 1) * TC],
                                    PSb(2 + mt % 2, 0, TC), ALU.mult).then_inc(hGv, 1)
                for mtD in range(NPT):
                    i8 = tc * NPT + mtD
                    v.wait_ge(hDn, i8 + 1)
                    if i8 >= 2:
                        v.wait_ge(hAc[i8 % 2], ((i8 - 2) // 2 + 1) * 16)
                    v.tensor_copy(TMP(i8), PSb(4 + mtD % 2, 0, TC)).then_inc(hEv, 1)
            for pe in range(1, (0 if "routed" in DBG_SKIP else NE) + 1):
                for ch in range(NCE):
                    cg = (pe - 1) * NCE + ch
                    for cc in range(ch * TCK, (ch + 1) * TCK):
                        tidx = (pe - 1) * NCC + cc
                        v.wait_ge(rTr, tidx + 1)
                        _last(*[v.tensor_copy(
                            XE(cg)[:, kt * TC + (cc % TCK) * 128:
                                   kt * TC + (cc % TCK) * 128 + 128],
                            PSB16(6 + cc % 2, kt))
                            for kt in range(NPT)]).then_inc(rTe, 1)
                    q = NTC + cg
                    for mt in range(NKI):
                        gidx = cg * NKI + mt
                        v.wait_ge(rU, gidx + 1)
                        v.wait_ge(rSl, gidx + 1)
                        v.tensor_tensor(GU(q)[:, mt * TC:(mt + 1) * TC],
                                        GU(q)[:, mt * TC:(mt + 1) * TC],
                                        PSb(2 + mt % 2, 0, TC), ALU.mult
                                        ).then_inc(rGv, 1)
                    for cc in range(ch * TCK, (ch + 1) * TCK):
                        _dn_evac(v, pe, cc)

        @blk.gpsimd
        def _(g):
            for tc in range(NTC):
                for mtD in range(NPT):
                    i8 = tc * NPT + mtD
                    g.wait_ge(hEv, i8 + 1)
                    g.dma_start(
                        out=d_xres[:, mtD * TP + tc * TC: mtD * TP + (tc + 1) * TC],
                        in_=TMP(i8), accum_op=ALU.add).then_inc(hAc[i8 % 2], 16)

        @blk.gpsimd
        def _(g):
            for pe in range(1, (0 if "routed" in DBG_SKIP else NE) + 1):
                if pe == 1:
                    g.wait_ge(hGt, NTC * NKI)
                else:
                    g.wait_ge(rG, (pe - 1) * NCE * NKI)
                g.dma_start(out=wg[:].rearrange("p (k n) -> p k n", n=I_FF),
                            in_=dram["gate_w"][pe - 1].rearrange(
                                "(k p) n -> p k n", p=128)).then_inc(rWg, 16)
                if pe == 1:
                    g.wait_ge(hUp, NTC * NKI)
                else:
                    g.wait_ge(rU, (pe - 1) * NCE * NKI)
                g.dma_start(out=wu[:].rearrange("p (k n) -> p k n", n=I_FF),
                            in_=dram["up_w"][pe - 1].rearrange(
                                "(k p) n -> p k n", p=128)).then_inc(rWu, 16)
                if pe == 1:
                    g.wait_ge(hDn, NTC * NPT)
                else:
                    g.wait_ge(rDn, (pe - 1) * NCC * 2)
                g.dma_start(out=wd[:].rearrange("p (k n) -> p k n", n=D),
                            in_=dram["down_w"][pe - 1].rearrange(
                                "(k p) n -> p k n", p=128)).then_inc(rWd, 16)

        def b2join(v):
            v.wait_ge(hAc[0], (NTC * NPT // 2) * 16)
            v.wait_ge(hAc[1], (NTC * NPT // 2) * 16)
            if "routed" not in DBG_SKIP:
                v.wait_ge(rSt[0], st_tot[0] * 16)
                v.wait_ge(rSt[1], st_tot[1] * 16)
            elif "dist" not in DBG_SKIP:
                v.wait_ge(sDx[0], SCT0)
                v.wait_ge(sDx[1], SCT1)
            return v.tensor_copy(idf[0:1, 0:1], idf[0:1, 0:1])
        C.step("vector", b2join)
        C.flush()

    if "combine" in DBG_SKIP:
        return
    with contextlib.ExitStack() as cs:
        o1 = cs.enter_context(nc.sbuf_tensor("o1_c", [128, 2 * D], F32))
        o2 = cs.enter_context(nc.sbuf_tensor("o2_c", [128, 2 * D], F32))
        rT = cs.enter_context(nc.sbuf_tensor("rT_c", [128, 2 * D], F32))
        acb = cs.enter_context(nc.sbuf_tensor("acb_c", [128, 2 * D], F32))
        cGa = [p.sem("cbGa0"), p.sem("cbGa1")]
        cSv, cTt, cEe = (p.sem(x) for x in ("cbSv", "cbTt", "cbEe"))
        cAa = [p.sem("cbAa0"), p.sem("cbAa1")]
        C.flush()
        tl2 = C.tail()

        @blk.gpsimd
        def _(g):
            g.wait_ge(p.sem(tl2[0]), tl2[1])
            for ch in range(NTT):
                if ch >= 2:
                    g.wait_ge(cSv, ch - 1)
                g.indirect_dma_start(
                    out=o1[:, (ch % 2) * D:(ch % 2 + 1) * D], out_offset=None,
                    in_=d_eo, in_offset=IndirectOffsetOnAxis(fi1[:, ch:ch + 1], 0)
                    ).then_inc(cGa[ch % 2], 16)
                g.indirect_dma_start(
                    out=o2[:, (ch % 2) * D:(ch % 2 + 1) * D], out_offset=None,
                    in_=d_eo, in_offset=IndirectOffsetOnAxis(fi2[:, ch:ch + 1], 0)
                    ).then_inc(cGa[ch % 2], 16)
                if ch >= 1:
                    g.wait_ge(cEe, ch)
                    for pt in range(NPT):
                        g.dma_start(
                            out=d_xres[:, pt * TP + (ch - 1) * 128:
                                       pt * TP + ch * 128],
                            in_=acb[:, ((ch - 1) % 2) * D + pt * 128:
                                    ((ch - 1) % 2) * D + (pt + 1) * 128],
                            accum_op=ALU.add).then_inc(cAa[(ch - 1) % 2], 16)
            g.wait_ge(cEe, NTT)
            for pt in range(NPT):
                g.dma_start(
                    out=d_xres[:, pt * TP + (NTT - 1) * 128: pt * TP + NTT * 128],
                    in_=acb[:, ((NTT - 1) % 2) * D + pt * 128:
                            ((NTT - 1) % 2) * D + (pt + 1) * 128],
                    accum_op=ALU.add).then_inc(cAa[(NTT - 1) % 2], 16)

        @blk.vector
        def _(v):
            v.wait_ge(p.sem(tl2[0]), tl2[1])
            for ch in range(NTT):
                v.wait_ge(cGa[ch % 2], (ch // 2 + 1) * 32)
                sl = (ch % 2) * D
                i = v.tensor_scalar(rT[:, sl:sl + D], o1[:, sl:sl + D],
                                    w1T[:, ch:ch + 1], None, ALU.mult)
                p.vg(v, i)
                v.scalar_tensor_tensor(rT[:, sl:sl + D], o2[:, sl:sl + D],
                                       w2T[:, ch:ch + 1], rT[:, sl:sl + D],
                                       ALU.mult, ALU.add).then_inc(cSv, 1)
                v.wait_ge(cTt, ch + 1)
                if ch >= 2:
                    nacc = ((ch - 2) // 2 + 1) * NPT
                    v.wait_ge(cAa[ch % 2], nacc * 16)
                _last(*[v.tensor_copy(acb[:, sl + pt * 128: sl + (pt + 1) * 128],
                                      PSb(pt // 4, (pt % 4) * 128, 128))
                        for pt in range(NPT)]).then_inc(cEe, 1)

        @blk.tensor
        def _(t):
            t.wait_ge(p.sem(tl2[0]), tl2[1])
            for ch in range(NTT):
                t.wait_ge(cSv, ch + 1)
                if ch >= 1:
                    t.wait_ge(cEe, ch)
                sl = (ch % 2) * D
                _last(*[t.transpose(PSb(pt // 4, (pt % 4) * 128, 128),
                                    rT[:, sl + pt * 128: sl + (pt + 1) * 128],
                                    idf[:])
                        for pt in range(NPT)]).then_inc(cTt, 1)

        def cjoin(v):
            v.wait_ge(cAa[0], (((NTT + 1) // 2) * NPT) * 16)
            v.wait_ge(cAa[1], ((NTT // 2) * NPT) * 16)
            return v.tensor_copy(idf[0:1, 0:1], idf[0:1, 0:1])
        C.step("vector", cjoin)
        C.flush()


def _emit_shift_mix(h, C, ln_sb, mix_pp_list, out_list, dtmp, tc):
    p = h["p"]
    def v_mix(v):
        for pt in range(NPT):
            base = pt * (TP + 1)
            cur = ln_sb[:, base + 1 + tc * TC: base + 1 + (tc + 1) * TC]
            prv = ln_sb[:, base + tc * TC: base + (tc + 1) * TC]
            i0 = v.tensor_tensor(dtmp[:], cur, prv, ALU.subtract)
            p.vg(v, i0)
            for mix_pp, out in zip(mix_pp_list, out_list):
                i1 = v.scalar_tensor_tensor(
                    out[:, pt * TC:(pt + 1) * TC], dtmp[:], mix_pp[:, pt:pt + 1], prv,
                    ALU.mult, ALU.add)
            if pt < NPT - 1:
                p.vg(v, i1)
        return i1
    C.step("vector", v_mix)


def _emit_ln_masked(h, C, d_xres, ln_sb, name):
    nc, p, blk = h["nc"], h["p"], h["blk"]
    PSb = h["PSb"]
    RTMP, m0, r0, epsA = h["RTMP"], h["m0"], h["r0"], h["epsA"]
    ln_stats_rows, bcast_row = h["ln_stats_rows"], h["bcast_row"]
    maskb = h["maskb"]
    with contextlib.ExitStack() as s0:
        xT = s0.enter_context(nc.sbuf_tensor("xT_" + name, [128, NPT * TP], F32))
        XT = lambda pt, tc: xT[:, pt * TP + tc * TC: pt * TP + (tc + 1) * TC]
        sq = s0.enter_context(nc.sbuf_tensor("sq_" + name, [128, 2 * TC], F32))
        C.step("sync", lambda sy: sy.dma_start(out=xT[:], in_=d_xres), dma=True)
        C.step("vector", lambda v: _last(*[
            v.memset(ln_sb[:, pt * (TP + 1): pt * (TP + 1) + 1], 0.0)
            for pt in range(NPT)]))
        for tc in range(NTC):
            ln_stats_rows(C, lambda pt, tc=tc: XT(pt, tc), NPT, TC, sq,
                          m0[:, tc * TC:(tc + 1) * TC], r0[:, tc * TC:(tc + 1) * TC],
                          RTMP[:, 0:TC], epsA, D)
            bcast_row(C, m0[:, tc * TC:(tc + 1) * TC], TC, 6)
            bcast_row(C, r0[:, tc * TC:(tc + 1) * TC], TC, 7)
            def v_lnm(v, tc=tc):
                for pt in range(NPT):
                    dst = ln_sb[:, pt * (TP + 1) + 1 + tc * TC: pt * (TP + 1) + 1 + (tc + 1) * TC]
                    i0 = v.tensor_tensor(sq[:, 0:TC], XT(pt, tc), PSb(6, 0, TC), ALU.subtract)
                    p.vg(v, i0)
                    i1 = v.tensor_tensor(sq[:, 0:TC], sq[:, 0:TC], PSb(7, 0, TC), ALU.mult)
                    p.vg(v, i1)
                    i2 = v.tensor_tensor(dst, sq[:, 0:TC], maskb[:, tc * TC:(tc + 1) * TC],
                                         ALU.mult)
                    if pt < NPT - 1:
                        p.vg(v, i2)
                return i2
            C.step("vector", v_lnm)
            C.flush()


def emit_stage_c(h):
    nc, p, blk, C, dram = h["nc"], h["p"], h["blk"], h["C"], h["dram"]
    PSb, sbuf = h["PSb"], h["sbuf"]
    d_xres = dram["xres"]
    d_kf, d_vf, d_sr, d_rw = (dram[k] for k in ("kf_scr", "vf_scr", "sr_scr", "rw_scr"))
    Wt = lambda w, kt, mt: w[:, kt * D + mt * 128: kt * D + (mt + 1) * 128]

    if "maskb" not in h:
        h["maskb"] = sbuf("maskb", [128, TP], F32)
        C.step("sync", lambda sy: sy.dma_start(out=h["maskb"][:], in_=dram["mask"]),
               dma=True)
    maskb = h["maskb"]
    vecsC = sbuf("vecsC", [128, 48], F32)
    for i, nm in enumerate(("l1_w", "l1_u", "l1_mk", "l1_mv", "l1_mr")):
        C.step("sync", lambda sy, i=i, nm=nm: sy.dma_start(
            out=vecsC[:, i * 8:(i + 1) * 8],
            in_=dram[nm].rearrange("(t p) -> p t", p=128),
            allow_slow_non_contiguous=True), dma=True)
    WV, UV, MK, MV, MR = (vecsC[:, i * 8:(i + 1) * 8] for i in range(5))
    DCOL = vecsC[:, 40:48]
    EU = UV

    ln2 = sbuf("ln2_sb", [128, NPT * (TP + 1)], BF16)
    _emit_ln_masked(h, C, d_xres, ln2, "c")

    def s_wexp(s):
        s.activation(DCOL[:, :], WV[:, :], AF.Exp)
        return s.activation(vecsC[:, 8:16], UV[:, :], AF.Exp)
    C.step("scalar", s_wexp)
    C.step("vector", lambda v: v.tensor_scalar_mul(DCOL[:, :], DCOL[:, :], -1.0))
    C.step("scalar", lambda s: s.activation(DCOL[:, :], DCOL[:, :], AF.Exp))

    with contextlib.ExitStack() as s1:
        wk = s1.enter_context(nc.sbuf_tensor("wkC", [128, NPT * D], BF16))
        wv = s1.enter_context(nc.sbuf_tensor("wvC", [128, NPT * D], BF16))
        wr = s1.enter_context(nc.sbuf_tensor("wrC", [128, NPT * D], BF16))
        xk = s1.enter_context(nc.sbuf_tensor("xkC", [128, NPT * TC], BF16))
        xv = s1.enter_context(nc.sbuf_tensor("xvC", [128, NPT * TC], BF16))
        xr = s1.enter_context(nc.sbuf_tensor("xrC", [128, NPT * TC], BF16))
        dtmp = s1.enter_context(nc.sbuf_tensor("dtmpC", [128, TC], F32))
        kstK = s1.enter_context(nc.sbuf_tensor("kstKC", [128, TC], F32))
        kstV = s1.enter_context(nc.sbuf_tensor("kstVC", [128, TC], F32))
        srt = s1.enter_context(nc.sbuf_tensor("srtC", [128, TC], BF16))
        for dst, nm in ((wk, "Wk_T"), (wv, "Wv_T"), (wr, "Wr_T")):
            C.step("gpsimd", lambda g, dst=dst, nm=nm: g.dma_start(
                out=dst[:].rearrange("p (k n) -> p k n", n=D),
                in_=dram[nm].rearrange("(k p) n -> p k n", p=128)), dma=True)
        for tc in range(NTC):
            _emit_shift_mix(h, C, ln2, [MK, MV, MR], [xk, xv, xr], dtmp, tc)
            items = []
            for mt in range(NPT):
                for wgt, src, dst_hbm in ((wk, xk, d_kf), (wv, xv, d_vf)):
                    buf = kstK if wgt is wk else kstV
                    items.append({
                        "mm": lambda t, bk, wgt=wgt, src=src, mt=mt: _last(*[
                            t.matmul(PSb(bk, 0, TC), Wt(wgt, kt, mt),
                                     src[:, kt * TC:(kt + 1) * TC],
                                     start=(kt == 0), stop=(kt == NPT - 1))
                            for kt in range(NPT)]),
                        "evac": lambda e, bk, buf=buf: e.tensor_copy(
                            buf[:], PSb(bk, 0, TC)),
                        "store": lambda sy, buf=buf, mt=mt, tc=tc, dh=dst_hbm:
                            sy.dma_start(
                                out=dh[mt * 128:(mt + 1) * 128, tc * TC:(tc + 1) * TC],
                                in_=buf[:]),
                        "st_class": "k" if wgt is wk else "v",
                        "buf_lag": 3})
                items.append({
                    "mm": lambda t, bk, mt=mt: _last(*[
                        t.matmul(PSb(bk, 0, TC), Wt(wr, kt, mt),
                                 xr[:, kt * TC:(kt + 1) * TC],
                                 start=(kt == 0), stop=(kt == NPT - 1))
                        for kt in range(NPT)]),
                    "evac": lambda e, bk: e.activation(
                        srt[:], PSb(bk, 0, TC), AF.Sigmoid),
                    "evac_eng": "scalar",
                    "store": lambda sy, mt=mt, tc=tc: sy.dma_start(
                        out=d_sr[mt * 128:(mt + 1) * 128, tc * TC:(tc + 1) * TC],
                        in_=srt[:]),
                    "st_class": "r",
                    "buf_lag": 3})
            pipelined_mm(h, "pC1", items, nbank=4, bank0=0)

    with contextlib.ExitStack() as s2:
        kf = s2.enter_context(nc.sbuf_tensor("kfC", [128, 2 * TP], F32))
        vf = s2.enter_context(nc.sbuf_tensor("vfC", [128, 2 * TP], F32))
        af = s2.enter_context(nc.sbuf_tensor("afC", [128, 2 * TP], F32))
        avf = s2.enter_context(nc.sbuf_tensor("avC", [128, TP], F32))
        Pf = s2.enter_context(nc.sbuf_tensor("PfC", [128, TP + 1], F32))
        Qf = s2.enter_context(nc.sbuf_tensor("QfC", [128, TP + 1], F32))
        dtl = s2.enter_context(nc.sbuf_tensor("dtlC", [128, TP], F32))
        srf = s2.enter_context(nc.sbuf_tensor("srfC", [128, 2 * TP], BF16))
        rwt = s2.enter_context(nc.sbuf_tensor("rwtC", [128, 2 * TP], BF16))
        SL = lambda buf, pt: buf[:, (pt % 2) * TP:(pt % 2 + 1) * TP]
        cLD = [p.sem("c2L0"), p.sem("c2L1")]
        cEX, cSC = p.sem("c2X"), p.sem("c2S")
        cST = [p.sem("c2T0"), p.sem("c2T1")]
        C.flush()
        tl2 = C.tail()

        @blk.sync
        def _(sy):
            sy.wait_ge(p.sem(tl2[0]), tl2[1])
            for pt in range(NPT):
                if pt >= 2:
                    sy.wait_ge(cSC, pt - 1)
                sy.dma_start(out=SL(kf, pt),
                             in_=d_kf[pt * 128:(pt + 1) * 128, :]).then_inc(cLD[pt % 2], 16)
                sy.dma_start(out=SL(vf, pt),
                             in_=d_vf[pt * 128:(pt + 1) * 128, :]).then_inc(cLD[pt % 2], 16)
                sy.dma_start(out=SL(srf, pt),
                             in_=d_sr[pt * 128:(pt + 1) * 128, :]).then_inc(cLD[pt % 2], 16)
                if pt >= 1:
                    sy.wait_ge(cSC, pt)
                    sy.dma_start(out=d_rw[(pt - 1) * 128:pt * 128, :],
                                 in_=SL(rwt, pt - 1)).then_inc(cST[(pt - 1) % 2], 16)
            sy.wait_ge(cSC, NPT)
            sy.dma_start(out=d_rw[(NPT - 1) * 128:NPT * 128, :],
                         in_=SL(rwt, NPT - 1)).then_inc(cST[(NPT - 1) % 2], 16)

        @blk.scalar
        def _(s):
            s.wait_ge(p.sem(tl2[0]), tl2[1])
            for pt in range(NPT):
                s.wait_ge(cLD[pt % 2], (pt // 2 + 1) * 48)
                if pt >= 2:
                    s.wait_ge(cSC, pt - 1)
                s.activation(SL(af, pt), SL(kf, pt), AF.Exp).then_inc(cEX, 1)

        @blk.vector
        def _(v):
            v.wait_ge(p.sem(tl2[0]), tl2[1])
            for pt in range(NPT):
                i0 = v.memset(dtl[:], 1.0)
                p.vg(v, i0)
                i1 = v.tensor_scalar_mul(dtl[:], dtl[:], DCOL[:, pt:pt + 1])
                v.memset(Pf[:, 0:1], 0.0)
                v.memset(Qf[:, 0:1], 0.0)
                v.wait_ge(cEX, pt + 1)
                if pt >= 2:
                    v.wait_ge(cST[pt % 2], (pt // 2) * 16)
                afp, kfp, vfp = SL(af, pt), SL(kf, pt), SL(vf, pt)
                i0 = v.tensor_tensor(afp, afp, maskb[:], ALU.mult)
                p.vg(v, i0)
                i1 = v.tensor_tensor(avf[:], afp, vfp, ALU.mult)
                p.vg(v, i1)
                i2 = v.tensor_tensor_scan(Pf[:, 1:TP + 1], dtl[:], avf[:], 0.0,
                                          ALU.mult, ALU.add)
                p.vg(v, i2)
                i3 = v.tensor_tensor_scan(Qf[:, 1:TP + 1], dtl[:], afp, 0.0,
                                          ALU.mult, ALU.add)
                p.vg(v, i3)
                i4 = v.scalar_tensor_tensor(avf[:], avf[:], vecsC[:, 8 + pt:9 + pt],
                                            Pf[:, 0:TP], ALU.mult, ALU.add)
                p.vg(v, i4)
                i5 = v.scalar_tensor_tensor(afp, afp, vecsC[:, 8 + pt:9 + pt],
                                            Qf[:, 0:TP], ALU.mult, ALU.add)
                p.vg(v, i5)
                i6 = v.tensor_scalar_max(afp, afp, 1e-30)
                p.vg(v, i6)
                i7 = v.reciprocal(kfp, afp)
                p.vg(v, i7)
                i8 = v.tensor_tensor(avf[:], avf[:], kfp, ALU.mult)
                p.vg(v, i8)
                v.tensor_tensor(SL(rwt, pt), avf[:], SL(srf, pt),
                                ALU.mult).then_inc(cSC, 1)

        def c2_join(v):
            v.wait_ge(cST[0], (NPT // 2) * 16)
            v.wait_ge(cST[1], (NPT // 2) * 16)
            return v.tensor_copy(h["idf"][0:1, 0:1], h["idf"][0:1, 0:1])
        C.step("vector", c2_join)

    with contextlib.ExitStack() as s3:
        wo = s3.enter_context(nc.sbuf_tensor("woC", [128, NPT * D], BF16))
        rwc = s3.enter_context(nc.sbuf_tensor("rwcC", [128, 2 * NPT * TC], BF16))
        tmp6 = s3.enter_context(nc.sbuf_tensor("tmp6C", [128, 2 * TC], F32))
        C.step("gpsimd", lambda g: g.dma_start(
            out=wo[:].rearrange("p (k n) -> p k n", n=D),
            in_=dram["Wo_T"].rearrange("(k p) n -> p k n", p=128)), dma=True)
        cM, cE = p.sem("c3M"), p.sem("c3E")
        cA = [p.sem("c3A0"), p.sem("c3A1")]
        cL = [p.sem("c3L0"), p.sem("c3L1")]
        C.flush()
        tl = C.tail()

        @blk.sync
        def _(sy):
            sy.wait_ge(p.sem(tl[0]), tl[1])
            for tc in range(NTC):
                if tc >= 2:
                    sy.wait_ge(cM, (tc - 1) * NPT)
                sl = tc % 2
                sy.dma_start(
                    out=rwc[:, sl * NPT * TC:(sl + 1) * NPT * TC]
                        .rearrange("p (k n) -> p k n", n=TC),
                    in_=d_rw[:, tc * TC:(tc + 1) * TC]
                        .rearrange("(k p) n -> p k n", p=128)
                ).then_inc(cL[sl], 16)

        @blk.tensor
        def _(t):
            t.wait_ge(p.sem(tl[0]), tl[1])
            for tc in range(NTC):
                t.wait_ge(cL[tc % 2], (tc // 2 + 1) * 16)
                for mt in range(NPT):
                    i = tc * NPT + mt
                    if i >= 4:
                        t.wait_ge(cE, i - 3)
                    _last(*[t.matmul(
                        PSb(i % 4, 0, TC), Wt(wo, kt, mt),
                        rwc[:, (tc % 2) * NPT * TC + kt * TC:
                               (tc % 2) * NPT * TC + (kt + 1) * TC],
                        start=(kt == 0), stop=(kt == NPT - 1))
                        for kt in range(NPT)]).then_inc(cM, 1)

        @blk.vector
        def _(v):
            v.wait_ge(p.sem(tl[0]), tl[1])
            for i in range(NTC * NPT):
                v.wait_ge(cM, i + 1)
                if i >= 2:
                    v.wait_ge(cA[i % 2], (i // 2) * 16)
                v.tensor_copy(tmp6[:, (i % 2) * TC:(i % 2 + 1) * TC],
                              PSb(i % 4, 0, TC)).then_inc(cE, 1)

        @blk.gpsimd
        def _(g):
            g.wait_ge(p.sem(tl[0]), tl[1])
            for tc in range(NTC):
                for mt in range(NPT):
                    i = tc * NPT + mt
                    g.wait_ge(cE, i + 1)
                    g.dma_start(
                        out=d_xres[:, mt * TP + tc * TC: mt * TP + (tc + 1) * TC],
                        in_=tmp6[:, (i % 2) * TC:(i % 2 + 1) * TC],
                        accum_op=ALU.add).then_inc(cA[i % 2], 16)

        def c3_join(v):
            v.wait_ge(cA[0], (NTC * NPT // 2) * 16)
            v.wait_ge(cA[1], (NTC * NPT // 2) * 16)
            return v.tensor_copy(h["idf"][0:1, 0:1], h["idf"][0:1, 0:1])
        C.step("vector", c3_join)


def emit_stage_d(h):
    nc, p, blk, C, dram = h["nc"], h["p"], h["blk"], h["C"], h["dram"]
    PSb, sbuf = h["PSb"], h["sbuf"]
    d_xres = dram["xres"]
    Wt = lambda w, kt, mt: w[:, kt * D + mt * 128: kt * D + (mt + 1) * 128]

    if "maskb" not in h:
        h["maskb"] = sbuf("maskb", [128, TP], F32)
        C.step("sync", lambda sy: sy.dma_start(out=h["maskb"][:], in_=dram["mask"]),
               dma=True)
    vecsD = sbuf("vecsD", [128, 16], F32)
    for i, nm in enumerate(("f_mur", "f_muk")):
        C.step("sync", lambda sy, i=i, nm=nm: sy.dma_start(
            out=vecsD[:, i * 8:(i + 1) * 8],
            in_=dram[nm].rearrange("(t p) -> p t", p=128),
            allow_slow_non_contiguous=True), dma=True)
    MUR, MUK = vecsD[:, 0:8], vecsD[:, 8:16]

    ln3 = sbuf("ln3_sb", [128, NPT * (TP + 1)], BF16)
    _emit_ln_masked(h, C, d_xres, ln3, "d")

    with contextlib.ExitStack() as s1:
        wrD = s1.enter_context(nc.sbuf_tensor("wrD", [128, NPT * D], BF16))
        wkD = s1.enter_context(nc.sbuf_tensor("wkD", [128, NPT * D], BF16))
        wvD = s1.enter_context(nc.sbuf_tensor("wvD", [128, NPT * D], BF16))
        xrD = s1.enter_context(nc.sbuf_tensor("xrD", [128, NPT * TC], BF16))
        xkD = s1.enter_context(nc.sbuf_tensor("xkD", [128, NPT * TC], BF16))
        dtmp = s1.enter_context(nc.sbuf_tensor("dtmpD", [128, TC], F32))
        srD = s1.enter_context(nc.sbuf_tensor("srD", [128, NPT * TC], BF16))
        kaD = s1.enter_context(nc.sbuf_tensor("kaD", [128, NPT * TC], BF16))
        tmpD = s1.enter_context(nc.sbuf_tensor("tmpD", [128, 2 * TC], F32))
        accD = s1.enter_context(nc.sbuf_tensor("accD", [128, TC], F32))
        for dst, nm in ((wrD, "f_Wr_T"), (wkD, "f_Wk_T"), (wvD, "f_Wv_T")):
            C.step("gpsimd", lambda g, dst=dst, nm=nm: g.dma_start(
                out=dst[:].rearrange("p (k n) -> p k n", n=D),
                in_=dram[nm].rearrange("(k p) n -> p k n", p=128)), dma=True)
        npmm = h.setdefault("_pmm", {}).setdefault(
            "pD1", {"M": 0, "Evector": 0, "Escalar": 0, "Egpsimd": 0,
                    "S0": 0, "S1": 0})
        for tc in range(NTC):
            _emit_shift_mix(h, C, ln3, [MUR, MUK], [xrD, xkD], dtmp, tc)
            base_v, base_s = npmm["Evector"], npmm["Escalar"]
            items = []
            for mt in range(NPT):
                items.append({
                    "mm": lambda t, bk, mt=mt: _last(*[
                        t.matmul(PSb(bk, 0, TC), Wt(wrD, kt, mt),
                                 xrD[:, kt * TC:(kt + 1) * TC],
                                 start=(kt == 0), stop=(kt == NPT - 1))
                        for kt in range(NPT)]),
                    "evac": lambda e, bk, mt=mt: e.activation(
                        srD[:, mt * TC:(mt + 1) * TC], PSb(bk, 0, TC), AF.Sigmoid),
                    "evac_eng": "scalar"})
                def v_ka(v, bk, mt=mt):
                    sl = mt % 2
                    i0 = v.tensor_scalar_max(tmpD[:, sl * TC:(sl + 1) * TC],
                                             PSb(bk, 0, TC), 0.0)
                    p.vg(v, i0)
                    return v.tensor_tensor(kaD[:, mt * TC:(mt + 1) * TC],
                                           PSb(bk, 0, TC),
                                           tmpD[:, sl * TC:(sl + 1) * TC], ALU.mult)
                items.append({
                    "mm": lambda t, bk, mt=mt: _last(*[
                        t.matmul(PSb(bk, 0, TC), Wt(wkD, kt, mt),
                                 xkD[:, kt * TC:(kt + 1) * TC],
                                 start=(kt == 0), stop=(kt == NPT - 1))
                        for kt in range(NPT)]),
                    "evac": v_ka})
            for mt in range(NPT):
                i_glob = 2 * NPT + mt
                items.append({
                    "mm": lambda t, bk, mt=mt: _last(*[
                        t.matmul(PSb(bk, 0, TC), Wt(wvD, kt, mt),
                                 kaD[:, kt * TC:(kt + 1) * TC],
                                 start=(kt == 0), stop=(kt == NPT - 1))
                        for kt in range(NPT)]),
                    "t_waits": ((("pD1Ev", base_v + NPT),) if mt == 0 else ()),
                    "v_waits": ((("pD1Es", base_s + NPT),) if mt == 0 else ()),
                    "evac": lambda e, bk, mt=mt: e.tensor_tensor(
                        accD[:], PSb(bk, 0, TC),
                        srD[:, mt * TC:(mt + 1) * TC], ALU.mult),
                    "store": lambda g, mt=mt, tc=tc: g.dma_start(
                        out=d_xres[:, mt * TP + tc * TC: mt * TP + (tc + 1) * TC],
                        in_=accD[:], accum_op=ALU.add),
                    "store_eng": "gpsimd",
                    "st_class": 0,
                    "buf_lag": 1})
            pipelined_mm(h, "pD1", items, nbank=4, bank0=0)


def emit_output(h):
    nc, p, blk, C, dram = h["nc"], h["p"], h["blk"], h["C"], h["dram"]
    PSb, idf = h["PSb"], h["idf"]
    p = h["p"]
    blk = h["blk"]
    NOT = CH // 128
    with contextlib.ExitStack() as s0:
        xT = s0.enter_context(nc.sbuf_tensor("xT_out", [128, NPT * TP], F32))
        yst = s0.enter_context(nc.sbuf_tensor("yst", [128, 2 * D], F32))
        C.step("sync", lambda sy: sy.dma_start(out=xT[:], in_=dram["xres"]), dma=True)
        oT, oE = p.sem("c3M"), p.sem("c3E")
        oS = [p.sem("c3L0"), p.sem("c3L1")]
        bT = bE = NTC * NPT
        bSl = [((NTC + 1) // 2) * 16, (NTC // 2) * 16]
        C.flush()
        tl = C.tail()

        @blk.tensor
        def _(t):
            t.wait_ge(p.sem(tl[0]), tl[1])
            for tt in range(NOT):
                if tt >= 2:
                    t.wait_ge(oE, bE + tt - 1)
                _last(*[t.transpose(
                    PSb(2 * (tt % 2) + pt // 4, (pt % 4) * 128, 128),
                    xT[:, pt * TP + W + tt * 128: pt * TP + W + (tt + 1) * 128],
                    idf[:]) for pt in range(NPT)]).then_inc(oT, 1)

        @blk.vector
        def _(v):
            v.wait_ge(p.sem(tl[0]), tl[1])
            for tt in range(NOT):
                v.wait_ge(oT, bT + tt + 1)
                if tt >= 2:
                    v.wait_ge(oS[tt % 2], bSl[tt % 2] + ((tt - 2) // 2 + 1) * 16)
                _last(*[v.tensor_copy(
                    yst[:, (tt % 2) * D + pt * 128:(tt % 2) * D + (pt + 1) * 128],
                    PSb(2 * (tt % 2) + pt // 4, (pt % 4) * 128, 128))
                    for pt in range(NPT)]).then_inc(oE, 1)

        @blk.sync
        def _(sy):
            sy.wait_ge(p.sem(tl[0]), tl[1])
            for tt in range(NOT):
                sy.wait_ge(oE, bE + tt + 1)
                sy.dma_start(out=dram["y"][tt * 128:(tt + 1) * 128, :],
                             in_=yst[:, (tt % 2) * D:(tt % 2 + 1) * D]
                             ).then_inc(oS[tt % 2], 16)

        def ojoin(v):
            v.wait_ge(oS[0], bSl[0] + ((NOT + 1) // 2) * 16)
            v.wait_ge(oS[1], bSl[1] + (NOT // 2) * 16)
            return v.tensor_copy(h["idf"][0:1, 0:1], h["idf"][0:1, 0:1])
        C.step("vector", ojoin)
        C.flush()


def emit_attention(h, d_kT, d_vh, d_qT, d_oT):
    nc, p, blk, C = h["nc"], h["p"], h["blk"], h["C"]
    PSb, PSf, onesrf = h["PSb"], h["PSf"], h["onesrf"]
    QCS = []
    off = 0
    while off < TP:
        w = min(512, TP - off)
        QCS.append((off, w))
        off += w
    NQC = len(QCS)
    with contextlib.ExitStack() as ph:
        ksb = ph.enter_context(nc.sbuf_tensor("ksb", [128, 2 * S], BF16))
        vsb = ph.enter_context(nc.sbuf_tensor("vsb", [128, 2 * NKT * 130], BF16))
        qhp = ph.enter_context(nc.sbuf_tensor("qhp", [128, 2 * TP], BF16))
        ohp = ph.enter_context(nc.sbuf_tensor("ohp", [128, 2 * TP], BF16))
        ohpB = ph.enter_context(nc.sbuf_tensor("ohpB", [64, 2 * TP], BF16))
        pb = ph.enter_context(nc.sbuf_tensor("pb", [128, 3 * 2 * 512], BF16))
        recr = ph.enter_context(nc.sbuf_tensor("recr", [1, 2 * 512], F32))
        brs = ph.enter_context(nc.sbuf_tensor("brs", [128, 2 * 512], F32))
        sxp, spv, sqn, srec, sbrr = (p.sem(x) for x in
                                     ("sxp", "spv", "sqn", "srec", "sbrr"))
        sost = [p.sem("sost0"), p.sem("sost1")]
        sqk = p.sem("sqk")
        skl = [p.sem("skl0"), p.sem("skl1")]
        C.flush()
        tl = C.tail()
        KSB = lambda hp: ksb[:, (hp % 2) * S:(hp % 2 + 1) * S]
        VSB = lambda hp: vsb[:, (hp % 2) * NKT * 130:(hp % 2 + 1) * NKT * 130]
        QHP = lambda hp: qhp[:, (hp % 2) * TP:(hp % 2 + 1) * TP]
        OHP = lambda hp: ohp[:, (hp % 2) * TP:(hp % 2 + 1) * TP]
        OHPB = lambda hp: ohpB[:, (hp % 2) * TP:(hp % 2 + 1) * TP]

        @blk.sync
        def _(sy):
            if tl:
                sy.wait_ge(p.sem(tl[0]), tl[1])
            for hp in range(NHP):
                if hp >= 2:
                    sy.wait_ge(spv, (hp - 1) * NQC * NKT)
                sy.dma_start(out=KSB(hp), in_=d_kT[hp * 128:(hp + 1) * 128, :]
                             ).then_inc(skl[hp % 2], 16)
                sy.dma_start(out=VSB(hp).rearrange("p (t c) -> p t c", c=130),
                             in_=d_vh[hp].rearrange("(t p) c -> p t c", p=128)
                             ).then_inc(skl[hp % 2], 16)
                sy.dma_start(out=QHP(hp), in_=d_qT[hp * 128:(hp + 1) * 128, :]
                             ).then_inc(skl[hp % 2], 16)
                if hp >= 1:
                    sy.wait_ge(sqn, hp * NQC)
                    sy.dma_start(out=d_oT[(hp - 1) * 128:(hp - 1) * 128 + 64, :],
                                 in_=OHP(hp - 1)[0:64, :]).then_inc(sost[(hp - 1) % 2], 16)
                    sy.dma_start(out=d_oT[(hp - 1) * 128 + 64:hp * 128, :],
                                 in_=OHPB(hp - 1)).then_inc(sost[(hp - 1) % 2], 16)
            sy.wait_ge(sqn, NHP * NQC)
            sy.dma_start(out=d_oT[(NHP - 1) * 128:(NHP - 1) * 128 + 64, :],
                         in_=OHP(NHP - 1)[0:64, :]).then_inc(sost[(NHP - 1) % 2], 16)
            sy.dma_start(out=d_oT[(NHP - 1) * 128 + 64:NHP * 128, :],
                         in_=OHPB(NHP - 1)).then_inc(sost[(NHP - 1) % 2], 16)

        @blk.tensor
        def _(t):
            if tl:
                t.wait_ge(p.sem(tl[0]), tl[1])
            nqk = 0
            for hp in range(NHP):
                t.wait_ge(skl[hp % 2], (hp // 2 + 1) * 48)
                for qc in range(NQC):
                    it = hp * NQC + qc
                    qo, qw = QCS[qc]

                    def emit_pv(j, it=it, hp=hp, qw=qw):
                        slot = j % 3
                        t.wait_ge(sxp, it * NKT + j + 1)
                        if j == 0 and it > 0:
                            t.wait_ge(sqn, it)
                        t.matmul(PSb(6, 0, qw)[0:65, :],
                                 VSB(hp)[:, j * 130: j * 130 + 65],
                                 pb[:, (slot * 2) * 512:(slot * 2) * 512 + qw],
                                 start=(j == 0), stop=(j == NKT - 1),
                                 skip_group_check=True)
                        t.matmul(PSb(7, 0, qw)[0:65, :],
                                 VSB(hp)[:, j * 130 + 65: j * 130 + 130],
                                 pb[:, (slot * 2 + 1) * 512:(slot * 2 + 1) * 512 + qw],
                                 start=(j == 0), stop=(j == NKT - 1),
                                 skip_group_check=True).then_inc(spv, 1)

                    for kt in range(NKT):
                        slot = kt % 3
                        if nqk >= 3:
                            t.wait_ge(sxp, nqk - 2)
                        if kt == 0 and it > 0:
                            t.wait_ge(sqn, it)
                        t.matmul(PSb(2 * slot, 0, qw),
                                 KSB(hp)[0:64, kt * 128:(kt + 1) * 128],
                                 QHP(hp)[0:64, qo: qo + qw],
                                 start=True, stop=True, skip_group_check=True)
                        t.matmul(PSb(2 * slot + 1, 0, qw),
                                 KSB(hp)[64:128, kt * 128:(kt + 1) * 128],
                                 QHP(hp)[64:128, qo: qo + qw],
                                 start=True, stop=True, tile_position=(64, 0),
                                 skip_group_check=True).then_inc(sqk, 1)
                        nqk += 1
                        if kt >= 1:
                            emit_pv(kt - 1)
                    emit_pv(NKT - 1)
                    t.wait_ge(srec, it + 1)
                    t.matmul(PSb(0, 0, qw), onesrf[:], recr[0:1, 0:qw],
                             start=True, stop=True, skip_group_check=True)
                    t.matmul(PSb(1, 0, qw), onesrf[:], recr[0:1, 512:512 + qw],
                             start=True, stop=True,
                             skip_group_check=True).then_inc(sbrr, 1)

        @blk.scalar
        def _(s):
            nxp = 0
            for hp in range(NHP):
                for qc in range(NQC):
                    qo, qw = QCS[qc]
                    for kt in range(NKT):
                        slot = kt % 3
                        s.wait_ge(sqk, nxp + 1)
                        if nxp >= 3:
                            s.wait_ge(spv, nxp - 2)
                        s.activation(
                            pb[:, slot * 2 * 512:].rearrange(
                                "p (b n) -> p b n", n=512)[:, 0:2, 0:qw],
                            PSf[:, 2 * slot * BANK:(2 * slot + 2) * BANK].rearrange(
                                "p (b n) -> p b n", b=2)[:, :, 0:qw],
                            AF.Exp, scale=0.125).then_inc(sxp, 1)
                        nxp += 1

        @blk.vector
        def _(v):
            for hp in range(NHP):
                for qc in range(NQC):
                    it = hp * NQC + qc
                    qo, qw = QCS[qc]
                    if qc == 0 and hp >= 2:
                        v.wait_ge(sost[hp % 2], ((hp - 2) // 2 + 1) * 32)
                    v.wait_ge(spv, (it + 1) * NKT)
                    v.reciprocal(recr[0:1, 0:qw], PSb(6, 0, qw)[64:65, :])
                    v.reciprocal(recr[0:1, 512:512 + qw],
                                 PSb(7, 0, qw)[64:65, :]).then_inc(srec, 1)
                    v.wait_ge(sbrr, it + 1)
                    ia = v.tensor_copy(brs[0:64, 0:qw], PSb(0, 0, qw)[0:64, :])
                    ib = v.tensor_copy(brs[0:64, 512:512 + qw], PSb(1, 0, qw)[0:64, :])
                    p.vg(v, ia)
                    p.vg(v, ib)
                    v.tensor_tensor(OHP(hp)[0:64, qo: qo + qw],
                                    PSb(6, 0, qw)[0:64, :], brs[0:64, 0:qw], ALU.mult)
                    v.tensor_tensor(OHPB(hp)[0:64, qo: qo + qw],
                                    PSb(7, 0, qw)[0:64, :], brs[0:64, 512:512 + qw],
                                    ALU.mult).then_inc(sqn, 1)

        C.step("vector", lambda v: _last(v.wait_ge(sost[0], (NHP // 2) * 32),
                                         v.wait_ge(sost[1], (NHP // 2) * 32),
                                         v.tensor_copy(brs[0:1, 0:1], brs[0:1, 0:1])))
        C.flush()



def _prep_core_inputs(c, full):
    b, j = c // 4, c % 4
    x = full["L0_attn_x"][b]
    lo = j * CH - W
    if lo < 0:
        xs = np.concatenate([np.zeros((W, D), np.float32), np.asarray(x[0:CH])], 0)
        mask = np.concatenate([np.zeros((128, W), np.float32),
                               np.ones((128, CH), np.float32)], 1)
    else:
        xs = np.ascontiguousarray(x[lo: (j + 1) * CH])
        mask = np.ones((128, TP), np.float32)
    T = lambda a: np.ascontiguousarray(np.asarray(a).T)
    A = lambda a: np.ascontiguousarray(np.asarray(a))
    return {
        "xs": np.ascontiguousarray(xs),
        "xb": A(x),
        "mask": mask,
        "ident": np.eye(128, dtype=np.float32),
        "w_q": A(full["L0_attn_w_q"]),
        "w_k": A(full["L0_attn_w_k"]),
        "w_v": A(full["L0_attn_w_v"]),
        "w_proj": A(full["L0_attn_w_proj"]),
        "b_proj": A(full["L0_attn_b_proj"]),
        "ln_g": A(full["L0_attn_ln_gamma"]),
        "ln_b": A(full["L0_attn_ln_beta"]),
        "sh_gate_T": T(full["L0_ffn_shared_gate"]),
        "sh_up_T": T(full["L0_ffn_shared_up"]),
        "sh_down_T": T(full["L0_ffn_shared_down"]),
        "gate_w": A(full["L0_ffn_gate_w"]),
        "up_w": A(full["L0_ffn_up_w"]),
        "down_w": A(full["L0_ffn_down_w"]),
        "router_T": T(full["L0_ffn_router_w"]),
        "r_bias": A(full["L0_ffn_routing_bias"]),
        "l1_w": A(full["L1_attn_w"]),
        "l1_u": A(full["L1_attn_u"]),
        "l1_mk": A(full["L1_attn_mix_k"]),
        "l1_mv": A(full["L1_attn_mix_v"]),
        "l1_mr": A(full["L1_attn_mix_r"]),
        "Wk_T": T(full["L1_attn_Wk"]),
        "Wv_T": T(full["L1_attn_Wv"]),
        "Wr_T": T(full["L1_attn_Wr"]),
        "Wo_T": T(full["L1_attn_Wo"]),
        "f_Wr_T": T(full["L1_ffn_W_r"]),
        "f_Wk_T": T(full["L1_ffn_W_k"]),
        "f_Wv_T": T(full["L1_ffn_W_v"]),
        "f_mur": A(full["L1_ffn_mu_r"]),
        "f_muk": A(full["L1_ffn_mu_k"]),
    }


ALL_STAGES = ("IN", "A", "B", "C", "D", "OUT")


def kernel(**inputs):
    nc, out_names = build_program(stages=ALL_STAGES)
    in_maps = [_prep_core_inputs(c, inputs) for c in range(8)]
    res = run_bass_kernel_spmd(nc, in_maps, core_ids=list(range(8)))
    out = np.zeros((B, S, D), np.float32)
    for c in range(8):
        b, j = c // 4, c % 4
        out[b, j * CH:(j + 1) * CH, :] = res.results[c]["y"]
    return out


# version 21
# speedup vs baseline: 1.0025x; 1.0025x over previous
import contextlib
import numpy as np
import ml_dtypes
import concourse.bass as bass
import concourse.mybir as mybir
from concourse.bass_utils import run_bass_kernel_spmd
from concourse.bass import IndirectOffsetOnAxis

F32 = mybir.dt.float32
BF16 = mybir.dt.bfloat16
AF = mybir.ActivationFunctionType
ALU = mybir.AluOpType

B, S, D = 2, 8192, 1024
W = 256
CH = 2048
TP = CH + W
TC = 384
NTC = TP // TC
NPT = D // 128
NTT = TP // 128
BCH = 512
NBC = S // BCH
NHP = 8
NKT = S // 128
HD = 64
I_FF = 2048
NPI = I_FF // 128
NE = 8
EPS_OUT = 1e-5
EPS_IN = 1e-6
BANK = 512

SPARSE_MOE = True
DBG_SKIP = set()
CAP = -(-(TP // 2) // TC) * TC
NCE = CAP // TC
NCC = CAP // 128
TCK = TC // 128
DUMP = NE * CAP
BIGF = 4.0e6


class Prog:
    def __init__(self, nc, block, es):
        self.nc, self.block, self.es = nc, block, es
        self._sems = {}

    def sem(self, name):
        if name not in self._sems:
            self._sems[name] = self.es.enter_context(self.nc.semaphore(name))
        return self._sems[name]

    def vg(self, v, inst):
        self._vch = getattr(self, "_vch", 0) + 1
        inst.then_inc(self.sem("vch"), 1)
        v.wait_ge(self.sem("vch"), self._vch)


class Chain:

    def __init__(self, p, tag):
        self.p = p
        self.tag = tag
        self.cums = {}
        self.steps = []
        self.last = None

    def _bump(self, name, amt):
        v = self.cums.get(name, 0) + amt
        self.cums[name] = v
        return v

    def step(self, eng, fn, dma=False):
        semname = (self.tag + "_D" + eng) if dma else self.tag
        amt = 16 if dma else 1
        v = self._bump(semname, amt)
        self.steps.append((eng, fn, semname, amt, self.last, dma))
        self.last = (semname, v)

    def tail(self):
        return self.last

    def flush(self):
        per_eng = {}
        for (eng, fn, semname, amt, wait, dma) in self.steps:
            per_eng.setdefault(eng, []).append((wait, fn, semname, amt))
        for eng_name, items in per_eng.items():
            def section(e, items=items):
                for wait, fn, semname, amt in items:
                    if wait is not None:
                        e.wait_ge(self.p.sem(wait[0]), wait[1])
                    fn(e).then_inc(self.p.sem(semname), amt)
            getattr(self.p.block, eng_name)(section)
        self.steps = []


def _last(*insts):
    return insts[-1]


def pipelined_mm(h, tag, items, nbank=4, bank0=0):
    nc, p, blk, C = h["nc"], h["p"], h["blk"], h["C"]
    reg = h.setdefault("_pmm", {}).setdefault(
        tag, {"M": 0, "Evector": 0, "Escalar": 0, "Egpsimd": 0, "S0": 0, "S1": 0})
    bM = reg["M"]
    bE = {en: reg["E" + en] for en in ("vector", "scalar", "gpsimd")}
    sMM = p.sem(tag + "M")
    engs_used = {it.get("evac_eng", "vector") for it in items}
    sEV = {en: p.sem(tag + "E" + en[0]) for en in engs_used}
    C.flush()
    tl = C.tail()
    n = len(items)
    has_store = any(it.get("store") for it in items)
    st_count = {}
    st_at = []
    ev_cnt = {"vector": 0, "scalar": 0, "gpsimd": 0}
    ev_at = []
    for i, it in enumerate(items):
        cl = it.get("st_class", i % 2)
        if it.get("store"):
            st_count[cl] = st_count.get(cl, 0) + 1
            st_at.append((cl, st_count[cl]))
        else:
            st_at.append((cl, 0))
        en = it.get("evac_eng", "vector")
        ev_cnt[en] += 1
        ev_at.append((en, ev_cnt[en]))
    bS = {cl: reg.setdefault("S%r" % (cl,), 0) for cl in st_count}
    sST = {cl: p.sem(tag + "S%r" % (cl,)) for cl in st_count}

    @blk.tensor
    def _(t):
        if tl:
            t.wait_ge(p.sem(tl[0]), tl[1])
        for i, it in enumerate(items):
            if i >= nbank:
                en, c = ev_at[i - nbank]
                t.wait_ge(sEV[en], bE[en] + c)
            for w in it.get("t_waits", ()):
                t.wait_ge(p.sem(w[0]), w[1])
            it["mm"](t, bank0 + i % nbank).then_inc(sMM, 1)

    def evac_section(engname):
        def sect(e):
            if tl:
                e.wait_ge(p.sem(tl[0]), tl[1])
            for i, it in enumerate(items):
                if it.get("evac_eng", "vector") != engname:
                    continue
                e.wait_ge(sMM, bM + i + 1)
                bl = it.get("buf_lag")
                if bl is not None and i - bl >= 0:
                    jcl, jc = st_at[i - bl]
                    if jc > 0:
                        e.wait_ge(sST[jcl], (bS[jcl] + jc) * 16)
                for w in it.get("v_waits", ()):
                    e.wait_ge(p.sem(w[0]), w[1])
                it["evac"](e, bank0 + i % nbank).then_inc(sEV[engname], 1)
        return sect
    engs = {it.get("evac_eng", "vector") for it in items}
    for en in engs:
        getattr(blk, en)(evac_section(en))

    if has_store:
        st_engs = {it.get("store_eng", "sync") for it in items if it.get("store")}
        def store_section(engname):
            def sect(sy):
                if tl:
                    sy.wait_ge(p.sem(tl[0]), tl[1])
                for i, it in enumerate(items):
                    if not it.get("store") or it.get("store_eng", "sync") != engname:
                        continue
                    en, c = ev_at[i]
                    sy.wait_ge(sEV[en], bE[en] + c)
                    it["store"](sy).then_inc(sST[st_at[i][0]], 16)
            return sect
        for en in st_engs:
            getattr(blk, en)(store_section(en))

    reg["M"] += n
    for en in ("vector", "scalar", "gpsimd"):
        reg["E" + en] += ev_cnt[en]
    for cl, c in st_count.items():
        reg["S%r" % (cl,)] += c
    fin = {en: reg["E" + en] for en in engs}
    fin_s = {cl: reg["S%r" % (cl,)] for cl in st_count}

    def join(v):
        for en, c in fin.items():
            v.wait_ge(sEV[en], c)
        for cl, c in fin_s.items():
            v.wait_ge(sST[cl], c * 16)
        return v.tensor_copy(h["idf"][0:1, 0:1], h["idf"][0:1, 0:1])
    C.step("vector", join)


def build_program(stages=("A",), dbg_out=None):
    nc = bass.Bass("TRN2", target_bir_lowering=False)
    es = contextlib.ExitStack()

    dram = {}
    def din(name, shape, dt=F32):
        dram[name] = nc.dram_tensor(name, shape, dt, kind="ExternalInput").ap()
        return dram[name]
    def dint(name, shape, dt):
        dram[name] = nc.dram_tensor(name, shape, dt, kind="Internal").ap()
        return dram[name]

    din("ident", [128, 128])
    dint("xres", [128, NPT * TP], F32)
    if "IN" in stages:
        din("xs", [TP, D])
    else:
        din("xres_in", [128, NPT * TP], F32)
    if "C" in stages or "D" in stages:
        din("mask", [128, TP])
    if "C" in stages:
        din("l1_w", [D]); din("l1_u", [D]); din("l1_mk", [D]); din("l1_mv", [D]); din("l1_mr", [D])
        din("Wk_T", [D, D]); din("Wv_T", [D, D]); din("Wr_T", [D, D]); din("Wo_T", [D, D])
        dint("kf_scr", [D, TP], F32)
        dint("vf_scr", [D, TP], F32)
        dint("sr_scr", [D, TP], BF16)
        dint("rw_scr", [D, TP], BF16)
    if "D" in stages:
        din("f_Wr_T", [D, D]); din("f_Wk_T", [D, D]); din("f_Wv_T", [D, D])
        din("f_mur", [D]); din("f_muk", [D])
    if "B" in stages:
        din("sh_gate_T", [D, I_FF]); din("sh_up_T", [D, I_FF]); din("sh_down_T", [I_FF, D])
        din("gate_w", [NE, D, I_FF]); din("up_w", [NE, D, I_FF]); din("down_w", [NE, I_FF, D])
        din("router_T", [D, NE]); din("r_bias", [NE])
        dint("ln1_scr", [D, TP], BF16)
        if SPARSE_MOE:
            dint("ln1_tm", [TP, D], BF16)
            dint("xg_e", [NE * CAP + 1, D], BF16)
            dint("eout_e", [NE * CAP + 1, D], F32)
    if "A" in stages:
        din("xb", [S, D])
        din("w_q", [D, D]); din("w_k", [D, D]); din("w_v", [D, D]); din("w_proj", [D, D])
        din("b_proj", [D]); din("ln_g", [D]); din("ln_b", [D])
        dint("kT_scr", [D, S], BF16)
        dint("v_scr", [NHP, S, 130], BF16)
        dint("qT_scr", [D, TP], BF16)
        dint("oT_scr", [D, TP], BF16)

    out_names = []
    if "OUT" in stages:
        dram["y"] = nc.dram_tensor("y", [CH, D], F32, kind="ExternalOutput").ap()
        out_names.append("y")
    if dbg_out:
        dram[dbg_out] = nc.dram_tensor(dbg_out, [128, NPT * TP], F32,
                                       kind="ExternalOutput").ap()
        out_names.append(dbg_out)

    with es:
        blk = es.enter_context(nc.Block())
        p = Prog(nc, blk, es)
        sbuf = lambda nm, shp, dt: es.enter_context(nc.sbuf_tensor(nm, shp, dt))

        h = {"dram": dram, "stages": stages}
        h["ones1f"] = ones1f = sbuf("ones1f", [128, 1], F32)
        h["onesrf"] = onesrf = sbuf("onesrf", [1, 128], F32)
        h["idf"] = idf = sbuf("idf", [128, 128], F32)
        h["idb"] = idb = sbuf("idb", [128, 128], BF16)
        h["epsA"] = epsA = sbuf("epsA", [1, 1], F32)
        h["epsB"] = epsB = sbuf("epsB", [1, 1], F32)
        rows = sbuf("rows", [1, TP + 2 * BCH], F32)
        h["RTMP"] = rows[0:1, 0:TP]
        h["RB0"] = rows[0:1, TP:TP + BCH]
        h["RB1"] = rows[0:1, TP + BCH:TP + 2 * BCH]
        h["m0"] = sbuf("m0", [1, TP], F32)
        h["r0"] = sbuf("r0", [1, TP], F32)
        PSf = es.enter_context(nc.psum_tensor("PSf", [128, 8 * BANK], F32))
        h["PSb"] = PSb = lambda k, off=0, n=BANK: PSf[:, k * BANK + off: k * BANK + off + n]
        h["PSf"] = PSf
        h["sbuf"] = sbuf
        h["nc"], h["p"], h["blk"] = nc, p, blk

        C = Chain(p, "main")
        h["C"] = C

        self_p = p
        ls_reg = {"T": 0, "S": 0}

        def ln_stats_rows(ch, src_tiles, npt, n, sq_buf, mrow, rrow, trow, eps_ap, nelem):
            ch.flush()
            tl = ch.tail()
            bT, bS = ls_reg["T"], ls_reg["S"]
            sT, sS = p.sem("lsT"), p.sem("lsS")

            @blk.scalar
            def _(s):
                if tl:
                    s.wait_ge(p.sem(tl[0]), tl[1])
                for pt in range(npt):
                    if pt >= 2:
                        s.wait_ge(sT, bT + pt - 1)
                    s.activation(sq_buf[:, (pt % 2) * n:(pt % 2 + 1) * n],
                                 src_tiles(pt), AF.Square).then_inc(sS, 1)

            @blk.tensor
            def _(t):
                if tl:
                    t.wait_ge(p.sem(tl[0]), tl[1])
                for pt in range(npt):
                    t.matmul(PSb(4, 0, n)[0:1, :], ones1f[:], src_tiles(pt),
                             start=(pt == 0), stop=(pt == npt - 1),
                             skip_group_check=True)
                    t.wait_ge(sS, bS + pt + 1)
                    t.matmul(PSb(5, 0, n)[0:1, :], ones1f[:],
                             sq_buf[:, (pt % 2) * n:(pt % 2 + 1) * n],
                             start=(pt == 0), stop=(pt == npt - 1),
                             skip_group_check=True).then_inc(sT, 1)
            ls_reg["T"] += npt
            ls_reg["S"] += npt

            def v_rows(v):
                v.wait_ge(sT, ls_reg["T"])
                i1 = v.tensor_scalar_mul(mrow, PSb(4, 0, n)[0:1, :], 1.0 / nelem)
                v.tensor_scalar_mul(trow, PSb(5, 0, n)[0:1, :], 1.0 / nelem)
                self_p.vg(v, i1)
                i2 = v.tensor_tensor(rrow, mrow, mrow, ALU.mult)
                self_p.vg(v, i2)
                return v.tensor_tensor(trow, trow, rrow, ALU.subtract)
            ch.step("vector", v_rows)
            ch.step("scalar", lambda s: s.activation(trow, trow, AF.Sqrt, bias=eps_ap[:, 0:1]))
            ch.step("vector", lambda v: v.reciprocal(rrow, trow))
        h["ln_stats_rows"] = ln_stats_rows

        def bcast_row(ch, row_ap, n, bank):
            ch.step("tensor", lambda t: t.matmul(PSb(bank, 0, n), onesrf[:], row_ap,
                                                 start=True, stop=True))
        h["bcast_row"] = bcast_row

        def normalize(ch, src_tiles, dst_tiles, npt, n, mb, rb):
            def v_norm(v):
                for pt in range(npt):
                    d = dst_tiles(pt)
                    i0 = v.tensor_tensor(d, src_tiles(pt), mb, ALU.subtract)
                    self_p.vg(v, i0)
                    i1 = v.tensor_tensor(d, d, rb, ALU.mult)
                return i1
            ch.step("vector", v_norm)
        h["normalize"] = normalize

        def mm_acc(t, psum_ap, lhsT_fn, rhs_fn, nk):
            for kt in range(nk):
                i1 = t.matmul(psum_ap, lhsT_fn(kt), rhs_fn(kt),
                              start=(kt == 0), stop=(kt == nk - 1))
            return i1
        h["mm_acc"] = mm_acc

        C.step("vector", lambda v: _last(
            v.memset(ones1f[:], 1.0), v.memset(onesrf[:], 1.0),
            v.memset(epsA[:], EPS_OUT), v.memset(epsB[:], EPS_IN)))
        C.step("sync", lambda sy: sy.dma_start(out=idf[:], in_=dram["ident"]), dma=True)
        C.step("gpsimd", lambda g: g.dma_start(out=idb[:], in_=dram["ident"]), dma=True)
        C.flush()

        if "IN" not in stages:
            C.step("sync", lambda sy: sy.dma_start(out=dram["xres"],
                                                   in_=dram["xres_in"]), dma=True)
        if "A" in stages:
            emit_stage_a(h)
        if "B" in stages:
            emit_stage_b(h)
        if "C" in stages:
            emit_stage_c(h)
        if "D" in stages:
            emit_stage_d(h)
        if "OUT" in stages:
            emit_output(h)

        if dbg_out:
            C.flush()
            tl = C.tail()
            st = p.sem("st_dbg")

            @blk.sync
            def _(sy):
                if tl:
                    sy.wait_ge(p.sem(tl[0]), tl[1])
                sy.dma_start(out=dram[dbg_out], in_=dram["xres"]).then_inc(st, 16)
                sy.wait_ge(st, 16)

        C.flush()
        finals = dict(C.cums)

        @blk.sync
        def _(sy):
            for name, val in finals.items():
                sy.wait_ge(p.sem(name), val)
    return nc, out_names


def emit_transpose_in(h, xT):
    nc, p, blk, C = h["nc"], h["p"], h["blk"], h["C"]
    PSb, idf = h["PSb"], h["idf"]
    d_src = h["dram"]["xs"]
    with contextlib.ExitStack() as ph:
        xstage = ph.enter_context(nc.sbuf_tensor("xstage", [128, 2 * D], F32))
        xld = [p.sem("xld0"), p.sem("xld1")]
        xtp, xcp = p.sem("xtp"), p.sem("xcp")
        C.flush()
        tl = C.tail()

        @blk.sync
        def _(sy):
            if tl:
                sy.wait_ge(p.sem(tl[0]), tl[1])
            for tt in range(NTT):
                if tt >= 2:
                    sy.wait_ge(xcp, tt - 1)
                sy.dma_start(out=xstage[:, (tt % 2) * D:(tt % 2 + 1) * D],
                             in_=d_src[tt * 128:(tt + 1) * 128, :]).then_inc(xld[tt % 2], 16)

        @blk.tensor
        def _(t):
            if tl:
                t.wait_ge(p.sem(tl[0]), tl[1])
            for tt in range(NTT):
                t.wait_ge(xld[tt % 2], (tt // 2 + 1) * 16)
                if tt >= 2:
                    t.wait_ge(xcp, tt - 1)
                for pt in range(NPT):
                    bank = (tt % 2) * 2 + pt // 4
                    ins = t.transpose(PSb(bank, (pt % 4) * 128, 128),
                                      xstage[:, (tt % 2) * D + pt * 128:
                                             (tt % 2) * D + (pt + 1) * 128],
                                      idf[:])
                ins.then_inc(xtp, 1)

        @blk.vector
        def _(v):
            for tt in range(NTT):
                v.wait_ge(xtp, tt + 1)
                for pt in range(NPT):
                    bank = (tt % 2) * 2 + pt // 4
                    ins = v.tensor_copy(xT[:, pt * TP + tt * 128: pt * TP + (tt + 1) * 128],
                                        PSb(bank, (pt % 4) * 128, 128))
                ins.then_inc(xcp, 1)

        C.step("vector", lambda v: _last(v.wait_ge(xcp, NTT),
                                         v.tensor_copy(h["idf"][0:1, 0:1], h["idf"][0:1, 0:1])))
        C.flush()


def emit_stage_a(h):
    nc, p, blk, C, dram = h["nc"], h["p"], h["blk"], h["C"], h["dram"]
    PSb, PSf, sbuf = h["PSb"], h["PSf"], h["sbuf"]
    ones1f, onesrf, idf, idb, epsA, epsB = (h[k] for k in
                                            ("ones1f", "onesrf", "idf", "idb", "epsA", "epsB"))
    RTMP, RB0, RB1, m0, r0 = h["RTMP"], h["RB0"], h["RB1"], h["m0"], h["r0"]
    ln_stats_rows, bcast_row, normalize, mm_acc = (
        h[k] for k in ("ln_stats_rows", "bcast_row", "normalize", "mm_acc"))
    d_kT, d_vh, d_qT, d_oT, d_xres = (dram[k] for k in
                                      ("kT_scr", "v_scr", "qT_scr", "oT_scr", "xres"))
    Wt = lambda w, kt, mt: w[:, kt * D + mt * 128: kt * D + (mt + 1) * 128]

    vecs = sbuf("vecs_a", [128, 24], F32)
    for i, nm in enumerate(("b_proj", "ln_g", "ln_b")):
        C.step("sync", lambda sy, i=i, nm=nm: sy.dma_start(
            out=vecs[:, i * 8:(i + 1) * 8],
            in_=dram[nm].rearrange("(t p) -> p t", p=128),
            allow_slow_non_contiguous=True), dma=True)
    BPJ, LNG, LNB = vecs[:, 0:8], vecs[:, 8:16], vecs[:, 16:24]

    with contextlib.ExitStack() as s1:
        xT = s1.enter_context(nc.sbuf_tensor("xT", [128, NPT * TP], F32))
        XT = lambda pt, tc: xT[:, pt * TP + tc * TC: pt * TP + (tc + 1) * TC]
        emit_transpose_in(h, xT)

        with contextlib.ExitStack() as ph:
            sq = ph.enter_context(nc.sbuf_tensor("sq_a", [128, 2 * TC], F32))
            for tc in range(NTC):
                ln_stats_rows(C, lambda pt, tc=tc: XT(pt, tc), NPT, TC, sq,
                              m0[:, tc * TC:(tc + 1) * TC], r0[:, tc * TC:(tc + 1) * TC],
                              RTMP[:, 0:TC], epsA, D)
            C.flush()

        with contextlib.ExitStack() as ph:
            wq_sb = ph.enter_context(nc.sbuf_tensor("wq_sb", [128, NPT * D], BF16))
            lnq = ph.enter_context(nc.sbuf_tensor("lnq", [128, NPT * TC], BF16))
            qst = ph.enter_context(nc.sbuf_tensor("qst", [128, 2 * TC], BF16))
            C.step("gpsimd", lambda g: g.dma_start(
                out=wq_sb[:].rearrange("p (k n) -> p k n", n=D),
                in_=dram["w_q"].rearrange("(k p) n -> p k n", p=128)), dma=True)
            for tc in range(NTC):
                bcast_row(C, m0[:, tc * TC:(tc + 1) * TC], TC, 6)
                bcast_row(C, r0[:, tc * TC:(tc + 1) * TC], TC, 7)
                normalize(C, lambda pt, tc=tc: XT(pt, tc),
                          lambda pt: lnq[:, pt * TC:(pt + 1) * TC], NPT, TC,
                          PSb(6, 0, TC), PSb(7, 0, TC))
                items = []
                for mt in range(NPT):
                    items.append({
                        "mm": lambda t, bk, mt=mt: mm_acc(
                            t, PSb(bk, 0, TC), lambda kt: Wt(wq_sb, kt, mt),
                            lambda kt: lnq[:, kt * TC:(kt + 1) * TC], NPT),
                        "evac": lambda e, bk, mt=mt: e.tensor_copy(
                            qst[:, (mt % 2) * TC:(mt % 2 + 1) * TC], PSb(bk, 0, TC)),
                        "store": lambda sy, mt=mt, tc=tc: sy.dma_start(
                            out=d_qT[mt * 128:(mt + 1) * 128, tc * TC:(tc + 1) * TC],
                            in_=qst[:, (mt % 2) * TC:(mt % 2 + 1) * TC]),
                        "st_class": mt % 2,
                        "buf_lag": 2})
                pipelined_mm(h, "pA5", items, nbank=2, bank0=0)

        C.step("sync", lambda sy: sy.dma_start(out=d_xres, in_=xT[:]), dma=True)
        C.flush()

    with contextlib.ExitStack() as ph:
        wk_sb = ph.enter_context(nc.sbuf_tensor("wk_sb", [128, NPT * D], BF16))
        wv_sb = ph.enter_context(nc.sbuf_tensor("wv_sb", [128, NPT * D], BF16))
        bstage = ph.enter_context(nc.sbuf_tensor("bstage", [128, 2 * (BCH // 128) * D], F32))
        sjunk = ph.enter_context(nc.sbuf_tensor("sjunk_a2", [128, D], F32))
        vjunk = ph.enter_context(nc.sbuf_tensor("vjunk_a2", [128, D], BF16))
        stat = ph.enter_context(nc.sbuf_tensor("stat_a2", [128, 16], F32))
        epsc = ph.enter_context(nc.sbuf_tensor("epsc_a2", [128, 1], F32))
        lnb = ph.enter_context(nc.sbuf_tensor("lnb", [128, NPT * BCH], BF16))
        kst = ph.enter_context(nc.sbuf_tensor("kst", [128, NPT * BCH], BF16))
        vst = ph.enter_context(nc.sbuf_tensor("vst", [128, NPT * BCH], F32))
        vks = ph.enter_context(nc.sbuf_tensor("vks", [128, 2 * (BCH // 128) * 130], BF16))
        NT4 = BCH // 128
        NG = NT4 * NPT // 4
        BST = lambda bc, t: bstage[:, ((bc % 2) * NT4 + t) * D:
                                   ((bc % 2) * NT4 + t + 1) * D]
        VKS = lambda ipt: vks[:, (ipt % 2) * NT4 * 130:(ipt % 2 + 1) * NT4 * 130]

        sLdA = [p.sem("aLd0"), p.sem("aLd1")]
        sSq, sVar, sSqt, sStat, sNrm, sTpA, sCpA = (p.sem(x) for x in
            ("aSq", "aVar", "aSqt", "aStat", "aNrm", "aTpA", "aCpA"))
        sKmm, sVmm, sEvA, sVkTp, sVkCp = (p.sem(x) for x in
            ("aKmm", "aVmm", "aEvA", "aVkTp", "aVkCp"))
        sStK = [p.sem("aStK0"), p.sem("aStK1")]
        sStV = [p.sem("aStV0"), p.sem("aStV1")]

        C.step("gpsimd", lambda g: g.dma_start(
            out=wk_sb[:].rearrange("p (k n) -> p k n", n=D),
            in_=dram["w_k"].rearrange("(k p) n -> p k n", p=128)), dma=True)
        C.step("gpsimd", lambda g: g.dma_start(
            out=wv_sb[:].rearrange("p (k n) -> p k n", n=D),
            in_=dram["w_v"].rearrange("(k p) n -> p k n", p=128)), dma=True)
        C.step("vector", lambda v: _last(v.memset(vks[:], 1.0),
                                         v.memset(stat[:], 0.0),
                                         v.memset(epsc[:], EPS_OUT),
                                         v.memset(vjunk[:], 0.0),
                                         v.memset(sjunk[:], 0.0)))
        C.flush()
        tl = C.tail()
        wload_target = C.cums["main_Dgpsimd"]

        @blk.sync
        def _(sy):
            if tl:
                sy.wait_ge(p.sem(tl[0]), tl[1])
            for bc in range(NBC):
                if bc >= 2:
                    sy.wait_ge(sTpA, (bc - 1) * NG)
                sy.dma_start(
                    out=bstage[:, (bc % 2) * NT4 * D:(bc % 2 + 1) * NT4 * D]
                    .rearrange("p (t n) -> p t n", n=D),
                    in_=dram["xb"][bc * BCH:(bc + 1) * BCH, :].rearrange(
                        "(t p) n -> p t n", p=128)).then_inc(sLdA[bc % 2], 16)
                for mt in range(NPT):
                    sy.wait_ge(sEvA, bc * 16 + 2 * mt + 1)
                    sy.dma_start(
                        out=d_kT[mt * 128:(mt + 1) * 128, bc * BCH:(bc + 1) * BCH],
                        in_=kst[:, mt * BCH:(mt + 1) * BCH]
                    ).then_inc(sStK[bc % 2], 16)
                for pt in range(NPT):
                    ipt = bc * 8 + pt
                    sy.wait_ge(sVkCp, ipt + 1)
                    sy.dma_start(
                        out=d_vh[pt, bc * BCH:(bc + 1) * BCH, :].rearrange(
                            "(t p) c -> p t c", p=128),
                        in_=VKS(ipt)[:, 0:NT4 * 130].rearrange(
                            "p (t c) -> p t c", c=130)).then_inc(sStV[ipt % 2], 16)

        @blk.scalar
        def _(s):
            if tl:
                s.wait_ge(p.sem(tl[0]), tl[1])
            for bc in range(NBC):
                s.wait_ge(sLdA[bc % 2], (bc // 2 + 1) * 16)
                if bc >= 1:
                    s.wait_ge(sStat, bc)
                for t in range(NT4):
                    if bc * NT4 + t >= 1:
                        s.wait_ge(sSq, bc * NT4 + t)
                    s.activation(sjunk[:], BST(bc, t), AF.Square,
                                 accum_out=stat[:, 4 + t:5 + t]).then_inc(sSq, 1)
                s.wait_ge(sVar, bc + 1)
                s.activation(stat[:, 12:16], stat[:, 4:8], AF.Sqrt,
                             bias=epsc[:, 0:1]).then_inc(sSqt, 1)

        @blk.vector
        def _(v):
            if tl:
                v.wait_ge(p.sem(tl[0]), tl[1])
            for bc in range(NBC):
                v.wait_ge(sLdA[bc % 2], (bc // 2 + 1) * 16)
                if bc >= 1:
                    v.wait_ge(sStat, bc)
                for t in range(NT4):
                    i1 = v.tensor_scalar(vjunk[:], BST(bc, t), 1.0, 0.0,
                                         ALU.mult, ALU.add,
                                         accum_out=stat[:, t:t + 1])
                    p.vg(v, i1)
                v.wait_ge(sSq, (bc + 1) * NT4)
                i1 = v.tensor_scalar(stat[:, 8:12], stat[:, 0:4], 1.0 / D, None,
                                     ALU.mult)
                p.vg(v, i1)
                i2 = v.tensor_tensor(stat[:, 0:4], stat[:, 8:12], stat[:, 8:12],
                                     ALU.mult)
                p.vg(v, i2)
                i3 = v.tensor_scalar(stat[:, 4:8], stat[:, 4:8], 1.0 / D, None,
                                     ALU.mult)
                p.vg(v, i3)
                i4 = v.tensor_tensor(stat[:, 4:8], stat[:, 4:8], stat[:, 0:4],
                                     ALU.subtract).then_inc(sVar, 1)
                v.wait_ge(sSqt, bc + 1)
                i6 = v.reciprocal(stat[:, 12:16], stat[:, 12:16])
                p.vg(v, i6)
                for t in range(NT4):
                    if bc >= 1:
                        v.wait_ge(sTpA, (bc - 1) * NG + 2 * t + 2)
                    i7 = v.tensor_scalar(BST(bc, t), BST(bc, t),
                                         stat[:, 8 + t:9 + t], stat[:, 12 + t:13 + t],
                                         ALU.subtract, ALU.mult)
                    ins = i7
                    if t < NT4 - 1:
                        p.vg(v, i7)
                ins.then_inc(sNrm, NT4)
                v.nop().then_inc(sStat, 1)
                for g in range(NG):
                    v.wait_ge(sTpA, bc * NG + g + 1)
                    for i in range(4):
                        gi = g * 4 + i
                        pt, tt4 = divmod(gi, NT4)
                        ins = v.tensor_copy(
                            lnb[:, pt * BCH + tt4 * 128: pt * BCH + (tt4 + 1) * 128],
                            PSb(2 + g % 2, i * 128, 128))
                    ins.then_inc(sCpA, 1)
                if bc >= 1:
                    v.wait_ge(sStK[(bc - 1) % 2], ((bc - 1) // 2 + 1) * 8 * 16)
                for mt in range(NPT):
                    v.wait_ge(sKmm, bc * 8 + mt + 1)
                    v.tensor_copy(kst[:, mt * BCH:(mt + 1) * BCH],
                                  PSb(4 + mt % 2, 0, BCH)).then_inc(sEvA, 1)
                    v.wait_ge(sVmm, bc * 8 + mt + 1)
                    if bc >= 1:
                        v.wait_ge(sVkTp, (bc - 1) * 8 + mt + 1)
                    v.tensor_copy(vst[:, mt * BCH:(mt + 1) * BCH],
                                  PSb(6 + mt % 2, 0, BCH)).then_inc(sEvA, 1)
                for pt in range(NPT):
                    ipt = bc * 8 + pt
                    v.wait_ge(sVkTp, ipt + 1)
                    if ipt >= 2:
                        v.wait_ge(sStV[ipt % 2], ((ipt - 2) // 2 + 1) * 16)
                    for tt4 in range(NT4):
                        v.tensor_copy(VKS(ipt)[:, tt4 * 130 + 0: tt4 * 130 + 64],
                                      PSb(0 + pt % 2, tt4 * 128, 64))
                        ins = v.tensor_copy(
                            VKS(ipt)[:, tt4 * 130 + 65: tt4 * 130 + 129],
                            PSb(0 + pt % 2, tt4 * 128 + 64, 64))
                    ins.then_inc(sVkCp, 1)

        @blk.tensor
        def _(t):
            if tl:
                t.wait_ge(p.sem(tl[0]), tl[1])
            t.wait_ge(p.sem("main_Dgpsimd"), wload_target)
            for bc in range(NBC):
                for g in range(NG):
                    t.wait_ge(sNrm, (bc + 1) * NT4)
                    if bc * NG + g >= 2:
                        t.wait_ge(sCpA, bc * NG + g - 1)
                    for i in range(4):
                        gi = g * 4 + i
                        pt, tt4 = divmod(gi, NT4)
                        ins = t.transpose(PSb(2 + g % 2, i * 128, 128),
                                          BST(bc, tt4)[:, pt * 128:(pt + 1) * 128],
                                          idf[:])
                    ins.then_inc(sTpA, 1)
                t.wait_ge(sCpA, bc * NG + NG)
                for mt in range(NPT):
                    ikv = bc * 8 + mt
                    if ikv >= 2:
                        t.wait_ge(sEvA, (ikv - 2) * 2 + 2)
                    for kt in range(NPT):
                        ins = t.matmul(PSb(4 + mt % 2, 0, BCH), Wt(wk_sb, kt, mt),
                                       lnb[:, kt * BCH:(kt + 1) * BCH],
                                       start=(kt == 0), stop=(kt == NPT - 1),
                                       skip_group_check=True)
                    ins.then_inc(sKmm, 1)
                    if ikv >= 2:
                        t.wait_ge(sEvA, (ikv - 2) * 2 + 3)
                    for kt in range(NPT):
                        ins = t.matmul(PSb(6 + mt % 2, 0, BCH), Wt(wv_sb, kt, mt),
                                       lnb[:, kt * BCH:(kt + 1) * BCH],
                                       start=(kt == 0), stop=(kt == NPT - 1),
                                       skip_group_check=True)
                    ins.then_inc(sVmm, 1)
                for pt in range(NPT):
                    ipt = bc * 8 + pt
                    t.wait_ge(sEvA, bc * 16 + 2 * pt + 2)
                    if ipt >= 2:
                        t.wait_ge(sVkCp, ipt - 1)
                    for tt4 in range(NT4):
                        ins = t.transpose(PSb(0 + pt % 2, tt4 * 128, 128),
                                          vst[:, pt * BCH + tt4 * 128:
                                              pt * BCH + (tt4 + 1) * 128],
                                          idf[:])
                    ins.then_inc(sVkTp, 1)

        def a2join(v):
            v.wait_ge(sStK[0], ((NBC + 1) // 2) * 8 * 16)
            v.wait_ge(sStK[1], (NBC // 2) * 8 * 16)
            v.wait_ge(sStV[0], (NBC * 8 // 2) * 16)
            v.wait_ge(sStV[1], (NBC * 8 // 2) * 16)
            return v.tensor_copy(stat[0:1, 0:1], stat[0:1, 0:1])
        C.step("vector", a2join)
        C.flush()

    emit_attention(h, d_kT, d_vh, d_qT, d_oT)

    with contextlib.ExitStack() as s3:
        xT = s3.enter_context(nc.sbuf_tensor("xT5", [128, NPT * TP], F32))
        ybuf = s3.enter_context(nc.sbuf_tensor("ybuf", [128, NPT * TP], F32))
        wp_sb = s3.enter_context(nc.sbuf_tensor("wp_sb", [128, NPT * D], BF16))
        och = s3.enter_context(nc.sbuf_tensor("och", [128, NPT * TC], BF16))
        sq5 = s3.enter_context(nc.sbuf_tensor("sq5", [128, 2 * TC], F32))
        XT = lambda pt, tc: xT[:, pt * TP + tc * TC: pt * TP + (tc + 1) * TC]
        YT = lambda pt, tc: ybuf[:, pt * TP + tc * TC: pt * TP + (tc + 1) * TC]
        m1, r1 = m0, r0

        C.step("sync", lambda sy: sy.dma_start(out=xT[:], in_=d_xres), dma=True)
        C.step("gpsimd", lambda g: g.dma_start(
            out=wp_sb[:].rearrange("p (k n) -> p k n", n=D),
            in_=dram["w_proj"].rearrange("(k p) n -> p k n", p=128)), dma=True)

        for tc in range(NTC):
            C.step("sync", lambda sy, tc=tc: sy.dma_start(
                out=och[:].rearrange("p (k n) -> p k n", n=TC),
                in_=d_oT[:, tc * TC:(tc + 1) * TC].rearrange("(k p) n -> p k n", p=128)), dma=True)
            bcast_row(C, m0[:, tc * TC:(tc + 1) * TC], TC, 6)
            bcast_row(C, r0[:, tc * TC:(tc + 1) * TC], TC, 7)
            items = []
            for mt in range(NPT):
                def v_y(v, bk, mt=mt, tc=tc):
                    y = YT(mt, tc)
                    i0 = v.tensor_scalar_add(y, PSb(bk, 0, TC), BPJ[:, mt:mt + 1])
                    p.vg(v, i0)
                    i1 = v.tensor_tensor(sq5[:, 0:TC], XT(mt, tc), PSb(6, 0, TC),
                                         ALU.subtract)
                    p.vg(v, i1)
                    i2 = v.tensor_tensor(sq5[:, 0:TC], sq5[:, 0:TC], PSb(7, 0, TC),
                                         ALU.mult)
                    p.vg(v, i2)
                    return v.tensor_tensor(y, y, sq5[:, 0:TC], ALU.add)
                items.append({
                    "mm": lambda t, bk, mt=mt: mm_acc(
                        t, PSb(bk, 0, TC), lambda kt: Wt(wp_sb, kt, mt),
                        lambda kt: och[:, kt * TC:(kt + 1) * TC], NPT),
                    "evac": v_y})
            pipelined_mm(h, "pA5", items, nbank=4, bank0=0)
        for tc in range(NTC):
            ln_stats_rows(C, lambda pt, tc=tc: YT(pt, tc), NPT, TC, sq5,
                          m1[:, tc * TC:(tc + 1) * TC], r1[:, tc * TC:(tc + 1) * TC],
                          RTMP[:, 0:TC], epsB, D)
        C.flush()
        for tc in range(NTC):
            bcast_row(C, m1[:, tc * TC:(tc + 1) * TC], TC, 6)
            bcast_row(C, r1[:, tc * TC:(tc + 1) * TC], TC, 7)
            def v_fin(v, tc=tc):
                for pt in range(NPT):
                    n = sq5[:, 0:TC]
                    i0 = v.tensor_tensor(n, YT(pt, tc), PSb(6, 0, TC), ALU.subtract)
                    p.vg(v, i0)
                    i1 = v.tensor_tensor(n, n, PSb(7, 0, TC), ALU.mult)
                    p.vg(v, i1)
                    i2 = v.scalar_tensor_tensor(XT(pt, tc), n, LNG[:, pt:pt + 1], XT(pt, tc),
                                                ALU.mult, ALU.add)
                    p.vg(v, i2)
                    i3 = v.tensor_scalar_add(XT(pt, tc), XT(pt, tc), LNB[:, pt:pt + 1])
                return i3
            C.step("vector", v_fin)
            C.flush()
        C.step("sync", lambda sy: sy.dma_start(out=d_xres, in_=xT[:]), dma=True)
        C.flush()


def emit_precast(h):
    nc, p, blk, dram = h["nc"], h["p"], h["blk"], h["dram"]
    sPc = p.sem("pcast")
    C = h["C"]
    C.flush()
    tl = C.tail()

    @blk.gpsimd
    def _(g):
        if tl:
            g.wait_ge(p.sem(tl[0]), tl[1])
        for e in range(NE):
            g.dma_start(out=dram["wbf_g"][e], in_=dram["gate_w"][e]).then_inc(sPc, 16)
            g.dma_start(out=dram["wbf_u"][e], in_=dram["up_w"][e]).then_inc(sPc, 16)
            g.dma_start(out=dram["wbf_d"][e], in_=dram["down_w"][e]).then_inc(sPc, 16)
    h["sPcast"] = sPc


def emit_stage_b(h):
    nc, p, blk, C, dram = h["nc"], h["p"], h["blk"], h["C"], h["dram"]
    PSb, PSf, sbuf = h["PSb"], h["PSf"], h["sbuf"]
    ones1f, onesrf, idf, epsA = h["ones1f"], h["onesrf"], h["idf"], h["epsA"]
    RTMP, m0, r0 = h["RTMP"], h["m0"], h["r0"]
    ln_stats_rows, bcast_row, normalize = (h[k] for k in
                                           ("ln_stats_rows", "bcast_row", "normalize"))
    d_xres, d_ln1 = dram["xres"], dram["ln1_scr"]
    NKI = I_FF // 128
    w8T = sbuf("w8T", [8, TP], F32)
    h["w8T"] = w8T

    with contextlib.ExitStack() as s1:
        xT = s1.enter_context(nc.sbuf_tensor("xTb", [128, NPT * TP], F32))
        XT = lambda pt, tc: xT[:, pt * TP + tc * TC: pt * TP + (tc + 1) * TC]
        sq = s1.enter_context(nc.sbuf_tensor("sq_b", [128, 2 * TC], F32))
        lnc = s1.enter_context(nc.sbuf_tensor("lnc_b", [128, NPT * TC], BF16))
        rtf = s1.enter_context(nc.sbuf_tensor("rtf_b", [128, NPT * NE], F32))
        lnf = s1.enter_context(nc.sbuf_tensor("lnf_b", [128, NPT * TC], F32))
        pr = s1.enter_context(nc.sbuf_tensor("pr_b", [128, NTT * NE], F32))
        prb = s1.enter_context(nc.sbuf_tensor("prb_b", [128, NTT * NE], F32))
        w8 = s1.enter_context(nc.sbuf_tensor("w8_b", [128, NTT * NE], F32))
        neg = s1.enter_context(nc.sbuf_tensor("neg_b", [128, NE], F32))
        rb1 = s1.enter_context(nc.sbuf_tensor("rb1_b", [1, NE], F32))
        rbb = s1.enter_context(nc.sbuf_tensor("rbb_b", [128, NE], F32))
        rowm = s1.enter_context(nc.sbuf_tensor("rowm_b", [128, 4], F32))
        mskd = s1.enter_context(nc.sbuf_tensor("mskd_b", [128, NE], F32))
        gei = s1.enter_context(nc.sbuf_tensor("gei_b", [128, NE], mybir.dt.int32))
        if SPARSE_MOE:
            lntm0 = s1.enter_context(nc.sbuf_tensor("lntm0_b", [128, D], BF16))

        C.step("sync", lambda sy: sy.dma_start(out=xT[:], in_=d_xres), dma=True)
        for tc in range(NTC):
            ln_stats_rows(C, lambda pt, tc=tc: XT(pt, tc), NPT, TC, sq,
                          m0[:, tc * TC:(tc + 1) * TC], r0[:, tc * TC:(tc + 1) * TC],
                          RTMP[:, 0:TC], epsA, D)
        C.flush()

        C.step("sync", lambda sy: sy.dma_start(out=rb1[:], in_=dram["r_bias"][None, :]),
               dma=True)
        C.step("sync", lambda sy: sy.dma_start(
            out=rtf[:].rearrange("p (k n) -> p k n", n=NE),
            in_=dram["router_T"].rearrange("(k p) n -> p k n", p=128)), dma=True)
        bcast_row(C, rb1[0:1, :], NE, 5)
        C.step("vector", lambda v: _last(v.memset(neg[:], -1e30),
                                         v.tensor_copy(rbb[:], PSb(5, 0, NE))))
        for tc in range(NTC):
            bcast_row(C, m0[:, tc * TC:(tc + 1) * TC], TC, 6)
            bcast_row(C, r0[:, tc * TC:(tc + 1) * TC], TC, 7)
            normalize(C, lambda pt, tc=tc: XT(pt, tc),
                      lambda pt: lnf[:, pt * TC:(pt + 1) * TC], NPT, TC,
                      PSb(6, 0, TC), PSb(7, 0, TC))
            for st3 in range(TC // 128):
                tt = tc * (TC // 128) + st3
                C.step("tensor", lambda t, st3=st3: _last(*[
                    t.matmul(PSb(4, 0, NE)[0:128, :],
                             lnf[:, kt * TC + st3 * 128: kt * TC + (st3 + 1) * 128],
                             rtf[:, kt * NE:(kt + 1) * NE],
                             start=(kt == 0), stop=(kt == NPT - 1)) for kt in range(NPT)]))
                def v_lg(v, tt=tt):
                    return v.tensor_tensor(pr[:, tt * NE:(tt + 1) * NE],
                                           PSb(4, 0, NE)[0:128, :], rbb[:], ALU.add)
                C.step("vector", v_lg)
                C.step("scalar", lambda s, tt=tt: s.activation(
                    prb[:, tt * NE:(tt + 1) * NE], pr[:, tt * NE:(tt + 1) * NE],
                    AF.Sigmoid))
                def v_top2(v, tt=tt):
                    lz = pr[:, tt * NE:(tt + 1) * NE]
                    pz = prb[:, tt * NE:(tt + 1) * NE]
                    wz = w8[:, tt * NE:(tt + 1) * NE]
                    i1 = v.reduce_max(rowm[:, 0:1], lz, axis=mybir.AxisListType.X)
                    p.vg(v, i1)
                    i2 = v.tensor_scalar(gei[:], lz, rowm[:, 0:1], None, ALU.is_ge)
                    p.vg(v, i2)
                    i3 = v.select(mskd[:], gei[:], neg[:], lz, add_drain=True)
                    p.vg(v, i3)
                    return v.reduce_max(rowm[:, 1:2], mskd[:], axis=mybir.AxisListType.X)
                C.step("vector", v_top2)
                C.step("scalar", lambda s: s.activation(rowm[:, 2:4], rowm[:, 0:2],
                                                        AF.Sigmoid))
                def v_top2b(v, tt=tt):
                    lz = pr[:, tt * NE:(tt + 1) * NE]
                    pz = prb[:, tt * NE:(tt + 1) * NE]
                    wz = w8[:, tt * NE:(tt + 1) * NE]
                    i5 = v.tensor_tensor(rowm[:, 2:3], rowm[:, 2:3], rowm[:, 3:4], ALU.add)
                    p.vg(v, i5)
                    i6 = v.reciprocal(rowm[:, 3:4], rowm[:, 2:3])
                    p.vg(v, i6)
                    i7 = v.tensor_scalar(wz, lz, rowm[:, 1:2], None, ALU.is_ge)
                    p.vg(v, i7)
                    i8 = v.tensor_tensor(wz, wz, pz, ALU.mult)
                    p.vg(v, i8)
                    return v.tensor_scalar(wz, wz, rowm[:, 3:4], None, ALU.mult)
                C.step("vector", v_top2b)
                C.step("tensor", lambda t, tt=tt: t.transpose(
                    PSb(6, 0, 128)[0:NE, :], w8[:, tt * NE:(tt + 1) * NE], idf[:]))
                C.step("vector", lambda v, tt=tt: v.tensor_copy(
                    w8T[:, tt * 128:(tt + 1) * 128], PSb(6, 0, 128)[0:NE, :]))
            def v_cast(v, tc=tc):
                for pt in range(NPT):
                    i1 = v.tensor_copy(lnc[:, pt * TC:(pt + 1) * TC],
                                       lnf[:, pt * TC:(pt + 1) * TC])
                return i1
            C.step("vector", v_cast)
            for pt in range(NPT):
                C.step("sync", lambda sy, pt=pt, tc=tc: sy.dma_start(
                    out=d_ln1[pt * 128:(pt + 1) * 128, tc * TC:(tc + 1) * TC],
                    in_=lnc[:, pt * TC:(pt + 1) * TC]), dma=True)
            if SPARSE_MOE:
                for st3 in range(TC // 128):
                    def t_tm(t, st3=st3, tc=tc):
                        for pt in range(NPT):
                            ins = t.transpose(
                                PSb(pt // 4, (pt % 4) * 128, 128),
                                lnf[:, pt * TC + st3 * 128: pt * TC + (st3 + 1) * 128],
                                idf[:])
                        return ins
                    C.step("tensor", t_tm)
                    def v_tm(v, st3=st3):
                        for pt in range(NPT):
                            ins = v.tensor_copy(lntm0[:, pt * 128:(pt + 1) * 128],
                                                PSb(pt // 4, (pt % 4) * 128, 128))
                        return ins
                    C.step("vector", v_tm)
                    C.step("sync", lambda sy, st3=st3, tc=tc: sy.dma_start(
                        out=dram["ln1_tm"][tc * TC + st3 * 128: tc * TC + (st3 + 1) * 128, :],
                        in_=lntm0[:]), dma=True)
            C.flush()

    if SPARSE_MOE:
        emit_b15(h)
        emit_b2_sparse(h)
        return

        with contextlib.ExitStack() as ph:
            wg = ph.enter_context(nc.sbuf_tensor("wg_b", [128, NPT * I_FF], BF16))
            wu = ph.enter_context(nc.sbuf_tensor("wu_b", [128, NPT * I_FF], BF16))
            wd = ph.enter_context(nc.sbuf_tensor("wd_b", [128, NKI * D], BF16))
            lntc = ph.enter_context(nc.sbuf_tensor("lntc_b", [128, 2 * NPT * TC], BF16))
            gu = ph.enter_context(nc.sbuf_tensor("gu_b", [128, NKI * TC], BF16))
            wbb = ph.enter_context(nc.sbuf_tensor("wbb_b", [128, TC], F32))
            wrow = ph.enter_context(nc.sbuf_tensor("wrow_b", [1, TC], F32))
            tmp5 = ph.enter_context(nc.sbuf_tensor("tmp5_b", [128, 2 * TC], F32))
            WgT = lambda kt, mt: wg[:, kt * I_FF + mt * 128: kt * I_FF + (mt + 1) * 128]
            WuT = lambda kt, mt: wu[:, kt * I_FF + mt * 128: kt * I_FF + (mt + 1) * 128]
            WdT = lambda kt, mt: wd[:, kt * D + mt * 128: kt * D + (mt + 1) * 128]
            LNS = lambda pair: lntc[:, (pair % 2) * NPT * TC:(pair % 2 + 1) * NPT * TC]
            TMP = lambda i8: tmp5[:, (i8 % 2) * TC:(i8 % 2 + 1) * TC]

            sGt, sSl, sUp, sGv, sDn, sEv, sBc, sWc = (p.sem(x) for x in
                ("b2Gt", "b2Sl", "b2Up", "b2Gv", "b2Dn", "b2Ev", "b2Bc", "b2Wc"))
            sWg, sWu, sWd = p.sem("b2Wg"), p.sem("b2Wu"), p.sem("b2Wd")
            sLn = [p.sem("b2Ln0"), p.sem("b2Ln1")]
            sWr = p.sem("b2Wr")
            sAc = [p.sem("b2Ac0"), p.sem("b2Ac1")]
            C.flush()
            tl = C.tail()
            NP2 = NTC * 8
            srcs_of = lambda pe: ((dram["sh_gate_T"], dram["sh_up_T"], dram["sh_down_T"])
                                  if pe == 0 else
                                  (dram["gate_w"][pe - 1], dram["up_w"][pe - 1],
                                   dram["down_w"][pe - 1]))

            @blk.gpsimd
            def _(g):
                if tl:
                    g.wait_ge(p.sem(tl[0]), tl[1])
                for pe in range(NE + 1):
                    srcs = srcs_of(pe)
                    if pe >= 1:
                        g.wait_ge(sGt, pe * NTC * 16)
                    g.dma_start(out=wg[:].rearrange("p (k n) -> p k n", n=I_FF),
                                in_=srcs[0].rearrange("(k p) n -> p k n", p=128)
                                ).then_inc(sWg, 16)
                    if pe >= 1:
                        g.wait_ge(sUp, pe * NTC * 16)
                    g.dma_start(out=wu[:].rearrange("p (k n) -> p k n", n=I_FF),
                                in_=srcs[1].rearrange("(k p) n -> p k n", p=128)
                                ).then_inc(sWu, 16)
                    if pe >= 1:
                        g.wait_ge(sDn, pe * NP2)
                        g.wait_ge(sAc[0], (pe * NP2 // 2) * 16)
                        g.wait_ge(sAc[1], (pe * NP2 // 2) * 16)
                    g.dma_start(out=wd[:].rearrange("p (k n) -> p k n", n=D),
                                in_=srcs[2].rearrange("(k p) n -> p k n", p=128)
                                ).then_inc(sWd, 16)
                    for tc in range(NTC):
                        for mtD in range(NPT):
                            i8 = (pe * NTC + tc) * 8 + mtD
                            g.wait_ge(sEv, i8 + 1)
                            g.dma_start(
                                out=d_xres[:, mtD * TP + tc * TC: mtD * TP + (tc + 1) * TC],
                                in_=TMP(i8), accum_op=ALU.add).then_inc(sAc[i8 % 2], 16)

            @blk.sync
            def _(sy):
                if tl:
                    sy.wait_ge(p.sem(tl[0]), tl[1])
                for pe in range(NE + 1):
                    for tc in range(NTC):
                        pair = pe * NTC + tc
                        if pair >= 2:
                            sy.wait_ge(sUp, (pair - 1) * 16)
                        sy.dma_start(out=LNS(pair).rearrange("p (k n) -> p k n", n=TC),
                                     in_=d_ln1[:, tc * TC:(tc + 1) * TC].rearrange(
                                         "(k p) n -> p k n", p=128)
                                     ).then_inc(sLn[pair % 2], 16)
                        if pe >= 1:
                            wb = (pe - 1) * NTC + tc
                            sy.wait_ge(sBc, wb)
                            sy.dma_start(out=wrow[:],
                                         in_=w8T[pe - 1:pe, tc * TC:(tc + 1) * TC]
                                         ).then_inc(sWr, 16)

            @blk.tensor
            def _(t):
                if tl:
                    t.wait_ge(p.sem(tl[0]), tl[1])
                for pe in range(NE + 1):
                    for tc in range(NTC):
                        pair = pe * NTC + tc
                        t.wait_ge(sLn[pair % 2], (pair // 2 + 1) * 16)
                        if pe >= 1:
                            wb = (pe - 1) * NTC + tc
                            t.wait_ge(sWr, (wb + 1) * 16)
                            if wb >= 1:
                                t.wait_ge(sWc, wb)
                            t.matmul(PSb(6, 0, TC), onesrf[:], wrow[0:1, :],
                                     start=True, stop=True,
                                     skip_group_check=True).then_inc(sBc, 1)
                        if tc == 0:
                            t.wait_ge(sWg, (pe + 1) * 16)
                            t.wait_ge(sWu, (pe + 1) * 16)
                        for mt in range(NKI):
                            i16 = pair * 16 + mt
                            if i16 >= 2:
                                t.wait_ge(sSl, i16 - 1)
                            for kt in range(NPT):
                                ins = t.matmul(PSb(mt % 2, 0, TC), WgT(kt, mt),
                                               LNS(pair)[:, kt * TC:(kt + 1) * TC],
                                               start=(kt == 0), stop=(kt == NPT - 1),
                                               skip_group_check=True)
                            ins.then_inc(sGt, 1)
                            if i16 >= 2:
                                t.wait_ge(sGv, i16 - 1)
                            for kt in range(NPT):
                                ins = t.matmul(PSb(2 + mt % 2, 0, TC), WuT(kt, mt),
                                               LNS(pair)[:, kt * TC:(kt + 1) * TC],
                                               start=(kt == 0), stop=(kt == NPT - 1),
                                               skip_group_check=True)
                            ins.then_inc(sUp, 1)
                        if tc == 0:
                            t.wait_ge(sWd, (pe + 1) * 16)
                        t.wait_ge(sGv, pair * 16 + 16)
                        for mtD in range(NPT):
                            i8 = pair * 8 + mtD
                            if i8 >= 2:
                                t.wait_ge(sEv, i8 - 1)
                            for kt in range(NKI):
                                ins = t.matmul(PSb(4 + mtD % 2, 0, TC), WdT(kt, mtD),
                                               gu[:, kt * TC:(kt + 1) * TC],
                                               start=(kt == 0), stop=(kt == NKI - 1),
                                               skip_group_check=True)
                            ins.then_inc(sDn, 1)

            @blk.scalar
            def _(s):
                for pe in range(NE + 1):
                    for tc in range(NTC):
                        pair = pe * NTC + tc
                        for mt in range(NKI):
                            i16 = pair * 16 + mt
                            s.wait_ge(sGt, i16 + 1)
                            if pair >= 1:
                                s.wait_ge(sDn, pair * 8)
                            s.activation(gu[:, mt * TC:(mt + 1) * TC],
                                         PSb(mt % 2, 0, TC), AF.Silu).then_inc(sSl, 1)

            @blk.vector
            def _(v):
                for pe in range(NE + 1):
                    for tc in range(NTC):
                        pair = pe * NTC + tc
                        if pe >= 1:
                            wb = (pe - 1) * NTC + tc
                            v.wait_ge(sBc, wb + 1)
                            v.wait_ge(sEv, pair * 8)
                            v.tensor_copy(wbb[:], PSb(6, 0, TC)).then_inc(sWc, 1)
                        for mt in range(NKI):
                            i16 = pair * 16 + mt
                            v.wait_ge(sUp, i16 + 1)
                            v.wait_ge(sSl, i16 + 1)
                            v.tensor_tensor(gu[:, mt * TC:(mt + 1) * TC],
                                            gu[:, mt * TC:(mt + 1) * TC],
                                            PSb(2 + mt % 2, 0, TC), ALU.mult).then_inc(sGv, 1)
                        for mtD in range(NPT):
                            i8 = pair * 8 + mtD
                            v.wait_ge(sDn, i8 + 1)
                            if i8 >= 2:
                                v.wait_ge(sAc[i8 % 2], ((i8 - 2) // 2 + 1) * 16)
                            if pe == 0:
                                v.tensor_copy(TMP(i8), PSb(4 + mtD % 2, 0, TC)
                                              ).then_inc(sEv, 1)
                            else:
                                v.tensor_tensor(TMP(i8), PSb(4 + mtD % 2, 0, TC), wbb[:],
                                                ALU.mult).then_inc(sEv, 1)

            def joinf(v):
                v.wait_ge(sAc[0], ((NE + 1) * NP2 // 2) * 16)
                v.wait_ge(sAc[1], ((NE + 1) * NP2 // 2) * 16)
                return v.tensor_copy(wbb[0:1, 0:1], wbb[0:1, 0:1])
            C.step("vector", joinf)
            C.flush()




I32 = mybir.dt.int32


def emit_b15(h):
    nc, p, blk, C, dram = h["nc"], h["p"], h["blk"], h["C"], h["dram"]
    PSb, sbuf, idf = h["PSb"], h["sbuf"], h["idf"]
    w8T = h["w8T"]
    h["fi1"] = fi1 = sbuf("fi1", [128, NTT], I32)
    h["fi2"] = fi2 = sbuf("fi2", [128, NTT], I32)
    h["w1T"] = w1T = sbuf("w1T", [128, NTT], F32)
    h["w2T"] = w2T = sbuf("w2T", [128, NTT], F32)
    with contextlib.ExitStack() as s0:
        sb = lambda nm, shp, dt=F32: s0.enter_context(nc.sbuf_tensor(nm, shp, dt))
        m8 = sb("m8_i", [8, TP]); pf8 = sb("pf8_i", [8, TP])
        slm = sb("slm_i", [8, TP]); cnt8 = sb("cnt8_i", [8, TP])
        r1 = sb("r1_i", [8, TP]); r2 = sb("r2_i", [8, TP])
        son = sb("son_i", [8, TP]); bigv = sb("bigv_i", [8, TP])
        ltm = sb("ltm_i", [8, 8]); ec8 = sb("ec8_i", [128, 8])
        t8 = sb("t8_i", [128, 48]); f1f = sb("f1f_i", [128, NTT])
        f2f = sb("f2f_i", [128, NTT])

        def v_prep(v):
            i = v.memset(son[:], 1.0)
            p.vg(v, i)
            i = v.tensor_scalar(m8[:], w8T[:], 0.0, None, ALU.is_gt)
            p.vg(v, i)
            i = v.tensor_tensor_scan(pf8[:], son[:], m8[:], 0.0, ALU.mult, ALU.add)
            p.vg(v, i)
            i = v.tensor_scalar(bigv[:], m8[:], -BIGF, BIGF, ALU.mult, ALU.add)
            p.vg(v, i)
            i = v.scalar_tensor_tensor(slm[:], pf8[:], -1.0, m8[:], ALU.add, ALU.mult)
            p.vg(v, i)
            i = v.tensor_tensor(slm[:], slm[:], bigv[:], ALU.add)
            p.vg(v, i)
            i = v.tensor_scalar(son[:], slm[:], float(CAP), BIGF, ALU.is_ge, ALU.mult)
            p.vg(v, i)
            return v.tensor_tensor(slm[:], slm[:], son[:], ALU.add)
        C.step("vector", v_prep)
        C.step("gpsimd", lambda g: _last(
            g.iota(ltm[:], pattern=[[-1, NE]], channel_multiplier=1,
                   allow_small_or_imprecise_dtypes=True),
            g.iota(ec8[:], pattern=[[CAP, NE]], channel_multiplier=0,
                   allow_small_or_imprecise_dtypes=True)))
        C.step("vector", lambda v: v.tensor_scalar(ltm[:], ltm[:], 0.0, None,
                                                   ALU.is_lt))
        for tcc in range(NTC):
            C.step("tensor", lambda t, tcc=tcc: t.matmul(
                PSb(0, 0, TC)[0:8, :], ltm[0:8, 0:8],
                m8[:, tcc * TC:(tcc + 1) * TC], start=True, stop=True))
            C.step("vector", lambda v, tcc=tcc: v.tensor_copy(
                cnt8[:, tcc * TC:(tcc + 1) * TC], PSb(0, 0, TC)[0:8, :]))

        def v_rank(v):
            i = v.tensor_scalar(son[:], cnt8[:], 0.0, None, ALU.is_equal)
            p.vg(v, i)
            i = v.tensor_tensor(r1[:], son[:], m8[:], ALU.mult)
            p.vg(v, i)
            i = v.tensor_scalar(son[:], cnt8[:], 1.0, None, ALU.is_equal)
            p.vg(v, i)
            return v.tensor_tensor(r2[:], son[:], m8[:], ALU.mult)
        C.step("vector", v_rank)

        for ch in range(NTT):
            def t_tr(t, ch=ch):
                for k, x in enumerate((slm, r1, r2, w8T)):
                    ins = t.transpose(PSb(1, k * 8, 8),
                                      x[:, ch * 128:(ch + 1) * 128],
                                      idf[0:8, 0:8])
                return ins
            C.step("tensor", t_tr)
            def v_col(v, ch=ch):
                stp, scp, r1t, r2t, w8t, acc = (t8[:, k * 8:(k + 1) * 8]
                                                for k in range(6))
                i = v.tensor_copy(stp, PSb(1, 0, 8))
                p.vg(v, i)
                i = v.tensor_tensor(scp, stp, ec8[:], ALU.add)
                i2 = v.tensor_copy(r1t, PSb(1, 8, 8))
                i3 = v.tensor_copy(r2t, PSb(1, 16, 8))
                i4 = v.tensor_copy(w8t, PSb(1, 24, 8))
                p.vg(v, i4)
                i = v.tensor_tensor(acc, scp, r1t, ALU.mult)
                p.vg(v, i)
                i = v.reduce_sum(f1f[:, ch:ch + 1], acc, axis=mybir.AxisListType.X)
                p.vg(v, i)
                i = v.tensor_tensor(acc, scp, r2t, ALU.mult)
                p.vg(v, i)
                i = v.reduce_sum(f2f[:, ch:ch + 1], acc, axis=mybir.AxisListType.X)
                p.vg(v, i)
                i = v.tensor_tensor(acc, w8t, r1t, ALU.mult)
                p.vg(v, i)
                i = v.reduce_sum(w1T[:, ch:ch + 1], acc, axis=mybir.AxisListType.X)
                p.vg(v, i)
                i = v.tensor_tensor(acc, w8t, r2t, ALU.mult)
                p.vg(v, i)
                return v.reduce_sum(w2T[:, ch:ch + 1], acc, axis=mybir.AxisListType.X)
            C.step("vector", v_col)

        def v_cast(v):
            i = v.tensor_scalar(f1f[:], f1f[:], float(DUMP), None, ALU.min)
            p.vg(v, i)
            i = v.tensor_scalar(f2f[:], f2f[:], float(DUMP), None, ALU.min)
            p.vg(v, i)
            i = v.tensor_copy(fi1[:], f1f[:])
            p.vg(v, i)
            return v.tensor_copy(fi2[:], f2f[:])
        C.step("vector", v_cast)
        C.flush()


def emit_b2_sparse(h):
    nc, p, blk, C, dram = h["nc"], h["p"], h["blk"], h["C"], h["dram"]
    PSb, sbuf, idf, idb, onesrf = (h[k] for k in
                                   ("PSb", "sbuf", "idf", "idb", "onesrf"))
    d_xres, d_ln1 = dram["xres"], dram["ln1_scr"]
    d_tm, d_xg, d_eo = dram["ln1_tm"], dram["xg_e"], dram["eout_e"]
    fi1, fi2, w1T, w2T = h["fi1"], h["fi2"], h["w1T"], h["w2T"]
    NKI = I_FF // 128
    PSf = h["PSf"]
    PSB16 = lambda bank, kt: PSf[:, bank * BANK + kt * 64:
                                 bank * BANK + kt * 64 + 64].bitcast(BF16)

    with contextlib.ExitStack() as ph:
        wg = ph.enter_context(nc.sbuf_tensor("wg_b", [128, NPT * I_FF], BF16))
        wu = ph.enter_context(nc.sbuf_tensor("wu_b", [128, NPT * I_FF], BF16))
        wd = ph.enter_context(nc.sbuf_tensor("wd_b", [128, NKI * D], BF16))
        lntc = ph.enter_context(nc.sbuf_tensor("lntc_b", [128, NPT * TC], BF16))
        gu = ph.enter_context(nc.sbuf_tensor("gu_b", [128, 2 * NKI * TC], BF16))
        xeT = ph.enter_context(nc.sbuf_tensor("xeT_b", [128, 2 * NPT * TC], BF16))
        xgl = ph.enter_context(nc.sbuf_tensor("xgl_b", [128, 2 * D], BF16))
        stg = ph.enter_context(nc.sbuf_tensor("stg_b", [128, D], F32))
        tmp5 = ph.enter_context(nc.sbuf_tensor("tmp5_b", [128, 2 * TC], F32))
        lntm = ph.enter_context(nc.sbuf_tensor("lntm_b", [128, 2 * D], BF16))
        WgT = lambda kt, mt: wg[:, kt * I_FF + mt * 128: kt * I_FF + (mt + 1) * 128]
        WuT = lambda kt, mt: wu[:, kt * I_FF + mt * 128: kt * I_FF + (mt + 1) * 128]
        WdT = lambda kt, mt: wd[:, kt * D + mt * 128: kt * D + (mt + 1) * 128]
        GU = lambda q: gu[:, (q % 2) * NKI * TC:(q % 2 + 1) * NKI * TC]
        XE = lambda cg: xeT[:, (cg % 2) * NPT * TC:(cg % 2 + 1) * NPT * TC]
        TMP = lambda i8: tmp5[:, (i8 % 2) * TC:(i8 % 2 + 1) * TC]

        hLn = [p.sem("shLn0"), p.sem("shLn1")]
        hGt, hSl, hUp, hGv, hDn, hEv = (p.sem(x) for x in
            ("shGt", "shSl", "shUp", "shGv", "shDn", "shEv"))
        hAc = [p.sem("shAc0"), p.sem("shAc1")]
        hW = p.sem("shW")
        sLt = [p.sem("dsLt0"), p.sem("dsLt1")]
        sDx = [p.sem("dsDx0"), p.sem("dsDx1")]
        rWg, rWu, rWd = p.sem("rtWg"), p.sem("rtWu"), p.sem("rtWd")
        rXg = [p.sem("rtXg0"), p.sem("rtXg1")]
        rTr, rTe = p.sem("rtTr"), p.sem("rtTe")
        rG, rSl, rU, rGv = p.sem("rtG"), p.sem("rtSl"), p.sem("rtU"), p.sem("rtGv")
        rDn, rDe = p.sem("rtDn"), p.sem("rtDe")
        rSt = [p.sem("rtSt0"), p.sem("rtSt1")]

        def v_z(v):
            return v.memset(stg[0:1, 0:D], 0.0)
        C.step("vector", v_z)
        C.step("sync", lambda sy: sy.dma_start(out=d_eo[DUMP:DUMP + 1, :],
                                               in_=stg[0:1, 0:D]), dma=True)
        C.flush()
        tl = C.tail()

        xg_wait = {}
        xgc = [0, 0]
        st_wait = {}
        stc = [0, 0]
        for pe in range(1, NE + 1):
            for cc in range(NCC):
                xgc[cc % 2] += 1
                xg_wait[(pe, cc)] = (cc % 2, xgc[cc % 2])
            for cc in range(NCC):
                stc[cc % 2] += 1
                st_wait[(pe, cc)] = (cc % 2, stc[cc % 2])
        xg_tot = tuple(xgc)
        st_tot = tuple(stc)
        SCT0 = 32 * ((NTT + 1) // 2)
        SCT1 = 32 * (NTT // 2)

        def gu_free_wait(e, q):
            if q < 2:
                return
            qq = q - 2
            if qq < NTC:
                e.wait_ge(hDn, (qq + 1) * NPT)
            else:
                e.wait_ge(rDn, (qq - NTC + 1) * TCK * 2)

        @blk.gpsimd
        def _(g):
            g.wait_ge(p.sem(tl[0]), tl[1])
            g.dma_start(out=wg[:].rearrange("p (k n) -> p k n", n=I_FF),
                        in_=dram["sh_gate_T"].rearrange("(k p) n -> p k n", p=128)
                        ).then_inc(hW, 16)
            g.dma_start(out=wu[:].rearrange("p (k n) -> p k n", n=I_FF),
                        in_=dram["sh_up_T"].rearrange("(k p) n -> p k n", p=128)
                        ).then_inc(hW, 16)
            g.dma_start(out=wd[:].rearrange("p (k n) -> p k n", n=D),
                        in_=dram["sh_down_T"].rearrange("(k p) n -> p k n", p=128)
                        ).then_inc(hW, 16)

        @blk.gpsimd
        def _(g):
            g.wait_ge(p.sem(tl[0]), tl[1])
            for ch in range(0 if "dist" in DBG_SKIP else NTT):
                g.wait_ge(sLt[ch % 2], (ch // 2 + 1) * 16)
                src = lntm[:, (ch % 2) * D:(ch % 2 + 1) * D]
                g.indirect_dma_start(
                    out=d_xg, out_offset=IndirectOffsetOnAxis(fi1[:, ch:ch + 1], 0),
                    in_=src, in_offset=None).then_inc(sDx[ch % 2], 16)
                g.indirect_dma_start(
                    out=d_xg, out_offset=IndirectOffsetOnAxis(fi2[:, ch:ch + 1], 0),
                    in_=src, in_offset=None).then_inc(sDx[ch % 2], 16)

        @blk.sync
        def _(sy):
            sy.wait_ge(p.sem(tl[0]), tl[1])
            for ch in range(NTT):
                if ch >= 2:
                    sy.wait_ge(sDx[ch % 2], (ch // 2) * 32)
                sy.dma_start(out=lntm[:, (ch % 2) * D:(ch % 2 + 1) * D],
                             in_=d_tm[ch * 128:(ch + 1) * 128, :]
                             ).then_inc(sLt[ch % 2], 16)
            for tc in range(NTC):
                if tc >= 1:
                    sy.wait_ge(hUp, tc * NKI)
                sy.dma_start(out=lntc[:].rearrange("p (k n) -> p k n", n=TC),
                             in_=d_ln1[:, tc * TC:(tc + 1) * TC].rearrange(
                                 "(k p) n -> p k n", p=128)
                             ).then_inc(hLn[tc % 2], 16)
            for pe in range(1, (0 if "routed" in DBG_SKIP else NE) + 1):
                def xg_load(sy, cc):
                    if pe == 1 and cc == 0:
                        sy.wait_ge(sDx[0], SCT0)
                        sy.wait_ge(sDx[1], SCT1)
                    tidx = (pe - 1) * NCC + cc
                    if tidx >= 2:
                        sy.wait_ge(rTr, tidx - 1)
                    sy.dma_start(out=xgl[:, (cc % 2) * D:(cc % 2 + 1) * D],
                                 in_=d_xg[(pe - 1) * CAP + cc * 128:
                                          (pe - 1) * CAP + (cc + 1) * 128, :]
                                 ).then_inc(rXg[cc % 2], 16)
                def eo_store(sy, cc):
                    de_idx = (pe - 1) * NCC + cc
                    sy.wait_ge(rDe, de_idx + 1)
                    sy.dma_start(out=d_eo[(pe - 1) * CAP + cc * 128:
                                          (pe - 1) * CAP + (cc + 1) * 128, :],
                                 in_=stg[:]).then_inc(rSt[cc % 2], 16)
                for cc in range(NCC):
                    xg_load(sy, cc)
                    if cc >= TCK:
                        eo_store(sy, cc - TCK)
                for cc in range(NCC - TCK, NCC):
                    eo_store(sy, cc)

        @blk.tensor
        def _(t):
            t.wait_ge(p.sem(tl[0]), tl[1])
            for tc in range(NTC):
                t.wait_ge(hLn[tc % 2], (tc // 2 + 1) * 16)
                if tc == 0:
                    t.wait_ge(hW, 48)
                for mt in range(NKI):
                    i16 = tc * NKI + mt
                    if i16 >= 2:
                        t.wait_ge(hSl, i16 - 1)
                    _last(*[t.matmul(PSb(mt % 2, 0, TC), WgT(kt, mt),
                                     lntc[:, kt * TC:(kt + 1) * TC],
                                     start=(kt == 0), stop=(kt == NPT - 1),
                                     skip_group_check=True)
                            for kt in range(NPT)]).then_inc(hGt, 1)
                    if i16 >= 2:
                        t.wait_ge(hGv, i16 - 1)
                    _last(*[t.matmul(PSb(2 + mt % 2, 0, TC), WuT(kt, mt),
                                     lntc[:, kt * TC:(kt + 1) * TC],
                                     start=(kt == 0), stop=(kt == NPT - 1),
                                     skip_group_check=True)
                            for kt in range(NPT)]).then_inc(hUp, 1)

                t.wait_ge(hGv, (tc + 1) * NKI)
                for mtD in range(NPT):
                    i8 = tc * NPT + mtD
                    if i8 >= 2:
                        t.wait_ge(hEv, i8 - 1)
                    _last(*[t.matmul(PSb(4 + mtD % 2, 0, TC), WdT(kt, mtD),
                                     GU(tc)[:, kt * TC:(kt + 1) * TC],
                                     start=(kt == 0), stop=(kt == NKI - 1),
                                     skip_group_check=True)
                            for kt in range(NKI)]).then_inc(hDn, 1)
            for pe in range(1, (0 if "routed" in DBG_SKIP else NE) + 1):
                for ch in range(NCE):
                    cg = (pe - 1) * NCE + ch
                    for cc in range(ch * TCK, (ch + 1) * TCK):
                        tidx = (pe - 1) * NCC + cc
                        par, cnt = xg_wait[(pe, cc)]
                        t.wait_ge(rXg[par], cnt * 16)
                        if tidx >= 2:
                            t.wait_ge(rTe, tidx - 1)
                        if cg >= 2 and cc % TCK == 0:
                            t.wait_ge(rU, (cg - 1) * NKI)
                        _last(*[t.transpose(PSB16(6 + cc % 2, kt),
                                            xgl[:, (cc % 2) * D + kt * 128:
                                                (cc % 2) * D + (kt + 1) * 128],
                                            idb[:])
                                for kt in range(NPT)]).then_inc(rTr, 1)
                    if ch == 0:
                        t.wait_ge(rWg, pe * 16)
                        t.wait_ge(rWu, pe * 16)
                    t.wait_ge(rTe, (pe - 1) * NCC + (ch + 1) * TCK)
                    for mt in range(NKI):
                        gidx = cg * NKI + mt
                        if gidx >= 2:
                            t.wait_ge(rSl, gidx - 1)
                        elif pe == 1:
                            t.wait_ge(hSl, NTC * NKI)
                            t.wait_ge(hGv, NTC * NKI)
                        _last(*[t.matmul(PSb(mt % 2, 0, TC), WgT(kt, mt),
                                         XE(cg)[:, kt * TC:(kt + 1) * TC],
                                         start=(kt == 0), stop=(kt == NPT - 1),
                                         skip_group_check=True)
                                for kt in range(NPT)]).then_inc(rG, 1)
                        if gidx >= 2:
                            t.wait_ge(rGv, gidx - 1)
                        _last(*[t.matmul(PSb(2 + mt % 2, 0, TC), WuT(kt, mt),
                                         XE(cg)[:, kt * TC:(kt + 1) * TC],
                                         start=(kt == 0), stop=(kt == NPT - 1),
                                         skip_group_check=True)
                                for kt in range(NPT)]).then_inc(rU, 1)
                    if ch == 0:
                        t.wait_ge(rWd, pe * 16)
                    t.wait_ge(rGv, (cg + 1) * NKI)
                    q = NTC + cg
                    for cc in range(ch * TCK, (ch + 1) * TCK):
                        de_idx = (pe - 1) * NCC + cc
                        if de_idx >= 1:
                            t.wait_ge(rDe, de_idx)
                        if pe == 1 and cc == 0:
                            t.wait_ge(hEv, NTC * NPT)
                        for nh in range(2):
                            _last(*[t.matmul(
                                PSb(4 + nh, 0, 512),
                                GU(q)[:, kt * TC + (cc % TCK) * 128:
                                      kt * TC + (cc % TCK) * 128 + 128],
                                wd[:, kt * D + nh * 512: kt * D + (nh + 1) * 512],
                                start=(kt == 0), stop=(kt == NKI - 1),
                                skip_group_check=True)
                                for kt in range(NKI)]).then_inc(rDn, 1)

        @blk.scalar
        def _(s):
            for tc in range(NTC):
                for mt in range(NKI):
                    i16 = tc * NKI + mt
                    s.wait_ge(hGt, i16 + 1)
                    if mt == 0:
                        gu_free_wait(s, tc)
                    s.activation(GU(tc)[:, mt * TC:(mt + 1) * TC],
                                 PSb(mt % 2, 0, TC), AF.Silu).then_inc(hSl, 1)
            for pe in range(1, (0 if "routed" in DBG_SKIP else NE) + 1):
                for ch in range(NCE):
                    cg = (pe - 1) * NCE + ch
                    q = NTC + cg
                    for mt in range(NKI):
                        gidx = cg * NKI + mt
                        s.wait_ge(rG, gidx + 1)
                        if mt == 0:
                            gu_free_wait(s, q)
                        s.activation(GU(q)[:, mt * TC:(mt + 1) * TC],
                                     PSb(mt % 2, 0, TC), AF.Silu).then_inc(rSl, 1)

        def _dn_evac(v, pe, cc):
            de_idx = (pe - 1) * NCC + cc
            v.wait_ge(rDn, de_idx * 2 + 2)
            if de_idx >= 1:
                pr = st_wait[(pe, cc - 1)] if cc >= 1 else st_wait[(pe - 1, NCC - 1)]
                v.wait_ge(rSt[pr[0]], pr[1] * 16)
            i = v.tensor_copy(stg[:, 0:512], PSb(4, 0, 512))
            p.vg(v, i)
            v.tensor_copy(stg[:, 512:1024], PSb(5, 0, 512)).then_inc(rDe, 1)

        @blk.vector
        def _(v):
            v.wait_ge(p.sem(tl[0]), tl[1])
            for tc in range(NTC):
                for mt in range(NKI):
                    i16 = tc * NKI + mt
                    v.wait_ge(hUp, i16 + 1)
                    v.wait_ge(hSl, i16 + 1)
                    v.tensor_tensor(GU(tc)[:, mt * TC:(mt + 1) * TC],
                                    GU(tc)[:, mt * TC:(mt + 1) * TC],
                                    PSb(2 + mt % 2, 0, TC), ALU.mult).then_inc(hGv, 1)
                for mtD in range(NPT):
                    i8 = tc * NPT + mtD
                    v.wait_ge(hDn, i8 + 1)
                    if i8 >= 2:
                        v.wait_ge(hAc[i8 % 2], ((i8 - 2) // 2 + 1) * 16)
                    v.tensor_copy(TMP(i8), PSb(4 + mtD % 2, 0, TC)).then_inc(hEv, 1)
            for pe in range(1, (0 if "routed" in DBG_SKIP else NE) + 1):
                for ch in range(NCE):
                    cg = (pe - 1) * NCE + ch
                    for cc in range(ch * TCK, (ch + 1) * TCK):
                        tidx = (pe - 1) * NCC + cc
                        v.wait_ge(rTr, tidx + 1)
                        _last(*[v.tensor_copy(
                            XE(cg)[:, kt * TC + (cc % TCK) * 128:
                                   kt * TC + (cc % TCK) * 128 + 128],
                            PSB16(6 + cc % 2, kt))
                            for kt in range(NPT)]).then_inc(rTe, 1)
                    q = NTC + cg
                    for mt in range(NKI):
                        gidx = cg * NKI + mt
                        v.wait_ge(rU, gidx + 1)
                        v.wait_ge(rSl, gidx + 1)
                        v.tensor_tensor(GU(q)[:, mt * TC:(mt + 1) * TC],
                                        GU(q)[:, mt * TC:(mt + 1) * TC],
                                        PSb(2 + mt % 2, 0, TC), ALU.mult
                                        ).then_inc(rGv, 1)
                    for cc in range(ch * TCK, (ch + 1) * TCK):
                        _dn_evac(v, pe, cc)

        @blk.gpsimd
        def _(g):
            for tc in range(NTC):
                for mtD in range(NPT):
                    i8 = tc * NPT + mtD
                    g.wait_ge(hEv, i8 + 1)
                    g.dma_start(
                        out=d_xres[:, mtD * TP + tc * TC: mtD * TP + (tc + 1) * TC],
                        in_=TMP(i8), accum_op=ALU.add).then_inc(hAc[i8 % 2], 16)

        @blk.gpsimd
        def _(g):
            for pe in range(1, (0 if "routed" in DBG_SKIP else NE) + 1):
                if pe == 1:
                    g.wait_ge(hGt, NTC * NKI)
                else:
                    g.wait_ge(rG, (pe - 1) * NCE * NKI)
                g.dma_start(out=wg[:].rearrange("p (k n) -> p k n", n=I_FF),
                            in_=dram["gate_w"][pe - 1].rearrange(
                                "(k p) n -> p k n", p=128)).then_inc(rWg, 16)
                if pe == 1:
                    g.wait_ge(hUp, NTC * NKI)
                else:
                    g.wait_ge(rU, (pe - 1) * NCE * NKI)
                g.dma_start(out=wu[:].rearrange("p (k n) -> p k n", n=I_FF),
                            in_=dram["up_w"][pe - 1].rearrange(
                                "(k p) n -> p k n", p=128)).then_inc(rWu, 16)
                if pe == 1:
                    g.wait_ge(hDn, NTC * NPT)
                else:
                    g.wait_ge(rDn, (pe - 1) * NCC * 2)
                g.dma_start(out=wd[:].rearrange("p (k n) -> p k n", n=D),
                            in_=dram["down_w"][pe - 1].rearrange(
                                "(k p) n -> p k n", p=128)).then_inc(rWd, 16)

        def b2join(v):
            v.wait_ge(hAc[0], (NTC * NPT // 2) * 16)
            v.wait_ge(hAc[1], (NTC * NPT // 2) * 16)
            if "routed" not in DBG_SKIP:
                v.wait_ge(rSt[0], st_tot[0] * 16)
                v.wait_ge(rSt[1], st_tot[1] * 16)
            elif "dist" not in DBG_SKIP:
                v.wait_ge(sDx[0], SCT0)
                v.wait_ge(sDx[1], SCT1)
            return v.tensor_copy(idf[0:1, 0:1], idf[0:1, 0:1])
        C.step("vector", b2join)
        C.flush()

    if "combine" in DBG_SKIP:
        return
    with contextlib.ExitStack() as cs:
        o1 = cs.enter_context(nc.sbuf_tensor("o1_c", [128, 2 * D], F32))
        o2 = cs.enter_context(nc.sbuf_tensor("o2_c", [128, 2 * D], F32))
        rT = cs.enter_context(nc.sbuf_tensor("rT_c", [128, 2 * D], F32))
        acb = cs.enter_context(nc.sbuf_tensor("acb_c", [128, 2 * D], F32))
        cGa = [p.sem("cbGa0"), p.sem("cbGa1")]
        cSv, cTt, cEe = (p.sem(x) for x in ("cbSv", "cbTt", "cbEe"))
        cAa = [p.sem("cbAa0"), p.sem("cbAa1")]
        C.flush()
        tl2 = C.tail()

        @blk.gpsimd
        def _(g):
            g.wait_ge(p.sem(tl2[0]), tl2[1])
            for ch in range(NTT):
                if ch >= 2:
                    g.wait_ge(cSv, ch - 1)
                g.indirect_dma_start(
                    out=o1[:, (ch % 2) * D:(ch % 2 + 1) * D], out_offset=None,
                    in_=d_eo, in_offset=IndirectOffsetOnAxis(fi1[:, ch:ch + 1], 0)
                    ).then_inc(cGa[ch % 2], 16)
                g.indirect_dma_start(
                    out=o2[:, (ch % 2) * D:(ch % 2 + 1) * D], out_offset=None,
                    in_=d_eo, in_offset=IndirectOffsetOnAxis(fi2[:, ch:ch + 1], 0)
                    ).then_inc(cGa[ch % 2], 16)
                if ch >= 1:
                    g.wait_ge(cEe, ch)
                    for pt in range(NPT):
                        g.dma_start(
                            out=d_xres[:, pt * TP + (ch - 1) * 128:
                                       pt * TP + ch * 128],
                            in_=acb[:, ((ch - 1) % 2) * D + pt * 128:
                                    ((ch - 1) % 2) * D + (pt + 1) * 128],
                            accum_op=ALU.add).then_inc(cAa[(ch - 1) % 2], 16)
            g.wait_ge(cEe, NTT)
            for pt in range(NPT):
                g.dma_start(
                    out=d_xres[:, pt * TP + (NTT - 1) * 128: pt * TP + NTT * 128],
                    in_=acb[:, ((NTT - 1) % 2) * D + pt * 128:
                            ((NTT - 1) % 2) * D + (pt + 1) * 128],
                    accum_op=ALU.add).then_inc(cAa[(NTT - 1) % 2], 16)

        @blk.vector
        def _(v):
            v.wait_ge(p.sem(tl2[0]), tl2[1])
            for ch in range(NTT):
                v.wait_ge(cGa[ch % 2], (ch // 2 + 1) * 32)
                sl = (ch % 2) * D
                i = v.tensor_scalar(rT[:, sl:sl + D], o1[:, sl:sl + D],
                                    w1T[:, ch:ch + 1], None, ALU.mult)
                p.vg(v, i)
                v.scalar_tensor_tensor(rT[:, sl:sl + D], o2[:, sl:sl + D],
                                       w2T[:, ch:ch + 1], rT[:, sl:sl + D],
                                       ALU.mult, ALU.add).then_inc(cSv, 1)
                v.wait_ge(cTt, ch + 1)
                if ch >= 2:
                    nacc = ((ch - 2) // 2 + 1) * NPT
                    v.wait_ge(cAa[ch % 2], nacc * 16)
                _last(*[v.tensor_copy(acb[:, sl + pt * 128: sl + (pt + 1) * 128],
                                      PSb(pt // 4, (pt % 4) * 128, 128))
                        for pt in range(NPT)]).then_inc(cEe, 1)

        @blk.tensor
        def _(t):
            t.wait_ge(p.sem(tl2[0]), tl2[1])
            for ch in range(NTT):
                t.wait_ge(cSv, ch + 1)
                if ch >= 1:
                    t.wait_ge(cEe, ch)
                sl = (ch % 2) * D
                _last(*[t.transpose(PSb(pt // 4, (pt % 4) * 128, 128),
                                    rT[:, sl + pt * 128: sl + (pt + 1) * 128],
                                    idf[:])
                        for pt in range(NPT)]).then_inc(cTt, 1)

        def cjoin(v):
            v.wait_ge(cAa[0], (((NTT + 1) // 2) * NPT) * 16)
            v.wait_ge(cAa[1], ((NTT // 2) * NPT) * 16)
            return v.tensor_copy(idf[0:1, 0:1], idf[0:1, 0:1])
        C.step("vector", cjoin)
        C.flush()


def _emit_shift_mix(h, C, ln_sb, mix_pp_list, out_list, dtmp, tc):
    p = h["p"]
    def v_mix(v):
        for pt in range(NPT):
            base = pt * (TP + 1)
            cur = ln_sb[:, base + 1 + tc * TC: base + 1 + (tc + 1) * TC]
            prv = ln_sb[:, base + tc * TC: base + (tc + 1) * TC]
            i0 = v.tensor_tensor(dtmp[:], cur, prv, ALU.subtract)
            p.vg(v, i0)
            for mix_pp, out in zip(mix_pp_list, out_list):
                i1 = v.scalar_tensor_tensor(
                    out[:, pt * TC:(pt + 1) * TC], dtmp[:], mix_pp[:, pt:pt + 1], prv,
                    ALU.mult, ALU.add)
            if pt < NPT - 1:
                p.vg(v, i1)
        return i1
    C.step("vector", v_mix)


def _emit_ln_masked(h, C, d_xres, ln_sb, name):
    nc, p, blk = h["nc"], h["p"], h["blk"]
    PSb = h["PSb"]
    RTMP, m0, r0, epsA = h["RTMP"], h["m0"], h["r0"], h["epsA"]
    ln_stats_rows, bcast_row = h["ln_stats_rows"], h["bcast_row"]
    maskb = h["maskb"]
    with contextlib.ExitStack() as s0:
        xT = s0.enter_context(nc.sbuf_tensor("xT_" + name, [128, NPT * TP], F32))
        XT = lambda pt, tc: xT[:, pt * TP + tc * TC: pt * TP + (tc + 1) * TC]
        sq = s0.enter_context(nc.sbuf_tensor("sq_" + name, [128, 2 * TC], F32))
        C.step("sync", lambda sy: sy.dma_start(out=xT[:], in_=d_xres), dma=True)
        C.step("vector", lambda v: _last(*[
            v.memset(ln_sb[:, pt * (TP + 1): pt * (TP + 1) + 1], 0.0)
            for pt in range(NPT)]))
        for tc in range(NTC):
            ln_stats_rows(C, lambda pt, tc=tc: XT(pt, tc), NPT, TC, sq,
                          m0[:, tc * TC:(tc + 1) * TC], r0[:, tc * TC:(tc + 1) * TC],
                          RTMP[:, 0:TC], epsA, D)
            bcast_row(C, m0[:, tc * TC:(tc + 1) * TC], TC, 6)
            bcast_row(C, r0[:, tc * TC:(tc + 1) * TC], TC, 7)
            def v_lnm(v, tc=tc):
                for pt in range(NPT):
                    dst = ln_sb[:, pt * (TP + 1) + 1 + tc * TC: pt * (TP + 1) + 1 + (tc + 1) * TC]
                    i0 = v.tensor_tensor(sq[:, 0:TC], XT(pt, tc), PSb(6, 0, TC), ALU.subtract)
                    p.vg(v, i0)
                    i1 = v.tensor_tensor(sq[:, 0:TC], sq[:, 0:TC], PSb(7, 0, TC), ALU.mult)
                    p.vg(v, i1)
                    i2 = v.tensor_tensor(dst, sq[:, 0:TC], maskb[:, tc * TC:(tc + 1) * TC],
                                         ALU.mult)
                    if pt < NPT - 1:
                        p.vg(v, i2)
                return i2
            C.step("vector", v_lnm)
            C.flush()


def emit_stage_c(h):
    nc, p, blk, C, dram = h["nc"], h["p"], h["blk"], h["C"], h["dram"]
    PSb, sbuf = h["PSb"], h["sbuf"]
    d_xres = dram["xres"]
    d_kf, d_vf, d_sr, d_rw = (dram[k] for k in ("kf_scr", "vf_scr", "sr_scr", "rw_scr"))
    Wt = lambda w, kt, mt: w[:, kt * D + mt * 128: kt * D + (mt + 1) * 128]

    if "maskb" not in h:
        h["maskb"] = sbuf("maskb", [128, TP], F32)
        C.step("sync", lambda sy: sy.dma_start(out=h["maskb"][:], in_=dram["mask"]),
               dma=True)
    maskb = h["maskb"]
    vecsC = sbuf("vecsC", [128, 48], F32)
    for i, nm in enumerate(("l1_w", "l1_u", "l1_mk", "l1_mv", "l1_mr")):
        C.step("sync", lambda sy, i=i, nm=nm: sy.dma_start(
            out=vecsC[:, i * 8:(i + 1) * 8],
            in_=dram[nm].rearrange("(t p) -> p t", p=128),
            allow_slow_non_contiguous=True), dma=True)
    WV, UV, MK, MV, MR = (vecsC[:, i * 8:(i + 1) * 8] for i in range(5))
    DCOL = vecsC[:, 40:48]
    EU = UV

    ln2 = sbuf("ln2_sb", [128, NPT * (TP + 1)], BF16)
    _emit_ln_masked(h, C, d_xres, ln2, "c")

    def s_wexp(s):
        s.activation(DCOL[:, :], WV[:, :], AF.Exp)
        return s.activation(vecsC[:, 8:16], UV[:, :], AF.Exp)
    C.step("scalar", s_wexp)
    C.step("vector", lambda v: v.tensor_scalar_mul(DCOL[:, :], DCOL[:, :], -1.0))
    C.step("scalar", lambda s: s.activation(DCOL[:, :], DCOL[:, :], AF.Exp))

    with contextlib.ExitStack() as s1:
        wk = s1.enter_context(nc.sbuf_tensor("wkC", [128, NPT * D], BF16))
        wv = s1.enter_context(nc.sbuf_tensor("wvC", [128, NPT * D], BF16))
        wr = s1.enter_context(nc.sbuf_tensor("wrC", [128, NPT * D], BF16))
        xk = s1.enter_context(nc.sbuf_tensor("xkC", [128, NPT * TC], BF16))
        xv = s1.enter_context(nc.sbuf_tensor("xvC", [128, NPT * TC], BF16))
        xr = s1.enter_context(nc.sbuf_tensor("xrC", [128, NPT * TC], BF16))
        dtmp = s1.enter_context(nc.sbuf_tensor("dtmpC", [128, TC], F32))
        kstK = s1.enter_context(nc.sbuf_tensor("kstKC", [128, TC], F32))
        kstV = s1.enter_context(nc.sbuf_tensor("kstVC", [128, TC], F32))
        srt = s1.enter_context(nc.sbuf_tensor("srtC", [128, TC], BF16))
        for dst, nm in ((wk, "Wk_T"), (wv, "Wv_T"), (wr, "Wr_T")):
            C.step("gpsimd", lambda g, dst=dst, nm=nm: g.dma_start(
                out=dst[:].rearrange("p (k n) -> p k n", n=D),
                in_=dram[nm].rearrange("(k p) n -> p k n", p=128)), dma=True)
        for tc in range(NTC):
            _emit_shift_mix(h, C, ln2, [MK, MV, MR], [xk, xv, xr], dtmp, tc)
            items = []
            for mt in range(NPT):
                for wgt, src, dst_hbm in ((wk, xk, d_kf), (wv, xv, d_vf)):
                    buf = kstK if wgt is wk else kstV
                    items.append({
                        "mm": lambda t, bk, wgt=wgt, src=src, mt=mt: _last(*[
                            t.matmul(PSb(bk, 0, TC), Wt(wgt, kt, mt),
                                     src[:, kt * TC:(kt + 1) * TC],
                                     start=(kt == 0), stop=(kt == NPT - 1))
                            for kt in range(NPT)]),
                        "evac": lambda e, bk, buf=buf: e.tensor_copy(
                            buf[:], PSb(bk, 0, TC)),
                        "store": lambda sy, buf=buf, mt=mt, tc=tc, dh=dst_hbm:
                            sy.dma_start(
                                out=dh[mt * 128:(mt + 1) * 128, tc * TC:(tc + 1) * TC],
                                in_=buf[:]),
                        "st_class": "k" if wgt is wk else "v",
                        "buf_lag": 3})
                items.append({
                    "mm": lambda t, bk, mt=mt: _last(*[
                        t.matmul(PSb(bk, 0, TC), Wt(wr, kt, mt),
                                 xr[:, kt * TC:(kt + 1) * TC],
                                 start=(kt == 0), stop=(kt == NPT - 1))
                        for kt in range(NPT)]),
                    "evac": lambda e, bk: e.activation(
                        srt[:], PSb(bk, 0, TC), AF.Sigmoid),
                    "evac_eng": "scalar",
                    "store": lambda sy, mt=mt, tc=tc: sy.dma_start(
                        out=d_sr[mt * 128:(mt + 1) * 128, tc * TC:(tc + 1) * TC],
                        in_=srt[:]),
                    "st_class": "r",
                    "buf_lag": 3})
            pipelined_mm(h, "pC1", items, nbank=4, bank0=0)

    sRW = contextlib.ExitStack()
    rwt = sRW.enter_context(nc.sbuf_tensor("rwtC", [128, NPT * TP], BF16))
    with contextlib.ExitStack() as s2:
        kf = s2.enter_context(nc.sbuf_tensor("kfC", [128, 2 * TP], F32))
        vf = s2.enter_context(nc.sbuf_tensor("vfC", [128, TP], F32))
        af = s2.enter_context(nc.sbuf_tensor("afC", [128, TP], F32))
        avf = s2.enter_context(nc.sbuf_tensor("avC", [128, TP], F32))
        Pf = s2.enter_context(nc.sbuf_tensor("PfC", [128, TP + 1], F32))
        Qf = s2.enter_context(nc.sbuf_tensor("QfC", [128, TP + 1], F32))
        dtl = s2.enter_context(nc.sbuf_tensor("dtlC", [128, TP], F32))
        srf = s2.enter_context(nc.sbuf_tensor("srfC", [128, 2 * TP], BF16))
        SL = lambda buf, pt: buf[:, (pt % 2) * TP:(pt % 2 + 1) * TP]
        cLD = [p.sem("c2L0"), p.sem("c2L1")]
        cEX, cSC = p.sem("c2X"), p.sem("c2S")
        cST = [p.sem("c2T0"), p.sem("c2T1")]
        C.flush()
        tl2 = C.tail()

        @blk.sync
        def _(sy):
            sy.wait_ge(p.sem(tl2[0]), tl2[1])
            for pt in range(NPT):
                if pt >= 2:
                    sy.wait_ge(cSC, pt - 1)
                sy.dma_start(out=SL(kf, pt),
                             in_=d_kf[pt * 128:(pt + 1) * 128, :]).then_inc(cLD[pt % 2], 16)
                if pt >= 1:
                    sy.wait_ge(cSC, pt)
                sy.dma_start(out=vf[:],
                             in_=d_vf[pt * 128:(pt + 1) * 128, :]).then_inc(cLD[pt % 2], 16)
                sy.dma_start(out=SL(srf, pt),
                             in_=d_sr[pt * 128:(pt + 1) * 128, :]).then_inc(cLD[pt % 2], 16)


        @blk.scalar
        def _(s):
            s.wait_ge(p.sem(tl2[0]), tl2[1])
            for pt in range(NPT):
                s.wait_ge(cLD[pt % 2], (pt // 2 + 1) * 48)
                if pt >= 1:
                    s.wait_ge(cSC, pt)
                s.activation(af[:], SL(kf, pt), AF.Exp).then_inc(cEX, 1)

        @blk.vector
        def _(v):
            v.wait_ge(p.sem(tl2[0]), tl2[1])
            for pt in range(NPT):
                i0 = v.memset(dtl[:], 1.0)
                p.vg(v, i0)
                i1 = v.tensor_scalar_mul(dtl[:], dtl[:], DCOL[:, pt:pt + 1])
                v.memset(Pf[:, 0:1], 0.0)
                v.memset(Qf[:, 0:1], 0.0)
                v.wait_ge(cEX, pt + 1)
                afp, kfp, vfp = af[:], SL(kf, pt), vf[:]
                i0 = v.tensor_tensor(afp, afp, maskb[:], ALU.mult)
                p.vg(v, i0)
                i1 = v.tensor_tensor(avf[:], afp, vfp, ALU.mult)
                p.vg(v, i1)
                i2 = v.tensor_tensor_scan(Pf[:, 1:TP + 1], dtl[:], avf[:], 0.0,
                                          ALU.mult, ALU.add)
                p.vg(v, i2)
                i3 = v.tensor_tensor_scan(Qf[:, 1:TP + 1], dtl[:], afp, 0.0,
                                          ALU.mult, ALU.add)
                p.vg(v, i3)
                i4 = v.scalar_tensor_tensor(avf[:], avf[:], vecsC[:, 8 + pt:9 + pt],
                                            Pf[:, 0:TP], ALU.mult, ALU.add)
                p.vg(v, i4)
                i5 = v.scalar_tensor_tensor(afp, afp, vecsC[:, 8 + pt:9 + pt],
                                            Qf[:, 0:TP], ALU.mult, ALU.add)
                p.vg(v, i5)
                i6 = v.tensor_scalar_max(afp, afp, 1e-30)
                p.vg(v, i6)
                i7 = v.reciprocal(kfp, afp)
                p.vg(v, i7)
                i8 = v.tensor_tensor(avf[:], avf[:], kfp, ALU.mult)
                p.vg(v, i8)
                v.tensor_tensor(rwt[:, pt * TP:(pt + 1) * TP], avf[:],
                                SL(srf, pt), ALU.mult).then_inc(cSC, 1)

        def c2_join(v):
            v.wait_ge(cSC, NPT)
            return v.tensor_copy(h["idf"][0:1, 0:1], h["idf"][0:1, 0:1])
        C.step("vector", c2_join)

    with contextlib.ExitStack() as s3:
        wo = s3.enter_context(nc.sbuf_tensor("woC", [128, NPT * D], BF16))
        tmp6 = s3.enter_context(nc.sbuf_tensor("tmp6C", [128, 2 * TC], F32))
        C.step("gpsimd", lambda g: g.dma_start(
            out=wo[:].rearrange("p (k n) -> p k n", n=D),
            in_=dram["Wo_T"].rearrange("(k p) n -> p k n", p=128)), dma=True)
        cM, cE = p.sem("c3M"), p.sem("c3E")
        cA = [p.sem("c3A0"), p.sem("c3A1")]
        cL = [p.sem("c3L0"), p.sem("c3L1")]
        C.flush()
        tl = C.tail()

        @blk.tensor
        def _(t):
            t.wait_ge(p.sem(tl[0]), tl[1])
            for tc in range(NTC):
                for mt in range(NPT):
                    i = tc * NPT + mt
                    if i >= 4:
                        t.wait_ge(cE, i - 3)
                    _last(*[t.matmul(
                        PSb(i % 4, 0, TC), Wt(wo, kt, mt),
                        rwt[:, kt * TP + tc * TC: kt * TP + (tc + 1) * TC],
                        start=(kt == 0), stop=(kt == NPT - 1))
                        for kt in range(NPT)]).then_inc(cM, 1)

        @blk.vector
        def _(v):
            v.wait_ge(p.sem(tl[0]), tl[1])
            for i in range(NTC * NPT):
                v.wait_ge(cM, i + 1)
                if i >= 2:
                    v.wait_ge(cA[i % 2], (i // 2) * 16)
                v.tensor_copy(tmp6[:, (i % 2) * TC:(i % 2 + 1) * TC],
                              PSb(i % 4, 0, TC)).then_inc(cE, 1)

        @blk.gpsimd
        def _(g):
            g.wait_ge(p.sem(tl[0]), tl[1])
            for tc in range(NTC):
                for mt in range(NPT):
                    i = tc * NPT + mt
                    g.wait_ge(cE, i + 1)
                    g.dma_start(
                        out=d_xres[:, mt * TP + tc * TC: mt * TP + (tc + 1) * TC],
                        in_=tmp6[:, (i % 2) * TC:(i % 2 + 1) * TC],
                        accum_op=ALU.add).then_inc(cA[i % 2], 16)

        def c3_join(v):
            v.wait_ge(cA[0], (NTC * NPT // 2) * 16)
            v.wait_ge(cA[1], (NTC * NPT // 2) * 16)
            return v.tensor_copy(h["idf"][0:1, 0:1], h["idf"][0:1, 0:1])
        C.step("vector", c3_join)
    sRW.close()


def emit_stage_d(h):
    nc, p, blk, C, dram = h["nc"], h["p"], h["blk"], h["C"], h["dram"]
    PSb, sbuf = h["PSb"], h["sbuf"]
    d_xres = dram["xres"]
    Wt = lambda w, kt, mt: w[:, kt * D + mt * 128: kt * D + (mt + 1) * 128]

    if "maskb" not in h:
        h["maskb"] = sbuf("maskb", [128, TP], F32)
        C.step("sync", lambda sy: sy.dma_start(out=h["maskb"][:], in_=dram["mask"]),
               dma=True)
    vecsD = sbuf("vecsD", [128, 16], F32)
    for i, nm in enumerate(("f_mur", "f_muk")):
        C.step("sync", lambda sy, i=i, nm=nm: sy.dma_start(
            out=vecsD[:, i * 8:(i + 1) * 8],
            in_=dram[nm].rearrange("(t p) -> p t", p=128),
            allow_slow_non_contiguous=True), dma=True)
    MUR, MUK = vecsD[:, 0:8], vecsD[:, 8:16]

    ln3 = sbuf("ln3_sb", [128, NPT * (TP + 1)], BF16)
    _emit_ln_masked(h, C, d_xres, ln3, "d")

    with contextlib.ExitStack() as s1:
        wrD = s1.enter_context(nc.sbuf_tensor("wrD", [128, NPT * D], BF16))
        wkD = s1.enter_context(nc.sbuf_tensor("wkD", [128, NPT * D], BF16))
        wvD = s1.enter_context(nc.sbuf_tensor("wvD", [128, NPT * D], BF16))
        xrD = s1.enter_context(nc.sbuf_tensor("xrD", [128, NPT * TC], BF16))
        xkD = s1.enter_context(nc.sbuf_tensor("xkD", [128, NPT * TC], BF16))
        dtmp = s1.enter_context(nc.sbuf_tensor("dtmpD", [128, TC], F32))
        srD = s1.enter_context(nc.sbuf_tensor("srD", [128, NPT * TC], BF16))
        kaD = s1.enter_context(nc.sbuf_tensor("kaD", [128, NPT * TC], BF16))
        tmpD = s1.enter_context(nc.sbuf_tensor("tmpD", [128, 2 * TC], F32))
        accD = s1.enter_context(nc.sbuf_tensor("accD", [128, TC], F32))
        for dst, nm in ((wrD, "f_Wr_T"), (wkD, "f_Wk_T"), (wvD, "f_Wv_T")):
            C.step("gpsimd", lambda g, dst=dst, nm=nm: g.dma_start(
                out=dst[:].rearrange("p (k n) -> p k n", n=D),
                in_=dram[nm].rearrange("(k p) n -> p k n", p=128)), dma=True)
        npmm = h.setdefault("_pmm", {}).setdefault(
            "pD1", {"M": 0, "Evector": 0, "Escalar": 0, "Egpsimd": 0,
                    "S0": 0, "S1": 0})
        for tc in range(NTC):
            _emit_shift_mix(h, C, ln3, [MUR, MUK], [xrD, xkD], dtmp, tc)
            base_v, base_s = npmm["Evector"], npmm["Escalar"]
            items = []
            for mt in range(NPT):
                items.append({
                    "mm": lambda t, bk, mt=mt: _last(*[
                        t.matmul(PSb(bk, 0, TC), Wt(wrD, kt, mt),
                                 xrD[:, kt * TC:(kt + 1) * TC],
                                 start=(kt == 0), stop=(kt == NPT - 1))
                        for kt in range(NPT)]),
                    "evac": lambda e, bk, mt=mt: e.activation(
                        srD[:, mt * TC:(mt + 1) * TC], PSb(bk, 0, TC), AF.Sigmoid),
                    "evac_eng": "scalar"})
                def v_ka(v, bk, mt=mt):
                    sl = mt % 2
                    i0 = v.tensor_scalar_max(tmpD[:, sl * TC:(sl + 1) * TC],
                                             PSb(bk, 0, TC), 0.0)
                    p.vg(v, i0)
                    return v.tensor_tensor(kaD[:, mt * TC:(mt + 1) * TC],
                                           PSb(bk, 0, TC),
                                           tmpD[:, sl * TC:(sl + 1) * TC], ALU.mult)
                items.append({
                    "mm": lambda t, bk, mt=mt: _last(*[
                        t.matmul(PSb(bk, 0, TC), Wt(wkD, kt, mt),
                                 xkD[:, kt * TC:(kt + 1) * TC],
                                 start=(kt == 0), stop=(kt == NPT - 1))
                        for kt in range(NPT)]),
                    "evac": v_ka})
            for mt in range(NPT):
                i_glob = 2 * NPT + mt
                items.append({
                    "mm": lambda t, bk, mt=mt: _last(*[
                        t.matmul(PSb(bk, 0, TC), Wt(wvD, kt, mt),
                                 kaD[:, kt * TC:(kt + 1) * TC],
                                 start=(kt == 0), stop=(kt == NPT - 1))
                        for kt in range(NPT)]),
                    "t_waits": ((("pD1Ev", base_v + NPT),) if mt == 0 else ()),
                    "v_waits": ((("pD1Es", base_s + NPT),) if mt == 0 else ()),
                    "evac": lambda e, bk, mt=mt: e.tensor_tensor(
                        accD[:], PSb(bk, 0, TC),
                        srD[:, mt * TC:(mt + 1) * TC], ALU.mult),
                    "store": lambda g, mt=mt, tc=tc: g.dma_start(
                        out=d_xres[:, mt * TP + tc * TC: mt * TP + (tc + 1) * TC],
                        in_=accD[:], accum_op=ALU.add),
                    "store_eng": "gpsimd",
                    "st_class": 0,
                    "buf_lag": 1})
            pipelined_mm(h, "pD1", items, nbank=4, bank0=0)


def emit_output(h):
    nc, p, blk, C, dram = h["nc"], h["p"], h["blk"], h["C"], h["dram"]
    PSb, idf = h["PSb"], h["idf"]
    p = h["p"]
    blk = h["blk"]
    NOT = CH // 128
    with contextlib.ExitStack() as s0:
        xT = s0.enter_context(nc.sbuf_tensor("xT_out", [128, NPT * TP], F32))
        yst = s0.enter_context(nc.sbuf_tensor("yst", [128, 2 * D], F32))
        C.step("sync", lambda sy: sy.dma_start(out=xT[:], in_=dram["xres"]), dma=True)
        oT, oE = p.sem("c3M"), p.sem("c3E")
        oS = [p.sem("c3L0"), p.sem("c3L1")]
        bT = bE = NTC * NPT
        bSl = [0, 0]
        C.flush()
        tl = C.tail()

        @blk.tensor
        def _(t):
            t.wait_ge(p.sem(tl[0]), tl[1])
            for tt in range(NOT):
                if tt >= 2:
                    t.wait_ge(oE, bE + tt - 1)
                _last(*[t.transpose(
                    PSb(2 * (tt % 2) + pt // 4, (pt % 4) * 128, 128),
                    xT[:, pt * TP + W + tt * 128: pt * TP + W + (tt + 1) * 128],
                    idf[:]) for pt in range(NPT)]).then_inc(oT, 1)

        @blk.vector
        def _(v):
            v.wait_ge(p.sem(tl[0]), tl[1])
            for tt in range(NOT):
                v.wait_ge(oT, bT + tt + 1)
                if tt >= 2:
                    v.wait_ge(oS[tt % 2], bSl[tt % 2] + ((tt - 2) // 2 + 1) * 16)
                _last(*[v.tensor_copy(
                    yst[:, (tt % 2) * D + pt * 128:(tt % 2) * D + (pt + 1) * 128],
                    PSb(2 * (tt % 2) + pt // 4, (pt % 4) * 128, 128))
                    for pt in range(NPT)]).then_inc(oE, 1)

        @blk.sync
        def _(sy):
            sy.wait_ge(p.sem(tl[0]), tl[1])
            for tt in range(NOT):
                sy.wait_ge(oE, bE + tt + 1)
                sy.dma_start(out=dram["y"][tt * 128:(tt + 1) * 128, :],
                             in_=yst[:, (tt % 2) * D:(tt % 2 + 1) * D]
                             ).then_inc(oS[tt % 2], 16)

        def ojoin(v):
            v.wait_ge(oS[0], bSl[0] + ((NOT + 1) // 2) * 16)
            v.wait_ge(oS[1], bSl[1] + (NOT // 2) * 16)
            return v.tensor_copy(h["idf"][0:1, 0:1], h["idf"][0:1, 0:1])
        C.step("vector", ojoin)
        C.flush()


def emit_attention(h, d_kT, d_vh, d_qT, d_oT):
    nc, p, blk, C = h["nc"], h["p"], h["blk"], h["C"]
    PSb, PSf, onesrf = h["PSb"], h["PSf"], h["onesrf"]
    QCS = []
    off = 0
    while off < TP:
        w = min(512, TP - off)
        QCS.append((off, w))
        off += w
    NQC = len(QCS)
    with contextlib.ExitStack() as ph:
        ksb = ph.enter_context(nc.sbuf_tensor("ksb", [128, 2 * S], BF16))
        vsb = ph.enter_context(nc.sbuf_tensor("vsb", [128, 2 * NKT * 130], BF16))
        qhp = ph.enter_context(nc.sbuf_tensor("qhp", [128, 2 * TP], BF16))
        ohp = ph.enter_context(nc.sbuf_tensor("ohp", [128, 2 * TP], BF16))
        ohpB = ph.enter_context(nc.sbuf_tensor("ohpB", [64, 2 * TP], BF16))
        pb = ph.enter_context(nc.sbuf_tensor("pb", [128, 3 * 2 * 512], BF16))
        recr = ph.enter_context(nc.sbuf_tensor("recr", [1, 2 * 512], F32))
        brs = ph.enter_context(nc.sbuf_tensor("brs", [128, 2 * 512], F32))
        sxp, spv, sqn, srec, sbrr = (p.sem(x) for x in
                                     ("sxp", "spv", "sqn", "srec", "sbrr"))
        sost = [p.sem("sost0"), p.sem("sost1")]
        sqk = p.sem("sqk")
        skl = [p.sem("skl0"), p.sem("skl1")]
        C.flush()
        tl = C.tail()
        KSB = lambda hp: ksb[:, (hp % 2) * S:(hp % 2 + 1) * S]
        VSB = lambda hp: vsb[:, (hp % 2) * NKT * 130:(hp % 2 + 1) * NKT * 130]
        QHP = lambda hp: qhp[:, (hp % 2) * TP:(hp % 2 + 1) * TP]
        OHP = lambda hp: ohp[:, (hp % 2) * TP:(hp % 2 + 1) * TP]
        OHPB = lambda hp: ohpB[:, (hp % 2) * TP:(hp % 2 + 1) * TP]

        @blk.sync
        def _(sy):
            if tl:
                sy.wait_ge(p.sem(tl[0]), tl[1])
            for hp in range(NHP):
                if hp >= 2:
                    sy.wait_ge(spv, (hp - 1) * NQC * NKT)
                sy.dma_start(out=KSB(hp), in_=d_kT[hp * 128:(hp + 1) * 128, :]
                             ).then_inc(skl[hp % 2], 16)
                sy.dma_start(out=VSB(hp).rearrange("p (t c) -> p t c", c=130),
                             in_=d_vh[hp].rearrange("(t p) c -> p t c", p=128)
                             ).then_inc(skl[hp % 2], 16)
                sy.dma_start(out=QHP(hp), in_=d_qT[hp * 128:(hp + 1) * 128, :]
                             ).then_inc(skl[hp % 2], 16)
                if hp >= 1:
                    sy.wait_ge(sqn, hp * NQC)
                    sy.dma_start(out=d_oT[(hp - 1) * 128:(hp - 1) * 128 + 64, :],
                                 in_=OHP(hp - 1)[0:64, :]).then_inc(sost[(hp - 1) % 2], 16)
                    sy.dma_start(out=d_oT[(hp - 1) * 128 + 64:hp * 128, :],
                                 in_=OHPB(hp - 1)).then_inc(sost[(hp - 1) % 2], 16)
            sy.wait_ge(sqn, NHP * NQC)
            sy.dma_start(out=d_oT[(NHP - 1) * 128:(NHP - 1) * 128 + 64, :],
                         in_=OHP(NHP - 1)[0:64, :]).then_inc(sost[(NHP - 1) % 2], 16)
            sy.dma_start(out=d_oT[(NHP - 1) * 128 + 64:NHP * 128, :],
                         in_=OHPB(NHP - 1)).then_inc(sost[(NHP - 1) % 2], 16)

        @blk.tensor
        def _(t):
            if tl:
                t.wait_ge(p.sem(tl[0]), tl[1])
            nqk = 0
            for hp in range(NHP):
                t.wait_ge(skl[hp % 2], (hp // 2 + 1) * 48)
                for qc in range(NQC):
                    it = hp * NQC + qc
                    qo, qw = QCS[qc]

                    def emit_pv(j, it=it, hp=hp, qw=qw):
                        slot = j % 3
                        t.wait_ge(sxp, it * NKT + j + 1)
                        if j == 0 and it > 0:
                            t.wait_ge(sqn, it)
                        t.matmul(PSb(6, 0, qw)[0:65, :],
                                 VSB(hp)[:, j * 130: j * 130 + 65],
                                 pb[:, (slot * 2) * 512:(slot * 2) * 512 + qw],
                                 start=(j == 0), stop=(j == NKT - 1),
                                 skip_group_check=True)
                        t.matmul(PSb(7, 0, qw)[0:65, :],
                                 VSB(hp)[:, j * 130 + 65: j * 130 + 130],
                                 pb[:, (slot * 2 + 1) * 512:(slot * 2 + 1) * 512 + qw],
                                 start=(j == 0), stop=(j == NKT - 1),
                                 skip_group_check=True).then_inc(spv, 1)

                    for kt in range(NKT):
                        slot = kt % 3
                        if nqk >= 3:
                            t.wait_ge(sxp, nqk - 2)
                        if kt == 0 and it > 0:
                            t.wait_ge(sqn, it)
                        t.matmul(PSb(2 * slot, 0, qw),
                                 KSB(hp)[0:64, kt * 128:(kt + 1) * 128],
                                 QHP(hp)[0:64, qo: qo + qw],
                                 start=True, stop=True, skip_group_check=True)
                        t.matmul(PSb(2 * slot + 1, 0, qw),
                                 KSB(hp)[64:128, kt * 128:(kt + 1) * 128],
                                 QHP(hp)[64:128, qo: qo + qw],
                                 start=True, stop=True, tile_position=(64, 0),
                                 skip_group_check=True).then_inc(sqk, 1)
                        nqk += 1
                        if kt >= 1:
                            emit_pv(kt - 1)
                    emit_pv(NKT - 1)
                    t.wait_ge(srec, it + 1)
                    t.matmul(PSb(0, 0, qw), onesrf[:], recr[0:1, 0:qw],
                             start=True, stop=True, skip_group_check=True)
                    t.matmul(PSb(1, 0, qw), onesrf[:], recr[0:1, 512:512 + qw],
                             start=True, stop=True,
                             skip_group_check=True).then_inc(sbrr, 1)

        @blk.scalar
        def _(s):
            nxp = 0
            for hp in range(NHP):
                for qc in range(NQC):
                    qo, qw = QCS[qc]
                    for kt in range(NKT):
                        slot = kt % 3
                        s.wait_ge(sqk, nxp + 1)
                        if nxp >= 3:
                            s.wait_ge(spv, nxp - 2)
                        s.activation(
                            pb[:, slot * 2 * 512:].rearrange(
                                "p (b n) -> p b n", n=512)[:, 0:2, 0:qw],
                            PSf[:, 2 * slot * BANK:(2 * slot + 2) * BANK].rearrange(
                                "p (b n) -> p b n", b=2)[:, :, 0:qw],
                            AF.Exp, scale=0.125).then_inc(sxp, 1)
                        nxp += 1

        @blk.vector
        def _(v):
            for hp in range(NHP):
                for qc in range(NQC):
                    it = hp * NQC + qc
                    qo, qw = QCS[qc]
                    if qc == 0 and hp >= 2:
                        v.wait_ge(sost[hp % 2], ((hp - 2) // 2 + 1) * 32)
                    v.wait_ge(spv, (it + 1) * NKT)
                    v.reciprocal(recr[0:1, 0:qw], PSb(6, 0, qw)[64:65, :])
                    v.reciprocal(recr[0:1, 512:512 + qw],
                                 PSb(7, 0, qw)[64:65, :]).then_inc(srec, 1)
                    v.wait_ge(sbrr, it + 1)
                    ia = v.tensor_copy(brs[0:64, 0:qw], PSb(0, 0, qw)[0:64, :])
                    ib = v.tensor_copy(brs[0:64, 512:512 + qw], PSb(1, 0, qw)[0:64, :])
                    p.vg(v, ia)
                    p.vg(v, ib)
                    v.tensor_tensor(OHP(hp)[0:64, qo: qo + qw],
                                    PSb(6, 0, qw)[0:64, :], brs[0:64, 0:qw], ALU.mult)
                    v.tensor_tensor(OHPB(hp)[0:64, qo: qo + qw],
                                    PSb(7, 0, qw)[0:64, :], brs[0:64, 512:512 + qw],
                                    ALU.mult).then_inc(sqn, 1)

        C.step("vector", lambda v: _last(v.wait_ge(sost[0], (NHP // 2) * 32),
                                         v.wait_ge(sost[1], (NHP // 2) * 32),
                                         v.tensor_copy(brs[0:1, 0:1], brs[0:1, 0:1])))
        C.flush()



def _prep_core_inputs(c, full):
    b, j = c // 4, c % 4
    x = full["L0_attn_x"][b]
    lo = j * CH - W
    if lo < 0:
        xs = np.concatenate([np.zeros((W, D), np.float32), np.asarray(x[0:CH])], 0)
        mask = np.concatenate([np.zeros((128, W), np.float32),
                               np.ones((128, CH), np.float32)], 1)
    else:
        xs = np.ascontiguousarray(x[lo: (j + 1) * CH])
        mask = np.ones((128, TP), np.float32)
    T = lambda a: np.ascontiguousarray(np.asarray(a).T)
    A = lambda a: np.ascontiguousarray(np.asarray(a))
    return {
        "xs": np.ascontiguousarray(xs),
        "xb": A(x),
        "mask": mask,
        "ident": np.eye(128, dtype=np.float32),
        "w_q": A(full["L0_attn_w_q"]),
        "w_k": A(full["L0_attn_w_k"]),
        "w_v": A(full["L0_attn_w_v"]),
        "w_proj": A(full["L0_attn_w_proj"]),
        "b_proj": A(full["L0_attn_b_proj"]),
        "ln_g": A(full["L0_attn_ln_gamma"]),
        "ln_b": A(full["L0_attn_ln_beta"]),
        "sh_gate_T": T(full["L0_ffn_shared_gate"]),
        "sh_up_T": T(full["L0_ffn_shared_up"]),
        "sh_down_T": T(full["L0_ffn_shared_down"]),
        "gate_w": A(full["L0_ffn_gate_w"]),
        "up_w": A(full["L0_ffn_up_w"]),
        "down_w": A(full["L0_ffn_down_w"]),
        "router_T": T(full["L0_ffn_router_w"]),
        "r_bias": A(full["L0_ffn_routing_bias"]),
        "l1_w": A(full["L1_attn_w"]),
        "l1_u": A(full["L1_attn_u"]),
        "l1_mk": A(full["L1_attn_mix_k"]),
        "l1_mv": A(full["L1_attn_mix_v"]),
        "l1_mr": A(full["L1_attn_mix_r"]),
        "Wk_T": T(full["L1_attn_Wk"]),
        "Wv_T": T(full["L1_attn_Wv"]),
        "Wr_T": T(full["L1_attn_Wr"]),
        "Wo_T": T(full["L1_attn_Wo"]),
        "f_Wr_T": T(full["L1_ffn_W_r"]),
        "f_Wk_T": T(full["L1_ffn_W_k"]),
        "f_Wv_T": T(full["L1_ffn_W_v"]),
        "f_mur": A(full["L1_ffn_mu_r"]),
        "f_muk": A(full["L1_ffn_mu_k"]),
    }


ALL_STAGES = ("IN", "A", "B", "C", "D", "OUT")


def kernel(**inputs):
    nc, out_names = build_program(stages=ALL_STAGES)
    in_maps = [_prep_core_inputs(c, inputs) for c in range(8)]
    res = run_bass_kernel_spmd(nc, in_maps, core_ids=list(range(8)))
    out = np.zeros((B, S, D), np.float32)
    for c in range(8):
        b, j = c // 4, c % 4
        out[b, j * CH:(j + 1) * CH, :] = res.results[c]["y"]
    return out
